# version 01
# baseline (speedup 1.0000x reference)
import numpy as np
import ml_dtypes
from contextlib import ExitStack

import concourse.bass as bass
import concourse.mybir as mybir
from concourse.bass_utils import run_bass_kernel_spmd

F32 = mybir.dt.float32
BF16 = mybir.dt.bfloat16
AF = mybir.ActivationFunctionType
ALU = mybir.AluOpType
AX = mybir.AxisListType
BF = ml_dtypes.bfloat16

NCORES = 8
B = 2
T = 8192
D = 1024
TPC = 2048
DC = D // 128
TT = 512
NTT = TPC // TT
NTC = TPC // 128
NH = 16
DH = 64


class Prog:
    def __init__(self):
        self.nc = bass.Bass("TRN2", target_bir_lowering=False, debug=False,
                            detect_race_conditions=False)
        self.st = ExitStack()
        self._sems = {}
        self.cnt = {}

    def sem(self, name):
        if name not in self._sems:
            self._sems[name] = self.st.enter_context(self.nc.semaphore(name))
            self.cnt[name] = 0
        return self._sems[name]

    def inc(self, inst, name, n=1):
        inst.then_inc(self.sem(name), n)
        self.cnt[name] += n
        return self.cnt[name]

    def wait(self, eng, name, val=None):
        v = self.cnt.get(name, 0) if val is None else val
        if v > 0:
            eng.wait_ge(self.sem(name), v)

    def sb(self, name, shape, dtype=F32):
        return self.st.enter_context(self.nc.sbuf_tensor(name, shape, dtype))

    def ps(self, name, shape, dtype=F32):
        return self.st.enter_context(self.nc.psum_tensor(name, shape, dtype))

    def din(self, name, shape, dtype=F32):
        return self.nc.dram_tensor(name, list(shape), dtype, kind="ExternalInput").ap()

    def dout(self, name, shape, dtype=F32):
        return self.nc.dram_tensor(name, list(shape), dtype, kind="ExternalOutput").ap()

    def dtmp(self, name, shape, dtype=F32):
        return self.nc.dram_tensor(name, list(shape), dtype).ap()

    def block(self):
        return self.st.enter_context(self.nc.Block())

    def vsf(self, v, inst):
        n = self.inc(inst, "vsf", 1)
        v.wait_ge(self.sem("vsf"), n)

    def close(self):
        self.st.close()


def _run(p, in_maps):
    res = run_bass_kernel_spmd(p.nc, in_maps, core_ids=list(range(NCORES)))
    p.close()
    return res.results


def host_fm(a2d):
    return np.ascontiguousarray(np.asarray(a2d, dtype=np.float32).T)


IDENT128 = np.eye(128, dtype=np.float32)



def emit_ln_tile(p, blk, pre, xt, xn_bf, sq, srow, psA, psB, psC, psD, ones,
                 nchunk=DC, tmp=None, xn_f32=None, dmodel=D,
                 gain_ap=None, bias_ap=None, eps_ap=None, extra_wait=()):
    @blk.tensor
    def _(t):
        p.wait(t, "ln_xld")
        for w in extra_wait:
            p.wait(t, w)
        for c in range(nchunk):
            mm = t.matmul(psA[0:1, :], ones[:, 0:1], xt[:, c, :],
                          start=(c == 0), stop=(c == nchunk - 1))
        p.inc(mm, "ln_s1", 1)

    sqg0 = p.cnt.get("sqg", 0)

    @blk.scalar
    def _(s):
        p.wait(s, "ln_xld")
        for w in extra_wait:
            p.wait(s, w)
        for c in range(nchunk):
            g = sqg0 + c
            if g >= 1:
                s.wait_ge(p.sem("sqc"), g)
            p.inc(s.activation(out=sq[:, 0, :], in_=xt[:, c, :], func=AF.Square),
                  "sqg", 1)

    @blk.tensor
    def _(t):
        for c in range(nchunk):
            g = sqg0 + c
            t.wait_ge(p.sem("sqg"), g + 1)
            mm = t.matmul(psB[0:1, :], ones[:, 0:1], sq[:, 0, :],
                          start=(c == 0), stop=(c == nchunk - 1))
            p.inc(mm, "sqc", 1)

    @blk.vector
    def _(v):
        p.wait(v, "ln_s1")
        p.wait(v, "sqc")
        p.vsf(v, v.tensor_scalar_mul(srow[0:1, 0, :], psA[0:1, :], 1.0 / dmodel))
        p.vsf(v, v.tensor_scalar_mul(srow[0:1, 1, :], psB[0:1, :], 1.0 / dmodel))
        p.vsf(v, v.tensor_mul(psA[0:1, :], srow[0:1, 0, :], srow[0:1, 0, :]))
        p.inc(v.tensor_sub(srow[0:1, 1, :], srow[0:1, 1, :], psA[0:1, :]),
              "ln_va", 1)

    @blk.scalar
    def _(s):
        p.wait(s, "ln_va")
        s.activation(out=psB[0:1, :], in_=srow[0:1, 1, :], func=AF.Ln, bias=eps_ap)
        p.inc(s.activation(out=srow[0:1, 1, :], in_=psB[0:1, :], func=AF.Exp,
                           scale=-0.5), "ln_rm", 1)

    @blk.vector
    def _(v):
        p.wait(v, "ln_rm")
        p.vsf(v, v.tensor_mul(srow[0:1, 0, :], srow[0:1, 0, :], srow[0:1, 1, :]))
        p.inc(v.tensor_scalar_mul(srow[0:1, 0, :], srow[0:1, 0, :], -1.0),
              "ln_rw", 1)

    @blk.tensor
    def _(t):
        p.wait(t, "ln_rw")
        t.matmul(psC[:, :], ones[0:1, 0:1].to_broadcast([1, 128]),
                 srow[0:1, 0, :], start=True, stop=True)
        p.inc(t.matmul(psD[:, :], ones[0:1, 0:1].to_broadcast([1, 128]),
                       srow[0:1, 1, :], start=True, stop=True), "ln_bc", 1)

    @blk.vector
    def _(v):
        p.wait(v, "ln_bc")
        for c in range(nchunk):
            dst = xn_f32[c] if xn_f32 is not None else tmp[:, :]
            v.tensor_mul(dst, xt[:, c, :], psD[:, :])
            v.tensor_add(dst, dst, psC[:, :])
            if gain_ap is not None:
                v.tensor_scalar(dst, dst, gain_ap[c], bias_ap[c], ALU.mult, ALU.add)
            cp = v.tensor_copy(xn_bf[c], dst)
        p.inc(cp, "ln_xn", 1)



def emit_fp4_qdq(p, blk, xin, qa, qsc, qrc, qc1, qc2, qout, rows, width):
    nb = width // 16

    @blk.scalar
    def _(s):
        p.wait(s, "qqin")
        s.activation(out=qa[0:rows, 0:width], in_=xin, func=AF.Abs)
        p.inc(s.sign(qc2[0:rows, 0:width], xin), "qqab", 1)

    @blk.vector
    def _(v):
        p.wait(v, "qqab")
        a3 = qa[0:rows, 0:width].rearrange("p (b k) -> p b k", k=16)
        p.vsf(v, v.tensor_reduce(qsc[0:rows, 0:nb, 0:1], a3, axis=AX.X, op=ALU.max))
        p.vsf(v, v.tensor_scalar(qsc[0:rows, 0:nb, 0:1], qsc[0:rows, 0:nb, 0:1],
                                 1.0 / 6.0, 1e-12, ALU.mult, ALU.max))
        p.vsf(v, v.reciprocal(qrc[0:rows, 0:nb, 0:1], qsc[0:rows, 0:nb, 0:1]))
        v.tensor_tensor(a3, a3, qrc[0:rows, 0:nb, 0:1].to_broadcast([rows, nb, 16]),
                        ALU.mult)
        z = qa[0:rows, 0:width]
        c1 = qc1[0:rows, 0:width]
        v.tensor_scalar(c1, z, 0.25, None, ALU.is_gt)
        for th in (0.75, 1.25, 1.75):
            v.scalar_tensor_tensor(c1, z, th, c1, ALU.is_gt, ALU.add)
        v.tensor_scalar_mul(c1, c1, 0.5)
        for th in (2.5, 3.5):
            v.scalar_tensor_tensor(c1, z, th, c1, ALU.is_gt, ALU.add)
        v.scalar_tensor_tensor(c1, z, 5.0, c1, ALU.is_gt, ALU.add)
        v.scalar_tensor_tensor(c1, z, 5.0, c1, ALU.is_gt, ALU.add)
        c13 = qc1[0:rows, 0:width].rearrange("p (b k) -> p b k", k=16)
        v.tensor_tensor(c13, c13, qsc[0:rows, 0:nb, 0:1].to_broadcast([rows, nb, 16]),
                        ALU.mult)
        p.inc(v.tensor_mul(qout[0:rows, 0:width], qc1[0:rows, 0:width],
                           qc2[0:rows, 0:width]), "qqou", 1)



def build_launch_A(debug_comb=False):
    p = Prog()
    x_d = p.din("x", [D, TPC])
    id_d = p.din("ident_in", [128, 128])
    wqkv_d = p.din("wqkv", [D, 3 * D])
    rf_d = p.din("rf", [NH, DH, DH])
    rw_d = p.din("router_w", [D, 8])
    gw_d = p.din("gate_w", [8, D, D])
    uw_d = p.din("up_w", [8, D, D])
    dw_d = p.din("down_w", [8, D, D])
    sgw_d = p.din("sgw", [D, 2048])
    suw_d = p.din("suw", [D, 2048])
    sdw_d = p.din("sdw", [2048, D])
    qg_d = p.din("qg", [352, 1024])
    qu_d = p.din("qu", [352, 1024])
    qd_d = p.din("qd", [128, 2816])

    phiq_o = p.dout("phiq", [NH, DH, TPC], BF16)
    kvp_o = p.dout("kvp", [NH, DH, DH + 1])
    g0_o = p.dout("g0", [D, TPC])
    qgq_o = p.dout("qgq", [352, 1024], BF16)
    quq_o = p.dout("quq", [352, 1024], BF16)
    qdq_o = p.dout("qdq", [128, 2816], BF16)

    comb_dram = (p.dout if debug_comb else p.dtmp)("combdram", [8, TPC])

    xt = p.sb("xt", [128, DC, TT])
    xn = p.sb("xn", [128, DC, TPC], BF16)
    sq = p.sb("sq", [128, 1, TT])
    srow = p.sb("srow", [1, 2, TT])
    ones = p.sb("ones", [128, 2])
    tmp = p.sb("tmpa", [128, TT])
    wbuf = p.sb("wbuf", [128, 2, DC, 128], BF16)
    wbuf2 = p.sb("wbuf2", [128, 2, DC, 128], BF16)
    wbuf3 = p.sb("wbuf3", [128, 16, 128], BF16)
    rf_sb = p.sb("rf_sb", [128, NH, DH], BF16)
    ar1 = p.sb("ar1", [128, DC, TPC], BF16)
    ar2 = p.sb("ar2", [128, DC, TPC], BF16)
    vstage = p.sb("vstage", [128, 1, TT], BF16)
    vtm = p.sb("vtm", [128, 2, DH + 1], BF16)
    fkst = p.sb("fkst", [128, 2, DH], BF16)
    fqst = p.sb("fqst", [64, 1, TT], BF16)
    kvsb = p.sb("kvsb", [64, 2, DH + 1])
    logit = p.sb("logit", [128, NTC, 8])
    lsc = p.sb("lsc", [128, 8])
    comb_tm = p.sb("comb_tm", [128, NTC, 8])
    comb_st = p.sb("comb_st", [8, 128])
    comb_row = p.sb("comb_row", [1, 1, TT])
    g0acc = p.sb("g0acc", [128, DC, TPC])
    sg = p.sb("sg", [128, 1, TT])
    rwsb = p.sb("rwsb", [128, DC, 8], BF16)
    ident = p.sb("ident", [128, 128])
    identb = p.sb("identb", [128, 128], BF16)
    qout = None
    qin = sq[:, 0, :]
    qa = g0acc[:, 2, 0:TT]
    qc1 = tmp
    qc2 = g0acc[:, 0, 0:TT]
    qsc = g0acc[:, 1, 0:TT // 16].rearrange("p (x o) -> p x o", o=1)
    qrc = g0acc[:, 1, TT // 16:2 * (TT // 16)].rearrange("p (x o) -> p x o", o=1)
    qout = ar1[:, 0, 0:TT]

    P0 = p.ps("P0", [128, TT])
    P1 = p.ps("P1", [128, TT])
    P2 = p.ps("P2", [128, TT])
    P3 = p.ps("P3", [128, TT])
    P4 = p.ps("P4", [128, TT])
    P5 = p.ps("P5", [128, TT])
    PT = p.ps("PT", [128, 128], BF16)
    PTf = p.ps("PTf", [8, 128])

    blk = p.block()

    cst = p.sb("cst", [128, 2])

    @blk.vector
    def _(v):
        v.memset(ones[:, :], 1.0)
        v.memset(vtm[:, 0, DH:DH + 1], 1.0)
        v.memset(vtm[:, 1, DH:DH + 1], 1.0)
        v.memset(cst[:, 0:1], 1e-5)
        p.inc(v.memset(cst[:, 1:2], 1.0), "init", 1)

    @blk.scalar
    def _(s):
        p.wait(s, "init")

    @blk.tensor
    def _(t):
        p.wait(t, "init")

    @blk.sync
    def _(sy):
        p.inc(sy.dma_start(out=ident[:, :], in_=id_d[:, :]), "wlds", 16)

    @blk.gpsimd
    def _(g):
        p.inc(g.dma_start(out=rf_sb[0:64, :, :],
                          in_=rf_d.rearrange("h d r -> d h r")), "wld", 16)
        p.inc(g.dma_start(out=rf_sb[64:128, :, :],
                          in_=rf_d.rearrange("h d r -> d h r")), "wld", 16)
        p.inc(g.dma_start(out=rwsb[:, :, :],
                          in_=rw_d.rearrange("(c p) e -> p c e", p=128)), "wld", 16)
        p.inc(g.dma_start(out=identb[:, :], in_=id_d[:, :]), "wld", 16)

    for it in range(NTT):
        pre = f"ln{it}_"
        tsl = slice(it * TT, (it + 1) * TT)

        @blk.sync
        def _(sy, it=it):
            if it >= 1:
                p.wait(sy, "ln_xn")
            p.inc(sy.dma_start(
                out=xt[:, :, :],
                in_=x_d.rearrange("(c p) t -> p c t", p=128)[:, :, it * TT:(it + 1) * TT]),
                "ln_xld", 16)

        emit_ln_tile(p, blk, pre, xt, [xn[:, c, tsl] for c in range(DC)],
                     sq, srow, P0, P1, P2, P3, ones, tmp=tmp, eps_ap=cst[0:1, 0:1])

        @blk.tensor
        def _(t, it=it):
            p.wait(t, "ln_xn")
            p.wait(t, "wld")
            for tc in range(4):
                col0 = it * TT + tc * 128
                idx = it * 4 + tc
                if idx > 0:
                    t.wait_ge(p.sem("rtev"), idx)
                for c in range(DC):
                    mm = t.matmul(P4[:, 0:8], xn[:, c, col0:col0 + 128],
                                  rwsb[:, c, :], start=(c == 0), stop=(c == DC - 1))
                p.inc(mm, "rtmm", 1)

        @blk.vector
        def _(v, it=it):
            for tc in range(4):
                v.wait_ge(p.sem("rtmm"), it * 4 + tc + 1)
                p.inc(v.tensor_copy(logit[:, it * 4 + tc, :], P4[:, 0:8]), "rtev", 1)

    for tc in range(NTC):
        @blk.vector
        def _(v, tc=tc):
            if tc == 0:
                p.wait(v, "rtev")
            else:
                v.wait_ge(p.sem("rex"), tc)
            p.inc(v.reduce_max(lsc[:, 0:1], logit[:, tc, :], axis=AX.X, negate=True),
                  "rmx", 1)

        @blk.scalar
        def _(s, tc=tc):
            s.wait_ge(p.sem("rmx"), tc + 1)
            p.inc(s.activation(out=comb_tm[:, tc, :], in_=logit[:, tc, :], func=AF.Exp,
                               bias=lsc[:, 0:1], accum_out=lsc[:, 1:2]), "rex", 1)

        @blk.vector
        def _(v, tc=tc):
            v.wait_ge(p.sem("rex"), tc + 1)
            S = comb_tm[:, tc, :]
            M = logit[:, tc, :]
            p.vsf(v, v.reciprocal(lsc[:, 2:3], lsc[:, 1:2]))
            p.vsf(v, v.tensor_scalar_mul(S, S, lsc[:, 2:3]))
            p.vsf(v, v.reduce_max(lsc[:, 3:4], S, axis=AX.X, negate=True))
            p.vsf(v, v.tensor_scalar(M, S, lsc[:, 3:4], 0.0, ALU.add, ALU.is_ge))
            p.vsf(v, v.scalar_tensor_tensor(qa[:, 0:8], M, -1e30, S, ALU.mult, ALU.add))
            p.vsf(v, v.reduce_max(lsc[:, 4:5], qa[:, 0:8], axis=AX.X, negate=True))
            p.vsf(v, v.tensor_scalar(qc1[:, 0:8], qa[:, 0:8], lsc[:, 4:5], 0.0,
                                     ALU.add, ALU.is_ge))
            p.vsf(v, v.tensor_scalar_mul(lsc[:, 5:6], lsc[:, 3:4], -1.0))
            p.vsf(v, v.tensor_scalar_mul(lsc[:, 6:7], lsc[:, 4:5], -1.0))
            p.vsf(v, v.tensor_scalar_mul(M, M, lsc[:, 5:6]))
            p.vsf(v, v.tensor_scalar_mul(qc1[:, 0:8], qc1[:, 0:8], lsc[:, 6:7]))
            p.inc(v.tensor_add(S, M, qc1[:, 0:8]), "cmb", 1)

    @blk.tensor
    def _(t):
        p.wait(t, "cmb")
        p.wait(t, "wld")
        p.wait(t, "wlds")
        for tc in range(NTC):
            if tc > 0:
                t.wait_ge(p.sem("ctv"), tc)
            p.inc(t.transpose(PTf[0:8, 0:128], comb_tm[:, tc, :], ident[:, :]),
                  "ctm", 1)

    @blk.vector
    def _(v):
        for tc in range(NTC):
            v.wait_ge(p.sem("ctm"), tc + 1)
            if tc >= 1:
                v.wait_ge(p.sem("cbd"), 16 * tc)
            p.inc(v.tensor_copy(comb_st[0:8, :], PTf[0:8, 0:128]), "ctv", 1)

    @blk.sync
    def _(sy):
        for tc in range(NTC):
            sy.wait_ge(p.sem("ctv"), tc + 1)
            p.inc(sy.dma_start(out=comb_dram[:, tc * 128:(tc + 1) * 128],
                               in_=comb_st[0:8, :]), "cbd", 16)
        sy.wait_ge(p.sem("cbd"), p.cnt["cbd"])
        p.inc(sy.nop() if False else sy.dma_start(
            out=comb_row[0:1, 0:1, 0:1], in_=comb_dram[0:1, 0:1]), "cbl", 16)

    def qkv_w_slice(oc):
        return wqkv_d.rearrange("(c p) m -> p c m", p=128)[:, :, oc * 128:(oc + 1) * 128]

    @blk.gpsimd
    def _(g):
        for oc in range(16):
            if oc >= 2:
                g.wait_ge(p.sem("qev"), 4 * (oc - 1))
            p.inc(g.dma_start(out=wbuf[:, oc % 2, :, :], in_=qkv_w_slice(oc)),
                  "wq", 16)

    @blk.tensor
    def _(t):
        p.wait(t, "ln_xn")
        for oc in range(16):
            t.wait_ge(p.sem("wq"), 16 * (oc + 1))
            for it in range(NTT):
                gdx = 4 * oc + it
                if gdx >= 2:
                    t.wait_ge(p.sem("qev"), gdx - 1)
                ps = P0 if gdx % 2 == 0 else P1
                for c in range(DC):
                    mm = t.matmul(ps[:, :], wbuf[:, oc % 2, c, :],
                                  xn[:, c, it * TT:(it + 1) * TT],
                                  start=(c == 0), stop=(c == DC - 1))
                p.inc(mm, "qmm", 1)

    @blk.vector
    def _(v):
        for oc in range(16):
            dst = ar1 if oc < 8 else ar2
            for it in range(NTT):
                gdx = 4 * oc + it
                v.wait_ge(p.sem("qmm"), gdx + 1)
                ps = P0 if gdx % 2 == 0 else P1
                p.inc(v.tensor_copy(dst[:, oc % 8, it * TT:(it + 1) * TT], ps[:, :]),
                      "qev", 1)

    @blk.tensor
    def _(t):
        p.wait(t, "qev", 32)
        for h in range(NH):
            b0 = (h % 2) * 64
            for it in range(NTT):
                idx = h * NTT + it
                if idx > 0:
                    t.wait_ge(p.sem("fqev"), idx)
                p.inc(t.matmul(P2[0:64, :], rf_sb[b0:b0 + 64, h, :],
                               ar1[b0:b0 + 64, h // 2, it * TT:(it + 1) * TT],
                               start=True, stop=True), "fqmm", 1)

    @blk.vector
    def _(v):
        for h in range(NH):
            for it in range(NTT):
                idx = h * NTT + it
                v.wait_ge(p.sem("fqmm"), idx + 1)
                if idx >= 1:
                    v.wait_ge(p.sem("fqdm"), 16 * idx)
                p.inc(v.tensor_relu(fqst[:, 0, :], P2[0:64, :]), "fqev", 1)

    @blk.scalar
    def _(s):
        for h in range(NH):
            for it in range(NTT):
                idx = h * NTT + it
                s.wait_ge(p.sem("fqev"), idx + 1)
                p.inc(s.dma_start(out=phiq_o[h, :, it * TT:(it + 1) * TT],
                                  in_=fqst[:, 0, :]), "fqdm", 16)

    @blk.gpsimd
    def _(g):
        for vc in range(8):
            if vc == 0:
                g.wait_ge(p.sem("qev"), 60)
            elif vc == 1:
                g.wait_ge(p.sem("qev"), 64)
            else:
                g.wait_ge(p.sem("vev"), 4 * (vc - 1))
            p.inc(g.dma_start(out=wbuf[:, vc % 2, :, :], in_=qkv_w_slice(16 + vc)),
                  "wv", 16)

    for vc in range(8):
        h0 = 2 * vc
        for it in range(NTT):
            gdx = 4 * vc + it

            @blk.tensor
            def _(t, vc=vc, it=it, gdx=gdx):
                if it == 0:
                    t.wait_ge(p.sem("wv"), 16 * (vc + 1))
                    if vc > 0:
                        t.wait_ge(p.sem("kvev"), vc)
                if gdx >= 2:
                    t.wait_ge(p.sem("vev"), gdx - 1)
                ps = P0 if gdx % 2 == 0 else P1
                for c in range(DC):
                    mm = t.matmul(ps[:, :], wbuf[:, vc % 2, c, :],
                                  xn[:, c, it * TT:(it + 1) * TT],
                                  start=(c == 0), stop=(c == DC - 1))
                p.inc(mm, "vmm", 1)

            @blk.vector
            def _(v, gdx=gdx):
                v.wait_ge(p.sem("vmm"), gdx + 1)
                if gdx > 0:
                    v.wait_ge(p.sem("kvmm"), gdx * 8)
                ps = P0 if gdx % 2 == 0 else P1
                p.inc(v.tensor_copy(vstage[:, 0, :], ps[:, :]), "vev", 1)

            for tc in range(4):
                for j in range(2):
                    h = h0 + j
                    kb = (h % 2) * 64
                    step = ((gdx * 4 + tc) * 2 + j)

                    @blk.tensor
                    def _(t, gdx=gdx, tc=tc, j=j, step=step):
                        t.wait_ge(p.sem("vev"), gdx + 1)
                        if step >= 2:
                            t.wait_ge(p.sem("vtev"), step - 1)
                        p.inc(t.transpose(PT[:, j * 64:(j + 1) * 64],
                                          vstage[j * 64:(j + 1) * 64, 0,
                                                 tc * 128:(tc + 1) * 128],
                                          identb[j * 64:j * 64 + 64, j * 64:j * 64 + 64]),
                              "vtmm", 1)

                    @blk.vector
                    def _(v, j=j, step=step):
                        v.wait_ge(p.sem("vtmm"), step + 1)
                        if step >= 2:
                            v.wait_ge(p.sem("kvmm"), step - 1)
                        p.inc(v.tensor_copy(vtm[:, j, 0:DH], PT[:, j * 64:(j + 1) * 64]),
                              "vtev", 1)

                    @blk.tensor
                    def _(t, h=h, j=j, kb=kb, it=it, tc=tc, step=step):
                        t.wait_ge(p.sem("vtev"), step + 1)
                        if step >= 2:
                            t.wait_ge(p.sem("fkev"), step - 1)
                        p.inc(t.matmul(P3[:, j * 64:(j + 1) * 64],
                                       ar2[kb:kb + 64, h // 2,
                                           it * TT + tc * 128:it * TT + (tc + 1) * 128],
                                       rf_sb[kb:kb + 64, h, :], start=True, stop=True),
                              "fkmm", 1)

                    @blk.vector
                    def _(v, j=j, step=step):
                        v.wait_ge(p.sem("fkmm"), step + 1)
                        p.inc(v.tensor_relu(fkst[:, j, :], P3[:, j * 64:(j + 1) * 64]),
                              "fkev", 1)

                    @blk.tensor
                    def _(t, j=j, step=step, it=it, tc=tc, vc=vc):
                        t.wait_ge(p.sem("fkev"), step + 1)
                        if vc == 0 and it == 0 and tc == 0 and j == 1:
                            t.wait_ge(p.sem("fqev"), 64)
                        first = (it == 0 and tc == 0)
                        last = (it == NTT - 1 and tc == 3)
                        psKV = P5 if j == 0 else P2
                        p.inc(t.matmul(psKV[0:64, 0:65], fkst[:, j, :], vtm[:, j, :],
                                       start=first, stop=last), "kvmm", 1)

        @blk.vector
        def _(v, vc=vc, h0=h0):
            v.wait_ge(p.sem("kvmm"), (4 * vc + 4) * 8)
            if vc > 0:
                v.wait_ge(p.sem("kvdm"), 32 * vc)
            v.tensor_copy(kvsb[:, 0, :], P5[0:64, 0:65])
            p.inc(v.tensor_copy(kvsb[:, 1, :], P2[0:64, 0:65]), "kvev", 1)

        @blk.scalar
        def _(s, vc=vc, h0=h0):
            s.wait_ge(p.sem("kvev"), vc + 1)
            p.inc(s.dma_start(out=kvp_o[h0, :, :], in_=kvsb[:, 0, :]), "kvdm", 16)
            p.inc(s.dma_start(out=kvp_o[h0 + 1, :, :], in_=kvsb[:, 1, :]), "kvdm", 16)

    @blk.gpsimd
    def _(g):
        for e in range(8):
            if e == 0:
                g.wait_ge(p.sem("fkmm"), 256)
            else:
                g.wait_ge(p.sem("dmm"), e * 32)
            p.inc(g.dma_start(out=ar2[:, :, 0:1024],
                              in_=gw_d[e].rearrange("(c p) f -> p c f", p=128)),
                  "mwl", 16)
            p.inc(g.dma_start(out=ar2[:, :, 1024:2048],
                              in_=uw_d[e].rearrange("(c p) f -> p c f", p=128)),
                  "mwl", 16)
            g.wait_ge(p.sem("hmm"), (e + 1) * 32)
            p.inc(g.dma_start(out=ar2[:, :, 0:1024],
                              in_=dw_d[e].rearrange("(c p) f -> p c f", p=128)),
                  "mdl", 16)

    for e in range(8):
        hev_base = p.cnt.get("hev", 0)

        @blk.sync
        def _(sy, e=e):
            for it in range(NTT):
                gi = e * NTT + it
                if gi >= 1:
                    sy.wait_ge(p.sem("cbc"), gi)
                p.inc(sy.dma_start(out=comb_row[0:1, 0, :],
                                   in_=comb_dram[e:e + 1, it * TT:(it + 1) * TT]),
                      "crl", 16)

        @blk.tensor
        def _(t, e=e, hev_base=hev_base):
            t.wait_ge(p.sem("mwl"), 32 * (e + 1))
            if e == 0:
                p.wait(t, "cbl")
                p.wait(t, "qev")
                p.wait(t, "vev")
                p.wait(t, "kvev")
            for it in range(NTT):
                cur_base = e * 32 + it * 8
                if cur_base > 0:
                    t.wait_ge(p.sem("hev"), cur_base)
                gi = e * NTT + it
                t.wait_ge(p.sem("crl"), 16 * (gi + 1))
                p.inc(t.matmul(P5[:, 0:TT], ones[0:1, 0:1].to_broadcast([1, 128]),
                               comb_row[0:1, 0, :], start=True, stop=True), "cbc", 1)
                for fc in range(DC):
                    if fc >= 2:
                        t.wait_ge(p.sem("hev"), cur_base + fc - 1)
                    psG = P0 if fc % 2 == 0 else P2
                    psU = P1 if fc % 2 == 0 else P3
                    for c in range(DC):
                        t.matmul(psG[:, :], ar2[:, c, fc * 128:(fc + 1) * 128],
                                 xn[:, c, it * TT:(it + 1) * TT],
                                 start=(c == 0), stop=(c == DC - 1))
                    for c in range(DC):
                        mm = t.matmul(psU[:, :], ar2[:, c, 1024 + fc * 128:1024 + (fc + 1) * 128],
                                      xn[:, c, it * TT:(it + 1) * TT],
                                      start=(c == 0), stop=(c == DC - 1))
                    p.inc(mm, "hmm", 1)

        @blk.scalar
        def _(s, e=e):
            for it in range(NTT):
                for fc in range(DC):
                    idx = p.cnt.get("slu", 0)
                    s.wait_ge(p.sem("hmm"), e * 32 + it * 8 + fc + 1)
                    gidx = e * 32 + it * 8 + fc
                    if gidx >= 1:
                        s.wait_ge(p.sem("hev"), gidx)
                    psG = P0 if fc % 2 == 0 else P2
                    p.inc(s.activation(out=sg[:, 0, :], in_=psG[:, :], func=AF.Silu),
                          "slu", 1)

        @blk.vector
        def _(v, e=e):
            for it in range(NTT):
                v.wait_ge(p.sem("cbc"), e * 4 + it + 1)
                for fc in range(DC):
                    v.wait_ge(p.sem("slu"), e * 32 + it * 8 + fc + 1)
                    psU = P1 if fc % 2 == 0 else P3
                    v.tensor_mul(tmp[:, :], sg[:, 0, :], psU[:, :])
                    v.tensor_mul(tmp[:, :], tmp[:, :], P5[:, 0:TT])
                    p.inc(v.tensor_copy(ar1[:, fc, it * TT:(it + 1) * TT], tmp[:, :]),
                          "hev", 1)

        @blk.tensor
        def _(t, e=e):
            t.wait_ge(p.sem("mdl"), 16 * (e + 1))
            t.wait_ge(p.sem("hev"), (e + 1) * 32)
            if e == 0:
                p.wait(t, "rtev")
            for it in range(NTT):
                for oc in range(DC):
                    idx = e * 32 + it * 8 + oc
                    if idx > 0:
                        t.wait_ge(p.sem("dev"), idx)
                    for fc in range(DC):
                        mm = t.matmul(P4[:, :], ar2[:, fc, oc * 128:(oc + 1) * 128],
                                      ar1[:, fc, it * TT:(it + 1) * TT],
                                      start=(fc == 0), stop=(fc == DC - 1))
                    p.inc(mm, "dmm", 1)

        @blk.vector
        def _(v, e=e):
            for it in range(NTT):
                for oc in range(DC):
                    idx = e * 32 + it * 8 + oc
                    v.wait_ge(p.sem("dmm"), idx + 1)
                    if e == 0:
                        ev = v.tensor_copy(g0acc[:, oc, it * TT:(it + 1) * TT], P4[:, :])
                    else:
                        ev = v.tensor_add(g0acc[:, oc, it * TT:(it + 1) * TT],
                                          g0acc[:, oc, it * TT:(it + 1) * TT], P4[:, :])
                    p.inc(ev, "dev", 1)

    for it in range(NTT):
        @blk.gpsimd
        def _(g, it=it):
            if it == 0:
                g.wait_ge(p.sem("dmm"), 256)
            for fc in range(16):
                idx = it * 16 + fc
                if idx >= 2:
                    g.wait_ge(p.sem("shmm"), idx - 1)
                p.inc(g.dma_start(
                    out=wbuf[:, idx % 2, :, :],
                    in_=sgw_d.rearrange("(c p) f -> p c f", p=128)[:, :, fc * 128:(fc + 1) * 128]),
                    "swl", 16)
                p.inc(g.dma_start(
                    out=wbuf2[:, idx % 2, :, :],
                    in_=suw_d.rearrange("(c p) f -> p c f", p=128)[:, :, fc * 128:(fc + 1) * 128]),
                    "swl", 16)

        @blk.tensor
        def _(t, it=it):
            for fc in range(16):
                idx = it * 16 + fc
                t.wait_ge(p.sem("swl"), 32 * (idx + 1))
                if idx >= 2:
                    t.wait_ge(p.sem("shev"), idx - 1)
                psG = P0 if fc % 2 == 0 else P2
                psU = P1 if fc % 2 == 0 else P3
                for c in range(DC):
                    t.matmul(psG[:, :], wbuf[:, idx % 2, c, :],
                             xn[:, c, it * TT:(it + 1) * TT],
                             start=(c == 0), stop=(c == DC - 1))
                for c in range(DC):
                    mm = t.matmul(psU[:, :], wbuf2[:, idx % 2, c, :],
                                  xn[:, c, it * TT:(it + 1) * TT],
                                  start=(c == 0), stop=(c == DC - 1))
                p.inc(mm, "shmm", 1)

        @blk.scalar
        def _(s, it=it):
            for fc in range(16):
                idx = it * 16 + fc
                s.wait_ge(p.sem("shmm"), idx + 1)
                if idx >= 1:
                    s.wait_ge(p.sem("shev"), idx)
                psG = P0 if fc % 2 == 0 else P2
                p.inc(s.activation(out=sg[:, 0, :], in_=psG[:, :], func=AF.Silu),
                      "shsl", 1)

        @blk.vector
        def _(v, it=it):
            for fc in range(16):
                idx = it * 16 + fc
                v.wait_ge(p.sem("shsl"), idx + 1)
                if it >= 1 and fc == 0:
                    v.wait_ge(p.sem("sdmm"), it * 8)
                psU = P1 if fc % 2 == 0 else P3
                v.tensor_mul(tmp[:, :], sg[:, 0, :], psU[:, :])
                p.inc(v.tensor_copy(ar1.rearrange("p c t -> p (c t)")
                                    [:, fc * TT:(fc + 1) * TT], tmp[:, :]), "shev", 1)

        @blk.gpsimd
        def _(g, it=it):
            g.wait_ge(p.sem("shev"), it * 16 + 16)
            for oc in range(DC):
                idx2 = it * 8 + oc
                if idx2 > 0:
                    g.wait_ge(p.sem("sdmm"), idx2)
                p.inc(g.dma_start(
                    out=wbuf3[:, :, :],
                    in_=sdw_d.rearrange("(k p) m -> p k m", p=128)[:, :, oc * 128:(oc + 1) * 128]),
                    "sdl", 16)

        @blk.tensor
        def _(t, it=it):
            hsv = ar1.rearrange("p c t -> p (c t)")
            for oc in range(DC):
                idx2 = it * 8 + oc
                t.wait_ge(p.sem("sdl"), 16 * (idx2 + 1))
                if idx2 == 0:
                    p.wait(t, "dev")
                else:
                    t.wait_ge(p.sem("sdev"), idx2)
                for kc in range(16):
                    mm = t.matmul(P4[:, :], wbuf3[:, kc, :], hsv[:, kc * TT:(kc + 1) * TT],
                                  start=(kc == 0), stop=(kc == 15))
                p.inc(mm, "sdmm", 1)

        @blk.vector
        def _(v, it=it):
            for oc in range(DC):
                idx2 = it * 8 + oc
                v.wait_ge(p.sem("sdmm"), idx2 + 1)
                p.inc(v.tensor_add(g0acc[:, oc, it * TT:(it + 1) * TT],
                                   g0acc[:, oc, it * TT:(it + 1) * TT], P4[:, :]),
                      "sdev", 1)

    @blk.scalar
    def _(s):
        p.wait(s, "sdev")
        for c in range(DC):
            ac = s.activation(out=g0acc[:, c, :], in_=g0acc[:, c, :], func=AF.Silu)
        p.inc(ac, "g0s", 1)

    @blk.sync
    def _(sy):
        p.wait(sy, "g0s")
        for c in range(DC):
            p.inc(sy.dma_start(out=g0_o[c * 128:(c + 1) * 128, :], in_=g0acc[:, c, :]),
                  "g0d", 16)

    qjobs = []
    for (src, dst, rows_total, width) in ((qg_d, qgq_o, 352, 1024),
                                          (qu_d, quq_o, 352, 1024),
                                          (qd_d, qdq_o, 128, 2816)):
        r0 = 0
        while r0 < rows_total:
            rr = min(128, rows_total - r0)
            c0 = 0
            while c0 < width:
                cc = min(TT, width - c0)
                qjobs.append((src, dst, r0, rr, c0, cc))
                c0 += cc
            r0 += rr

    for qi, (srcd, dst, r0, rr, c0, cc) in enumerate(qjobs):

        @blk.sync
        def _(sy, srcd=srcd, r0=r0, rr=rr, c0=c0, cc=cc, qi=qi):
            if qi == 0:
                p.wait(sy, "g0d")
            else:
                p.wait(sy, "qdm")
            p.inc(sy.dma_start(out=qin[0:rr, 0:cc], in_=srcd[r0:r0 + rr, c0:c0 + cc]),
                  "qqin", 16)

        emit_fp4_qdq(p, blk, qin[0:rr, 0:cc], qa, qsc, qrc, qc1, qc2, qout, rr, cc)

        @blk.sync
        def _(sy, dst=dst, r0=r0, rr=rr, c0=c0, cc=cc):
            p.wait(sy, "qqou")
            p.inc(sy.dma_start(out=dst[r0:r0 + rr, c0:c0 + cc], in_=qout[0:rr, 0:cc]),
                  "qdm", 16)

    @blk.sync
    def _(sy):
        p.wait(sy, "g0d")
        p.wait(sy, "kvdm")
        p.wait(sy, "fqdm")
        p.wait(sy, "qdm")

    return p


def kernel(**inputs):
    z = inputs
    BF = ml_dtypes.bfloat16

    def b128(b):
        return np.ascontiguousarray(np.asarray(b, np.float32).reshape(DC, 128).T)

    x_full = np.asarray(z["L0_attn_x"], np.float32).reshape(16384, 1024)

    pA = build_launch_A()
    inA = [dict(
        x=host_fm(x_full[c * 2048:(c + 1) * 2048]), ident_in=IDENT128,
        wqkv=z["L0_attn_W_qkv"], rf=z["L0_attn_RF"], router_w=z["L0_ffn_router_w"],
        gate_w=z["L0_ffn_gate_w"], up_w=z["L0_ffn_up_w"], down_w=z["L0_ffn_down_w"],
        sgw=z["L0_ffn_shared_gate_w"], suw=z["L0_ffn_shared_up_w"],
        sdw=z["L0_ffn_shared_down_w"],
        qg=np.ascontiguousarray(z["L1_ffn_w_gate"][c * 352:(c + 1) * 352]),
        qu=np.ascontiguousarray(z["L1_ffn_w_up"][c * 352:(c + 1) * 352]),
        qd=np.ascontiguousarray(z["L1_ffn_w_down"][c * 128:(c + 1) * 128]))
        for c in range(8)]
    rA = _run(pA, inA)

    kvp_batch = [np.stack([rA[b * 4 + i]["kvp"] for i in range(4)]).astype(np.float32)
                 for b in range(2)]
    theta = (10000.0 ** (-np.arange(32, dtype=np.float32) / 32.0)).astype(np.float32)

    pB = build_launch_B()
    inB = []
    for c in range(8):
        fr = (np.arange(2048, dtype=np.float32) + (c % 4) * 2048)[:, None] * theta[None, :]
        inB.append(dict(
            x=host_fm(x_full[c * 2048:(c + 1) * 2048]), g0=rA[c]["g0"],
            phiq=rA[c]["phiq"], kvp4=kvp_batch[c // 4],
            wout=z["L0_attn_W_out"], bout=b128(z["L0_attn_b_out"]),
            wqT=np.ascontiguousarray(z["L1_attn_Wq"].T),
            wkT=np.ascontiguousarray(z["L1_attn_Wk"].T),
            wvT=np.ascontiguousarray(z["L1_attn_Wv"].T),
            bq=b128(z["L1_attn_bq"]), bk=b128(z["L1_attn_bk"]),
            bv=np.asarray(z["L1_attn_bv"])[None, :],
            cosT=np.ascontiguousarray(np.tile(np.cos(fr).T, (4, 1))),
            sinT=np.ascontiguousarray(np.tile(np.sin(fr).T, (4, 1)))))
    rB = _run(pB, inB)

    wgq = np.concatenate([rA[c]["qgq"] for c in range(8)], 0)
    wuq = np.concatenate([rA[c]["quq"] for c in range(8)], 0)
    wdq = np.concatenate([rA[c]["qdq"] for c in range(8)], 0)
    kfull = [np.concatenate([rB[b * 4 + i]["k1"].reshape(16, 64, 2048)
                             for i in range(4)], axis=2) for b in range(2)]
    vfull = [np.concatenate([rB[b * 4 + i]["v1"].reshape(2048, 16, 64).transpose(1, 0, 2)
                             for i in range(4)], axis=1) for b in range(2)]
    kn2f = [np.concatenate([rB[b * 4 + i]["kn2"] for i in range(4)], axis=1)
            for b in range(2)]

    pC = build_launch_C()
    inC = [dict(
        x1=rB[c]["x1"], q1=np.ascontiguousarray(rB[c]["q1"].reshape(16, 64, 2048)),
        kfull=np.ascontiguousarray(kfull[c // 4]),
        vfull=np.ascontiguousarray(vfull[c // 4]),
        qn2=rB[c]["qn2"], kn2f=np.ascontiguousarray(kn2f[c // 4]),
        woT=np.ascontiguousarray(z["L1_attn_Wo"].T), bo=b128(z["L1_attn_bo"]),
        wgqT=np.ascontiguousarray(wgq.T), wuqT=np.ascontiguousarray(wuq.T),
        wdqT=np.ascontiguousarray(wdq.T), ident_in=IDENT128,
        ln_g=b128(z["L2_attn_ln_g"]), ln_b=b128(z["L2_attn_ln_b"]),
        w_in=z["L2_attn_w_in"], b_in=b128(z["L2_attn_b_in"])) for c in range(8)]
    rC = _run(pC, inC)

    ll = [np.concatenate([rC[b * 4 + i]["left_lin"] for i in range(4)], axis=1)
          for b in range(2)]

    pD = build_launch_D()
    cw = z["L2_attn_conv_w"][:, 0, :]
    cb = z["L2_attn_conv_b"]
    inD = []
    for c in range(8):
        b, s = c // 4, c % 4
        inD.append(dict(
            ll=np.ascontiguousarray(ll[b][s * 256:(s + 1) * 256]),
            conv_w=np.ascontiguousarray(
                cw[s * 256:(s + 1) * 256].reshape(2, 128, 4).transpose(1, 0, 2)),
            conv_b=np.ascontiguousarray(cb[s * 256:(s + 1) * 256].reshape(2, 128).T)))
    rD = _run(pD, inD)
    lc_dev = [np.concatenate([rD[b * 4 + i]["lc"] for i in range(4)], axis=0)
              for b in range(2)]

    pE = build_launch_E()
    Aval = -np.exp(np.asarray(z["L2_attn_A_log"], np.float64)).astype(np.float32)
    inE = []
    for c in range(8):
        b, s = c // 4, c % 4
        inE.append(dict(
            lcf=np.ascontiguousarray(lc_dev[b]),
            own=np.ascontiguousarray(lc_dev[b][s * 256:(s + 1) * 256]),
            wdt=np.ascontiguousarray(z["L2_attn_w_dt"][:, s * 256:(s + 1) * 256]),
            bdt=np.ascontiguousarray(
                z["L2_attn_b_dt"][s * 256:(s + 1) * 256].reshape(2, 128).T),
            wbc=z["L2_attn_w_bc"],
            A=np.ascontiguousarray(
                Aval[s * 256:(s + 1) * 256].reshape(2, 128, 16).transpose(1, 0, 2)),
            Dssm=np.ascontiguousarray(
                z["L2_attn_D_ssm"][s * 256:(s + 1) * 256].reshape(2, 128).T)))
    rE = _run(pE, inE)
    y_dev = [np.concatenate([rE[b * 4 + i]["yssm"] for i in range(4)], axis=0)
             for b in range(2)]

    perm = np.concatenate([np.arange(0, 1024, 2), np.arange(1, 1024, 2)])
    i_ = np.arange(512, dtype=np.float32)
    theta3 = (10000.0 ** (-2.0 * (i_ - 1.0) / 1024.0)).astype(np.float32)
    wqi, wki, wvi = np.split(z["L3_attn_in_proj_w"], 3, axis=0)
    bqi, bki, bvi = np.split(z["L3_attn_in_proj_b"], 3, axis=0)
    pF = build_launch_F()
    inF = []
    for c in range(8):
        b, s = c // 4, c % 4
        pos = np.arange(2048, dtype=np.float32) + s * 2048
        ang = pos[:, None] * theta3[None, :]
        inF.append(dict(
            x2=rC[c]["x2"], yssm=np.ascontiguousarray(
                y_dev[b][:, s * 2048:(s + 1) * 2048]),
            gate_wT=np.ascontiguousarray(z["L2_ffn_gate_w"].T),
            e_gate=z["L2_ffn_w_gate"], e_up=z["L2_ffn_w_up"],
            e_down=z["L2_ffn_w_down"],
            sw_gate=z["L2_ffn_sw_gate"], sw_up=z["L2_ffn_sw_up"],
            sw_down=z["L2_ffn_sw_down"],
            w_out=z["L2_attn_w_out"], b_out=b128(z["L2_attn_b_out"]),
            ln_g=b128(z["L2_attn_ln_g"]), ln_b=b128(z["L2_attn_ln_b"]),
            wq3T=np.ascontiguousarray(z["L3_attn_wq"].T[:, perm]),
            wk3T=np.ascontiguousarray(z["L3_attn_wk"].T[:, perm]),
            wv3T=np.ascontiguousarray(z["L3_attn_wv"].T[:, perm]),
            wqiT=np.ascontiguousarray(wqi.T[perm, :]),
            wkiT=np.ascontiguousarray(wki.T[perm, :]),
            wviT=np.ascontiguousarray(wvi.T[perm, :]),
            bqi=b128(bqi), bki=b128(bki),
            bvi=np.ascontiguousarray(bvi[None, :]),
            cos3=np.ascontiguousarray(np.tile(np.cos(ang).T.astype(np.float32), (2, 1))),
            sin3=np.ascontiguousarray(np.tile(np.sin(ang).T.astype(np.float32), (2, 1))),
            ident_in=IDENT128))
    rF = _run(pF, inF)

    k3f = [np.ascontiguousarray(
        np.concatenate([rF[b * 4 + i]["k3h"] for i in range(4)], axis=1)
        .reshape(16, 64, 8192)) for b in range(2)]
    v3f = [np.ascontiguousarray(
        np.concatenate([rF[b * 4 + i]["v3h"] for i in range(4)], axis=0)
        .reshape(8192, 16, 64).transpose(1, 0, 2)) for b in range(2)]
    kn3f = [np.concatenate([rF[b * 4 + i]["kn3"] for i in range(4)], axis=1)
            for b in range(2)]
    NCK = T // 128
    gmask = []
    for r in range(4):
        qoff = r * 2048
        m = np.zeros((4, NCK, 128, 512), np.float32)
        for tqb in range(4):
            for ck in range(NCK):
                k0 = ck * 128
                q0 = qoff + tqb * 512
                if k0 + 127 <= q0:
                    continue
                if k0 > q0 + 511:
                    m[tqb, ck] = -1e30
                else:
                    ki = k0 + np.arange(128)[:, None]
                    qi = q0 + np.arange(512)[None, :]
                    m[tqb, ck] = np.where(ki <= qi, 0.0, -1e30)
        gmask.append(np.ascontiguousarray(m.astype(BF)))
    pG = build_launch_G()
    inG = []
    for c in range(8):
        b, s = c // 4, c % 4
        kn = np.array(kn3f[b], np.float32)
        kn[:, (s + 1) * 2048:] = 0.0
        inG.append(dict(
            q3=np.ascontiguousarray(rF[c]["q3h"].reshape(16, 64, 2048)),
            k3f=k3f[b], v3f=v3f[b], qn3=rF[c]["qn3"],
            kn3f=np.ascontiguousarray(kn),
            woT=np.ascontiguousarray(z["L3_attn_out_w"].T.astype(np.float32)),
            bo=b128(z["L3_attn_out_b"]), mask=gmask[s]))
    rG = _run(pG, inG)

    hconsts = host_h_consts()
    hmasks = [host_h_masks(r) for r in range(4)]
    xcur = [np.concatenate([rF[b * 4 + i]["xn3"] for i in range(4)], axis=1)
            for b in range(2)]
    x4 = [None] * 8
    for li in range(6):
        wts = host_h_weights(z["L3_ffn_Wqkv"][li], z["L3_ffn_Wo"][li],
                             z["L3_ffn_W1"][li], z["L3_ffn_b1"][li],
                             z["L3_ffn_W2"][li], z["L3_ffn_b2"][li],
                             z["L3_ffn_ln_g"][li], z["L3_ffn_ln_b"][li])
        tabs = [host_h_tables(r * 2048, z["L3_ffn_q_scale"][li],
                              z["L3_ffn_k_scale"][li]) for r in range(4)]
        final = (li == 5)
        pH = build_launch_H(final=final)
        xpad = [np.concatenate([np.zeros((1024, 512), np.float32), xcur[b]], axis=1)
                for b in range(2)]
        inH = []
        for c in range(8):
            b, s = c // 4, c % 4
            d = dict(xin=np.ascontiguousarray(xpad[b][:, s * 2048:s * 2048 + 2560]),
                     cs4=tabs[s], mskh=hmasks[s], **wts, **hconsts)
            if final:
                d["x3f"] = rF[c]["x3"]
                d["s3f"] = rG[c]["s3"]
            inH.append(d)
        rH = _run(pH, inH)
        if final:
            x4 = [rH[c]["x4"] for c in range(8)]
        else:
            xcur = [np.concatenate([rH[b * 4 + i]["xout"][:, 512:2560]
                                    for i in range(4)], axis=1) for b in range(2)]

    out = np.empty((2, 8192, 1024), np.float32)
    for c in range(8):
        b, s = c // 4, c % 4
        out[b, s * 2048:(s + 1) * 2048, :] = x4[c].T
    return out



def build_launch_B():
    p = Prog()
    x_d = p.din("x", [D, TPC])
    g0_d = p.din("g0", [D, TPC])
    phiq_d = p.din("phiq", [NH, DH, TPC], BF16)
    kvp_d = p.din("kvp4", [4, NH, DH, DH + 1])
    wout_d = p.din("wout", [D, D])
    bout_d = p.din("bout", [128, DC])
    wq_d = p.din("wqT", [D, D])
    wk_d = p.din("wkT", [D, D])
    wv_d = p.din("wvT", [D, D])
    bq_d = p.din("bq", [128, DC])
    bk_d = p.din("bk", [128, DC])
    bv_d = p.din("bv", [1, D])
    cos_d = p.din("cosT", [128, TPC])
    sin_d = p.din("sinT", [128, TPC])

    x1_o = p.dout("x1", [D, TPC])
    q1_o = p.dout("q1", [D, TPC], BF16)
    k1_o = p.dout("k1", [D, TPC], BF16)
    v1_o = p.dout("v1", [TPC, D], BF16)
    qn2_o = p.dout("qn2", [NH, TPC])
    kn2_o = p.dout("kn2", [NH, TPC])

    xt = p.sb("xt", [128, DC, TT])
    gt = p.sb("gt", [128, DC, TT])
    xn = p.sb("xn", [128, DC, TPC], BF16)
    sq = p.sb("sq", [128, 1, TT])
    srow = p.sb("srow", [1, 2, TT])
    ones = p.sb("ones", [128, 2])
    tmp = p.sb("tmpa", [128, TT])
    cst = p.sb("cst", [128, 2])
    kvt = p.sb("kvt", [64, NH, DH + 1])
    kvb = p.sb("kvb", [64, NH, DH + 1], BF16)
    fqs = p.sb("fqs", [64, 2, TT], BF16)
    ofm = p.sb("ofm", [128, DC, TPC], BF16)
    nrow = p.sb("nrow", [1, TT])
    wo_sb = p.sb("wo_sb", [128, DC, D], BF16)
    wq_sb = p.sb("wq_sb", [128, DC, D], BF16)
    wk_sb = p.sb("wk_sb", [128, DC, D], BF16)
    wv_sb = p.sb("wv_sb", [128, DC, D], BF16)
    bias = p.sb("bias", [128, 3, DC])
    bvb = p.sb("bvb", [128, DC, 128])
    bvrow = p.sb("bvrow", [1, D])
    s0t = p.sb("s0t", [128, TT])
    rps = p.sb("rps", [128, TT])
    rpb = p.sb("rpb", [128, TT], BF16)
    cs_t = p.sb("cs_t", [128, 2, TT])
    eons = p.sb("eons", [128, 2])
    n2r = p.sb("n2r", [2, TT])
    vtile = rpb

    P0 = p.ps("P0", [128, TT])
    P1 = p.ps("P1", [128, TT])
    P2 = p.ps("P2", [128, TT])
    P3 = p.ps("P3", [128, TT])
    P4 = p.ps("P4", [128, TT])
    P5 = p.ps("P5", [128, TT])

    blk = p.block()

    @blk.vector
    def _(v):
        v.memset(ones[:, :], 1.0)
        v.memset(cst[:, 0:1], 1e-5)
        v.memset(cst[:, 1:2], 1e-6)
        v.memset(eons[:, :], 0.0)
        v.memset(eons[0:64, 0:1], 1.0)
        p.inc(v.memset(eons[64:128, 1:2], 1.0), "init", 1)

    @blk.scalar
    def _(s):
        p.wait(s, "init")

    @blk.tensor
    def _(t):
        p.wait(t, "init")

    @blk.gpsimd
    def _(g):
        for i in range(4):
            if i > 0:
                g.wait_ge(p.sem("kvl"), 16 * i)
            p.inc(g.dma_start(out=kvt[:, :, :],
                              in_=kvp_d[i].rearrange("h r c -> r h c"),
                              accum_op=(ALU.add if i > 0 else ALU.bypass)), "kvl", 16)
        p.inc(g.dma_start(out=bias[:, 0, :], in_=bout_d[:, :]), "bls", 16)
        p.inc(g.dma_start(out=bias[:, 1, :], in_=bq_d[:, :]), "bls", 16)
        p.inc(g.dma_start(out=bias[:, 2, :], in_=bk_d[:, :]), "bls", 16)
        p.inc(g.dma_start(out=bvrow[0:1, :], in_=bv_d[:, :]), "bls", 16)
        p.inc(g.dma_start(out=wo_sb[:, :, :],
                          in_=wout_d.rearrange("(c p) m -> p c m", p=128)), "wls", 16)
        p.inc(g.dma_start(out=wq_sb[:, :, :],
                          in_=wq_d.rearrange("(c p) m -> p c m", p=128)), "wls", 16)
        p.inc(g.dma_start(out=wk_sb[:, :, :],
                          in_=wk_d.rearrange("(c p) m -> p c m", p=128)), "wls", 16)
        p.inc(g.dma_start(out=wv_sb[:, :, :],
                          in_=wv_d.rearrange("(c p) m -> p c m", p=128)), "wls", 16)

    @blk.vector
    def _(v):
        p.wait(v, "kvl")
        p.inc(v.tensor_copy(kvb[:, :, :], kvt[:, :, :]), "kvc", 1)

    @blk.tensor
    def _(t):
        p.wait(t, "bls")
        for half in range(2):
            if half == 1:
                p.wait(t, "bvc", 1)
            p.inc(t.matmul(P5[:, :], ones[0:1, 0:1].to_broadcast([1, 128]),
                           bvrow[0:1, half * TT:(half + 1) * TT], start=True, stop=True),
                  "bvm", 1)

    @blk.vector
    def _(v):
        for half in range(2):
            v.wait_ge(p.sem("bvm"), half + 1)
            p.inc(v.tensor_copy(bvb.rearrange("p c f -> p (c f)")[:, half * TT:(half + 1) * TT],
                                P5[:, :]), "bvc", 1)

    for h in range(NH):
        for it in range(NTT):
            idx = h * NTT + it
            tsl = slice(it * TT, (it + 1) * TT)

            @blk.sync
            def _(sy, h=h, idx=idx, tsl=tsl):
                if idx >= 2:
                    p.wait(sy, "po_mm", idx - 1)
                p.inc(sy.dma_start(out=fqs[:, idx % 2, :], in_=phiq_d[h, :, tsl]),
                      "fql", 16)

            @blk.tensor
            def _(t, h=h, idx=idx):
                t.wait_ge(p.sem("fql"), 16 * (idx + 1))
                if idx == 0:
                    p.wait(t, "kvc")
                    p.wait(t, "bvc")
                if idx >= 2:
                    p.wait(t, "po_ev", idx - 1)
                ps = P0 if idx % 2 == 0 else P1
                p.inc(t.matmul(ps[0:65, :], kvb[:, h, :], fqs[:, idx % 2, :],
                               start=True, stop=True), "po_mm", 1)

            @blk.vector
            def _(v, idx=idx):
                v.wait_ge(p.sem("po_mm"), idx + 1)
                if idx >= 1:
                    p.wait(v, "po_bc", idx)
                ps = P0 if idx % 2 == 0 else P1
                p.vsf(v, v.tensor_scalar_add(nrow[0:1, :], ps[64:65, :], cst[0:1, 1:2]))
                p.vsf(v, v.reciprocal(nrow[0:1, :], nrow[0:1, :]))
                p.inc(v.tensor_copy(s0t[0:64, :], ps[0:64, :]), "po_cp", 1)

            @blk.tensor
            def _(t, idx=idx):
                t.wait_ge(p.sem("po_cp"), idx + 1)
                if idx >= 1:
                    p.wait(t, "po_ev", idx)
                p.inc(t.matmul(P2[0:64, :], ones[0:1, 0:1].to_broadcast([1, 64]),
                               nrow[0:1, :], start=True, stop=True), "po_bc", 1)

            @blk.vector
            def _(v, h=h, idx=idx, tsl=tsl):
                v.wait_ge(p.sem("po_bc"), idx + 1)
                p.inc(v.tensor_mul(ofm[(h % 2) * 64:(h % 2) * 64 + 64, h // 2, tsl],
                                   s0t[0:64, :], P2[0:64, :]), "po_ev", 1)

    for it in range(NTT):
        tsl = slice(it * TT, (it + 1) * TT)

        @blk.sync
        def _(sy, it=it, tsl=tsl):
            if it >= 1:
                p.wait(sy, "x1w", 16 * it)
                p.wait(sy, "ln_xn", it)
            p.inc(sy.dma_start(out=xt[:, :, :],
                               in_=x_d.rearrange("(c p) t -> p c t", p=128)[:, :, tsl]),
                  "ln_xld", 16)
            p.inc(sy.dma_start(out=gt[:, :, :],
                               in_=g0_d.rearrange("(c p) t -> p c t", p=128)[:, :, tsl]),
                  "g0l", 16)

        for oc in range(DC):
            idx = it * DC + oc

            @blk.tensor
            def _(t, it=it, oc=oc, idx=idx, tsl=tsl):
                if idx == 0:
                    p.wait(t, "wls")
                    p.wait(t, "po_ev")
                if idx >= 2:
                    t.wait_ge(p.sem("s0ev"), idx - 1)
                ps = P0 if idx % 2 == 0 else P1
                for c in range(DC):
                    mm = t.matmul(ps[:, :], wo_sb[:, c, oc * 128:(oc + 1) * 128],
                                  ofm[:, c, tsl], start=(c == 0), stop=(c == DC - 1))
                p.inc(mm, "s0mm", 1)

            @blk.vector
            def _(v, it=it, oc=oc, idx=idx):
                v.wait_ge(p.sem("s0mm"), idx + 1)
                v.wait_ge(p.sem("ln_xld"), 16 * (it + 1))
                v.wait_ge(p.sem("g0l"), 16 * (it + 1))
                ps = P0 if idx % 2 == 0 else P1
                p.inc(v.tensor_scalar(s0t[:, :], ps[:, :], bias[:, 0, oc:oc + 1],
                                      None, ALU.add), "s0ev", 1)
                v.tensor_mul(s0t[:, :], s0t[:, :], gt[:, oc, :])
                p.inc(v.tensor_add(xt[:, oc, :], xt[:, oc, :], s0t[:, :]), "x1c", 1)

        @blk.sync
        def _(sy, it=it, tsl=tsl):
            p.wait(sy, "x1c")
            p.inc(sy.dma_start(out=x1_o.rearrange("(c p) t -> p c t", p=128)[:, :, tsl],
                               in_=xt[:, :, :]), "x1w", 16)

        emit_ln_tile(p, blk, f"b{it}", xt, [xn[:, c, tsl] for c in range(DC)],
                     sq, srow, P2, P3, P4, P5, ones, tmp=tmp, eps_ap=cst[0:1, 0:1])

    for which, w_sb, bcol, out_d, n2_d in (("q", wq_sb, 1, q1_o, qn2_o),
                                           ("k", wk_sb, 2, k1_o, kn2_o)):
        for oc in range(DC):
            for it in range(NTT):
                idx = oc * NTT + it
                tsl = slice(it * TT, (it + 1) * TT)

                @blk.sync
                def _(sy, idx=idx, tsl=tsl, which=which):
                    if which == "k" and idx == 0:
                        p.wait(sy, "qev", 32)
                    if idx >= 1:
                        p.wait(sy, which + "ev", idx)
                    p.inc(sy.dma_start(out=cs_t[:, 0, :], in_=cos_d[:, tsl]), which + "cs", 16)
                    p.inc(sy.dma_start(out=cs_t[:, 1, :], in_=sin_d[:, tsl]), which + "cs", 16)

                @blk.tensor
                def _(t, oc=oc, idx=idx, tsl=tsl, which=which, w_sb=w_sb):
                    if which == "q" and idx == 0:
                        p.wait(t, "ln_xn")
                    if idx >= 2:
                        p.wait(t, which + "pe", idx - 1)
                    ps = P0 if idx % 2 == 0 else P1
                    for c in range(DC):
                        mm = t.matmul(ps[:, :], w_sb[:, c, oc * 128:(oc + 1) * 128],
                                      xn[:, c, tsl], start=(c == 0), stop=(c == DC - 1))
                    p.inc(mm, which + "mm", 1)

                @blk.vector
                def _(v, oc=oc, idx=idx, which=which, bcol=bcol):
                    v.wait_ge(p.sem(which + "mm"), idx + 1)
                    v.wait_ge(p.sem(which + "cs"), 32 * (idx + 1))
                    if which == "k" and idx == 0:
                        p.wait(v, "qsq", 32)
                        p.wait(v, "qqw", 16 * 32)
                    if idx >= 1:
                        p.wait(v, which + "sq", idx)
                        p.wait(v, which + "qw", 16 * idx)
                    ps = P0 if idx % 2 == 0 else P1
                    p.inc(v.tensor_scalar(rps[:, :], ps[:, :], bias[:, bcol, oc:oc + 1],
                                          None, ALU.add), which + "pe", 1)
                    for j in range(2):
                        lo = slice(64 * j, 64 * j + 32)
                        hi = slice(64 * j + 32, 64 * j + 64)
                        v.tensor_mul(tmp[lo, :], rps[lo, :], cs_t[lo, 0, :])
                        v.tensor_mul(s0t[lo, :], rps[hi, :], cs_t[hi, 1, :])
                        v.tensor_mul(tmp[hi, :], rps[hi, :], cs_t[hi, 0, :])
                        v.tensor_mul(s0t[hi, :], rps[lo, :], cs_t[lo, 1, :])
                        v.tensor_sub(rps[lo, :], tmp[lo, :], s0t[lo, :])
                        v.tensor_add(rps[hi, :], tmp[hi, :], s0t[hi, :])
                    p.inc(v.tensor_copy(rpb[:, :], rps[:, :]), which + "ev", 1)

                @blk.scalar
                def _(s, idx=idx, which=which):
                    s.wait_ge(p.sem(which + "ev"), idx + 1)
                    p.inc(s.activation(out=sq[:, 0, :], in_=rps[:, :], func=AF.Square),
                          which + "sq", 1)

                @blk.tensor
                def _(t, idx=idx, which=which):
                    t.wait_ge(p.sem(which + "sq"), idx + 1)
                    if which == "k" and idx == 0:
                        p.wait(t, "qnc", 32)
                    if idx >= 1:
                        p.wait(t, which + "nc", idx)
                    p.inc(t.matmul(P2[0:2, :], eons[:, :], sq[:, 0, :],
                                   start=True, stop=True), which + "n2", 1)

                @blk.vector
                def _(v, idx=idx, which=which):
                    v.wait_ge(p.sem(which + "n2"), idx + 1)
                    if idx >= 1:
                        p.wait(v, which + "nw", 16 * idx)
                    p.inc(v.tensor_copy(n2r[:, :], P2[0:2, :]), which + "nc", 1)

                @blk.scalar
                def _(s, oc=oc, idx=idx, tsl=tsl, which=which, out_d=out_d, n2_d=n2_d):
                    s.wait_ge(p.sem(which + "nc"), idx + 1)
                    p.inc(s.dma_start(
                        out=n2_d.rearrange("(o j) t -> j o t", j=2)[:, oc, tsl],
                        in_=n2r[:, :]), which + "nw", 16)
                    s.wait_ge(p.sem(which + "ev"), idx + 1)
                    p.inc(s.dma_start(
                        out=out_d.rearrange("(c p) t -> p c t", p=128)[:, oc, tsl],
                        in_=rpb[:, :]), which + "qw", 16)

    for tc in range(NTC):
        for half in range(2):
            idx = tc * 2 + half

            @blk.tensor
            def _(t, tc=tc, half=half, idx=idx):
                if idx == 0:
                    p.wait(t, "kev")
                if idx >= 2:
                    p.wait(t, "vtev", idx - 1)
                ps = P3 if idx % 2 == 0 else P4
                for c in range(DC):
                    mm = t.matmul(ps[:, :], xn[:, c, tc * 128:(tc + 1) * 128],
                                  wv_sb[:, c, half * TT:(half + 1) * TT],
                                  start=(c == 0), stop=(c == DC - 1))
                p.inc(mm, "vtmm", 1)

            @blk.vector
            def _(v, half=half, idx=idx):
                v.wait_ge(p.sem("vtmm"), idx + 1)
                if idx >= 1:
                    p.wait(v, "vtw", 16 * idx)
                ps = P3 if idx % 2 == 0 else P4
                p.inc(v.tensor_tensor(vtile[:, :], ps[:, :],
                                      bvb.rearrange("p c f -> p (c f)")
                                      [:, half * TT:(half + 1) * TT], ALU.add),
                      "vtev", 1)

            @blk.scalar
            def _(s, tc=tc, half=half, idx=idx):
                s.wait_ge(p.sem("vtev"), idx + 1)
                p.inc(s.dma_start(out=v1_o[tc * 128:(tc + 1) * 128,
                                           half * TT:(half + 1) * TT],
                                  in_=vtile[:, :]), "vtw", 16)

    @blk.sync
    def _(sy):
        p.wait(sy, "x1w")
        p.wait(sy, "qqw")
        p.wait(sy, "kqw")
        p.wait(sy, "qnw")
        p.wait(sy, "knw")
        p.wait(sy, "vtw")
    return p



def build_launch_C():
    p = Prog()
    x1_d = p.din("x1", [D, TPC])
    q1_d = p.din("q1", [NH, DH, TPC], BF16)
    kf_d = p.din("kfull", [NH, DH, T], BF16)
    vf_d = p.din("vfull", [NH, T, DH], BF16)
    qn2_d = p.din("qn2", [NH, TPC])
    kn2_d = p.din("kn2f", [NH, T])
    wo_d = p.din("woT", [D, D])
    bo_d = p.din("bo", [128, DC])
    wgq_d = p.din("wgqT", [D, 2816], BF16)
    wuq_d = p.din("wuqT", [D, 2816], BF16)
    wdq_d = p.din("wdqT", [2816, D], BF16)
    id_d = p.din("ident_in", [128, 128])
    lng_d = p.din("ln_g", [128, DC])
    lnb_d = p.din("ln_b", [128, DC])
    win_d = p.din("w_in", [D, D])
    bin_d = p.din("b_in", [128, DC])

    x2_o = p.dout("x2", [D, TPC])
    ll_o = p.dout("left_lin", [D, TPC])

    xn_dram = p.dtmp("xnf32d", [D, TPC])
    mb_dram = p.dtmp("mbd", [1, NH])

    xt = p.sb("xt", [128, DC, TT])
    xf = p.sb("xf", [128, DC, TT])
    xn2t = p.sb("xn2t", [128, DC, TT], BF16)
    g1 = p.sb("g1", [128, DC, TPC], BF16)
    sq = p.sb("sq", [128, 1, TT])
    srow = p.sb("srow", [1, 2, TT])
    ones = p.sb("ones", [128, 2])
    cst = p.sb("cst", [128, 2])
    tmp = p.sb("tmpa", [128, TT])
    identb = p.sb("identb", [128, 128], BF16)
    xnq = p.sb("xnq", [128, DC, 128], BF16)
    xtm = p.sb("xtm", [128, D])
    gtm = p.sb("gtm", [128, 2816])
    utm = p.sb("utm", [128, 2816])
    qa = p.sb("qa", [128, 2816])
    qc1 = p.sb("qc1", [128, 2816])
    qc2 = p.sb("qc2", [128, 2816])
    qsc = p.sb("qsc", [128, 2816 // 16, 1])
    qrc = p.sb("qrc", [128, 2816 // 16, 1])
    hq = p.sb("hq", [128, 2816], BF16)
    khb = p.sb("khb", [128, T], BF16)
    wst = khb.rearrange("p (c f) -> p c f", f=TT)[:, 0:DC, :]
    wstU = khb.rearrange("p (c f) -> p c f", f=TT)[:, DC:2 * DC, :]
    kh = khb[0:64, :]
    wst2 = p.sb("wst2", [128, 2, DC, 128], BF16)
    vh = gtm.bitcast(BF16)[:, 0:(T // 128) * (DH + 1)].rearrange(
        "p (c d) -> p c d", d=DH + 1)
    qh = utm.bitcast(BF16)[0:64, 0:TT]
    pbs = utm.bitcast(BF16)[:, 1024:1024 + 2 * TT].rearrange("p (b f) -> p b f", f=TT)
    ofm = qc1.bitcast(BF16)[:, 0:DC * TT].rearrange("p (c f) -> p c f", f=TT)
    onr = qc2[0:1, 0:TT]
    oev = qc2[0:64, TT:2 * TT]
    s1t = qc2[:, 2 * TT:3 * TT]
    mrow = p.sb("mrow", [1, NH])
    mb = p.sb("mb", [128, NH])
    nsc = p.sb("nsc", [16, 4])
    gb = p.sb("gb", [128, 4, DC])
    qn_sb = p.sb("qn_sb", [16, TPC])
    kn_sb = p.sb("kn_sb", [16, TPC])

    P0 = p.ps("P0", [128, TT])
    P1 = p.ps("P1", [128, TT])
    P2 = p.ps("P2", [128, TT])
    P3 = p.ps("P3", [128, TT])
    P4 = p.ps("P4", [128, TT])
    P5 = p.ps("P5", [128, TT])
    PTf = p.ps("PTf", [128, 128])
    PTb = p.ps("PTb", [128, 128], BF16)

    blk = p.block()

    @blk.vector
    def _(v):
        v.memset(ones[:, :], 1.0)
        v.memset(cst[:, 0:1], 1e-5)
        p.inc(v.memset(cst[:, 1:2], 1e-38), "init", 1)

    @blk.scalar
    def _(s):
        p.wait(s, "init")

    @blk.tensor
    def _(t):
        p.wait(t, "init")

    @blk.gpsimd
    def _(g):
        p.inc(g.dma_start(out=identb[:, :], in_=id_d[:, :]), "wld", 16)
        p.inc(g.dma_start(out=gb[:, 0, :], in_=lng_d[:, :]), "wld", 16)
        p.inc(g.dma_start(out=gb[:, 1, :], in_=lnb_d[:, :]), "wld", 16)
        p.inc(g.dma_start(out=gb[:, 2, :], in_=bin_d[:, :]), "wld", 16)
        p.inc(g.dma_start(out=gb[:, 3, :], in_=bo_d[:, :]), "wld", 16)

    @blk.sync
    def _(sy):
        p.inc(sy.dma_start(out=qn_sb[:, :], in_=qn2_d[:, :]), "nld", 16)

    @blk.vector
    def _(v):
        p.wait(v, "nld")
        p.vsf(v, v.reduce_max(nsc[:, 0:1], qn_sb[:, :], axis=AX.X))

    for kc4 in range(4):
        @blk.sync
        def _(sy, kc4=kc4):
            if kc4 > 0:
                p.wait(sy, "nkm", kc4)
            p.inc(sy.dma_start(out=kn_sb[:, :],
                               in_=kn2_d[:, kc4 * TPC:(kc4 + 1) * TPC]), "nld2", 16)

        @blk.vector
        def _(v, kc4=kc4):
            v.wait_ge(p.sem("nld2"), 16 * (kc4 + 1))
            p.vsf(v, v.reduce_max(nsc[:, 3:4], kn_sb[:, :], axis=AX.X))
            if kc4 == 0:
                p.inc(v.tensor_copy(nsc[:, 1:2], nsc[:, 3:4]), "nkm", 1)
            else:
                p.inc(v.tensor_max(nsc[:, 1:2], nsc[:, 1:2], nsc[:, 3:4]), "nkm", 1)

    @blk.vector
    def _(v):
        p.wait(v, "nkm")
        p.vsf(v, v.engine_nop())
        p.inc(v.tensor_mul(nsc[:, 2:3], nsc[:, 0:1], nsc[:, 1:2]), "nm1", 1)

    @blk.scalar
    def _(s):
        p.wait(s, "nm1")
        p.inc(s.activation(out=nsc[:, 3:4], in_=nsc[:, 2:3], func=AF.Sqrt), "nm2", 1)

    @blk.vector
    def _(v):
        p.wait(v, "nm2")
        p.inc(v.tensor_scalar_mul(nsc[:, 2:3], nsc[:, 3:4], -0.125), "nm3", 1)

    @blk.sync
    def _(sy):
        p.wait(sy, "nm3")
        p.inc(sy.dma_start(out=mb_dram.rearrange("o h -> (o h)")[None, :],
                           in_=nsc[:, 2:3]), "nm4", 16)
        sy.wait_ge(p.sem("nm4"), 16)
        p.inc(sy.dma_start(out=mrow[0:1, :], in_=mb_dram[0:1, :]), "nm5", 16)

    @blk.tensor
    def _(t):
        t.wait_ge(p.sem("nm5"), 16)
        p.inc(t.matmul(PTf[:, 0:NH], ones[0:1, 0:1].to_broadcast([1, 128]),
                       mrow[0:1, :], start=True, stop=True), "nm6", 1)

    @blk.vector
    def _(v):
        p.wait(v, "nm6")
        p.inc(v.tensor_copy(mb[:, :], PTf[:, 0:NH]), "nm7", 1)

    for it in range(NTT):
        tsl = slice(it * TT, (it + 1) * TT)

        @blk.sync
        def _(sy, it=it, tsl=tsl):
            if it >= 1:
                p.wait(sy, "ln_xn", it)
                p.wait(sy, "xfw", 16 * it)
            p.inc(sy.dma_start(out=xt[:, :, :],
                               in_=x1_d.rearrange("(c p) t -> p c t", p=128)[:, :, tsl]),
                  "ln_xld", 16)

        emit_ln_tile(p, blk, f"c{it}", xt, [g1[:, c, tsl] for c in range(DC)],
                     sq, srow, P2, P3, P4, P5, ones, tmp=tmp, eps_ap=cst[0:1, 0:1],
                     xn_f32=[xf[:, c, :] for c in range(DC)])

        @blk.sync
        def _(sy, it=it, tsl=tsl):
            p.wait(sy, "ln_xn")
            p.inc(sy.dma_start(out=xn_dram.rearrange("(c p) t -> p c t", p=128)[:, :, tsl],
                               in_=xf[:, :, :]), "xfw", 16)

    ident = p.sb("ident", [128, 128])

    @blk.sync
    def _(sy):
        p.inc(sy.dma_start(out=ident[:, :], in_=id_d[:, :]), "wlds", 16)

    fcw = [TT] * 5 + [256]
    fco = [0, 512, 1024, 1536, 2048, 2560]
    wst3 = p.sb("wst3", [128, 2, 22, 128], BF16)
    hfm = p.sb("hfm", [128, 22, 128], BF16)

    for tc in range(NTC):
        csl = slice(tc * 128, (tc + 1) * 128)

        @blk.sync
        def _(sy, tc=tc, csl=csl):
            if tc == 0:
                p.wait(sy, "xfw")
            else:
                p.wait(sy, "xqt", tc * DC)
            p.inc(sy.dma_start(out=xt[:, :, 0:128],
                               in_=xn_dram.rearrange("(c p) t -> p c t", p=128)[:, :, csl]),
                  "xcl", 16)

        @blk.tensor
        def _(t, tc=tc):
            t.wait_ge(p.sem("xcl"), 16 * (tc + 1))
            if tc == 0:
                p.wait(t, "wlds")
                p.wait(t, "wld")
            for c in range(DC):
                idx = tc * DC + c
                if idx >= 1:
                    p.wait(t, "xte", idx)
                p.inc(t.transpose(PTf[:, :], xt[:, c, 0:128], ident[:, :]), "xqt", 1)

        @blk.vector
        def _(v, tc=tc):
            for c in range(DC):
                idx = tc * DC + c
                v.wait_ge(p.sem("xqt"), idx + 1)
                cp = v.tensor_copy(xtm[:, c * 128:(c + 1) * 128], PTf[:, :])
                p.inc(cp, "xte", 1)
            p.inc(v.engine_nop(), "qqin", 1)

        emit_fp4_qdq(p, blk, xtm[:, :], qa[:, 0:1024], qsc, qrc, qc1[:, 0:1024],
                     qc2[:, 0:1024], hq[:, 0:1024], 128, 1024)

        @blk.tensor
        def _(t, tc=tc):
            p.wait(t, "qqou")
            for c in range(DC):
                idx = tc * DC + c
                if idx >= 1:
                    p.wait(t, "xqe", idx)
                p.inc(t.transpose(PTb[:, :], hq[:, c * 128:(c + 1) * 128], identb[:, :]),
                      "xqm", 1)

        @blk.vector
        def _(v, tc=tc):
            for c in range(DC):
                idx = tc * DC + c
                v.wait_ge(p.sem("xqm"), idx + 1)
                p.inc(v.tensor_copy(xnq[:, c, :], PTb[:, :]), "xqe", 1)

        @blk.gpsimd
        def _(g, tc=tc):
            for fc in range(6):
                idx = tc * 6 + fc
                if idx >= 1:
                    g.wait_ge(p.sem("gumm"), 2 * idx)
                p.inc(g.dma_start(out=wst[:, :, 0:fcw[fc]].rearrange("p c f -> p c f"),
                                  in_=wgq_d.rearrange("(c p) f -> p c f", p=128)
                                  [:, :, fco[fc]:fco[fc] + fcw[fc]]), "gul", 16)
                p.inc(g.dma_start(out=wstU[:, :, 0:fcw[fc]].rearrange("p c f -> p c f"),
                                  in_=wuq_d.rearrange("(c p) f -> p c f", p=128)
                                  [:, :, fco[fc]:fco[fc] + fcw[fc]]), "gul", 16)

        @blk.tensor
        def _(t, tc=tc):
            p.wait(t, "xqe")
            for fc in range(6):
                idx = tc * 6 + fc
                t.wait_ge(p.sem("gul"), 32 * (idx + 1))
                if idx >= 2:
                    t.wait_ge(p.sem("guev"), 2 * (idx - 1))
                for c in range(DC):
                    mmG = t.matmul(P0[:, 0:fcw[fc]], xnq[:, c, :],
                                   wst[:, c, 0:fcw[fc]],
                                   start=(c == 0), stop=(c == DC - 1))
                for c in range(DC):
                    mmU = t.matmul(P1[:, 0:fcw[fc]], xnq[:, c, :],
                                   wstU[:, c, 0:fcw[fc]],
                                   start=(c == 0), stop=(c == DC - 1))
                p.inc(mmU, "gumm", 2)

        @blk.vector
        def _(v, tc=tc):
            for fc in range(6):
                idx = tc * 6 + fc
                v.wait_ge(p.sem("gumm"), 2 * (idx + 1))
                v.tensor_copy(gtm[:, fco[fc]:fco[fc] + fcw[fc]], P0[:, 0:fcw[fc]])
                p.inc(v.tensor_copy(utm[:, fco[fc]:fco[fc] + fcw[fc]], P1[:, 0:fcw[fc]]),
                      "guev", 2)

        @blk.scalar
        def _(s, tc=tc):
            p.wait(s, "guev")
            p.inc(s.activation(out=qa[:, :], in_=gtm[:, :], func=AF.Silu), "gsl", 1)

        @blk.vector
        def _(v, tc=tc):
            p.wait(v, "gsl")
            p.inc(v.tensor_mul(gtm[:, :], qa[:, :], utm[:, :]), "qqin", 1)

        emit_fp4_qdq(p, blk, gtm[:, :], qa, qsc, qrc, qc1, qc2, hq, 128, 2816)

        @blk.tensor
        def _(t, tc=tc):
            p.wait(t, "qqou")
            for k in range(22):
                idx = tc * 22 + k
                if idx >= 1:
                    p.wait(t, "hfe", idx)
                p.inc(t.transpose(PTb[:, :], hq[:, k * 128:(k + 1) * 128], identb[:, :]),
                      "hfm", 1)

        @blk.vector
        def _(v, tc=tc):
            for k in range(22):
                idx = tc * 22 + k
                v.wait_ge(p.sem("hfm"), idx + 1)
                p.inc(v.tensor_copy(hfm[:, k, :], PTb[:, :]), "hfe", 1)

        @blk.gpsimd
        def _(g, tc=tc):
            for oc in range(DC):
                idx = tc * DC + oc
                if idx >= 2:
                    g.wait_ge(p.sem("dwmm"), idx - 1)
                p.inc(g.dma_start(out=wst3[:, idx % 2, :, :],
                                  in_=wdq_d.rearrange("(k p) m -> p k m", p=128)
                                  [:, :, oc * 128:(oc + 1) * 128]), "dwl", 16)

        @blk.tensor
        def _(t, tc=tc):
            p.wait(t, "hfe")
            for oc in range(DC):
                idx = tc * DC + oc
                t.wait_ge(p.sem("dwl"), 16 * (idx + 1))
                if idx >= 1:
                    t.wait_ge(p.sem("g1e"), idx)
                for k in range(22):
                    mm = t.matmul(P4[:, 0:128], wst3[:, idx % 2, k, :], hfm[:, k, :],
                                  start=(k == 0), stop=(k == 21))
                p.inc(mm, "dwmm", 1)

        @blk.scalar
        def _(s, tc=tc, csl=csl):
            for oc in range(DC):
                idx = tc * DC + oc
                s.wait_ge(p.sem("dwmm"), idx + 1)
                p.inc(s.activation(out=g1[:, oc, csl], in_=P4[:, 0:128], func=AF.Silu),
                      "g1e", 1)

    NCK = T // 128

    @blk.vector
    def _(v):
        p.wait(v, "g1e")
        p.inc(v.memset(vh[:, :, DH:DH + 1], 1.0), "vhi", 1)

    for tqb in range(NTT):
        tsl = slice(tqb * TT, (tqb + 1) * TT)

        for h in range(NH):
            hidx = tqb * NH + h

            @blk.gpsimd
            def _(g, h=h, tqb=tqb, hidx=hidx):
                if hidx == 0:
                    p.wait(g, "dwmm")
                else:
                    g.wait_ge(p.sem("avm"), hidx * NCK)
                p.inc(g.dma_start(out=kh[:, :], in_=kf_d[h]), "khl", 16)
                p.inc(g.dma_start(out=vh[:, :, 0:DH],
                                  in_=vf_d[h].rearrange("(c p) d -> p c d", p=128)),
                      "khl", 16)
                p.inc(g.dma_start(out=qh[:, 0:TT],
                                  in_=q1_d[h, :, tqb * TT:(tqb + 1) * TT]), "khl", 16)

            @blk.tensor
            def _(t, h=h, tqb=tqb, hidx=hidx):
                t.wait_ge(p.sem("khl"), 48 * (hidx + 1))
                if hidx >= 1:
                    t.wait_ge(p.sem("ocp"), hidx)
                if hidx == 0:
                    p.wait(t, "nm7")
                    p.wait(t, "guev")
                    p.wait(t, "g1e")
                    p.wait(t, "vhi")
                if h == 0 and tqb > 0:
                    p.wait(t, "s1e")
                    p.wait(t, "ln_xn")
                    p.wait(t, "ll_ev")
                base = hidx * NCK

                def smm(ck):
                    psB = P0 if ck % 2 == 0 else P1
                    if base + ck >= 2:
                        t.wait_ge(p.sem("pex"), base + ck - 1)
                    p.inc(t.matmul(psB[:, :], kh[:, ck * 128:(ck + 1) * 128],
                                   qh[:, 0:TT], start=True, stop=True), "smm", 1)

                smm(0)
                smm(1)
                for ck in range(NCK):
                    t.wait_ge(p.sem("pex"), base + ck + 1)
                    p.inc(t.matmul(P3[0:65, :], vh[:, ck, :], pbs[:, ck % 2, :],
                                   start=(ck == 0), stop=(ck == NCK - 1)), "avm", 1)
                    if ck + 2 < NCK:
                        smm(ck + 2)

            @blk.scalar
            def _(s, h=h, hidx=hidx):
                base = hidx * NCK
                for ck in range(NCK):
                    s.wait_ge(p.sem("smm"), base + ck + 1)
                    if base + ck >= 2:
                        s.wait_ge(p.sem("avm"), base + ck - 1)
                    psB = P0 if ck % 2 == 0 else P1
                    p.inc(s.activation(out=pbs[:, ck % 2, :], in_=psB[:, :], func=AF.Exp,
                                       bias=mb[:, h:h + 1], scale=0.125), "pex", 1)

            @blk.vector
            def _(v, h=h, hidx=hidx):
                v.wait_ge(p.sem("avm"), (hidx + 1) * NCK)
                p.wait(v, "s1mm")
                p.vsf(v, v.tensor_scalar_add(onr[0:1, :], P3[64:65, :], cst[0:1, 1:2]))
                p.vsf(v, v.reciprocal(onr[0:1, :], onr[0:1, :]))
                p.inc(v.tensor_copy(oev[:, :], P3[0:64, :]), "ocp", 1)

            @blk.tensor
            def _(t, hidx=hidx):
                t.wait_ge(p.sem("ocp"), hidx + 1)
                if hidx >= 1:
                    t.wait_ge(p.sem("oev"), hidx)
                p.inc(t.matmul(P4[0:64, :], ones[0:1, 0:1].to_broadcast([1, 64]),
                               onr[0:1, :], start=True, stop=True), "obc", 1)

            @blk.vector
            def _(v, h=h, hidx=hidx):
                v.wait_ge(p.sem("obc"), hidx + 1)
                p.inc(v.tensor_mul(ofm[(h % 2) * 64:(h % 2) * 64 + 64, h // 2, :],
                                   oev[:, :], P4[0:64, :]), "oev", 1)

        @blk.sync
        def _(sy, tqb=tqb, tsl=tsl):
            if tqb == 0:
                p.wait(sy, "xqt")
            else:
                p.wait(sy, "x2w", 16 * tqb)
            p.inc(sy.dma_start(out=xt[:, :, :],
                               in_=x1_d.rearrange("(c p) t -> p c t", p=128)[:, :, tsl]),
                  "x2l", 16)

        @blk.gpsimd
        def _(g, tqb=tqb):
            for oc in range(DC):
                idx = tqb * DC + oc
                if idx >= 2:
                    g.wait_ge(p.sem("s1mm"), idx - 1)
                p.inc(g.dma_start(out=wst2[:, idx % 2, :, :],
                                  in_=wo_d.rearrange("(c p) m -> p c m", p=128)
                                  [:, :, oc * 128:(oc + 1) * 128]), "wol", 16)

        for oc in range(DC):
            idx = tqb * DC + oc

            @blk.tensor
            def _(t, tqb=tqb, oc=oc, idx=idx):
                t.wait_ge(p.sem("oev"), (tqb + 1) * NH)
                t.wait_ge(p.sem("wol"), 16 * (idx + 1))
                if idx >= 2:
                    t.wait_ge(p.sem("s1e"), idx - 1)
                ps = P0 if idx % 2 == 0 else P1
                for c in range(DC):
                    mm = t.matmul(ps[:, :], wst2[:, idx % 2, c, :], ofm[:, c, :],
                                  start=(c == 0), stop=(c == DC - 1))
                p.inc(mm, "s1mm", 1)

            @blk.vector
            def _(v, tqb=tqb, oc=oc, idx=idx, tsl=tsl):
                v.wait_ge(p.sem("s1mm"), idx + 1)
                v.wait_ge(p.sem("x2l"), 16 * (tqb + 1))
                ps = P0 if idx % 2 == 0 else P1
                p.inc(v.tensor_scalar(s1t[:, :], ps[:, :], gb[:, 3, oc:oc + 1],
                                      None, ALU.add), "s1e", 1)
                v.tensor_mul(s1t[:, :], s1t[:, :], g1[:, oc, tsl])
                p.inc(v.tensor_add(xf[:, oc, :], xt[:, oc, :], s1t[:, :]), "x2c", 1)

        @blk.sync
        def _(sy, tqb=tqb, tsl=tsl):
            p.wait(sy, "x2c")
            p.inc(sy.dma_start(out=x2_o.rearrange("(c p) t -> p c t", p=128)[:, :, tsl],
                               in_=xf[:, :, :]), "x2w", 16)

        emit_ln_tile(p, blk, f"d{tqb}", xf, [xn2t[:, c, :] for c in range(DC)],
                     sq, srow, P2, P3, P4, P5, ones, tmp=tmp, eps_ap=cst[0:1, 0:1],
                     gain_ap=[gb[:, 0, c:c + 1] for c in range(DC)],
                     bias_ap=[gb[:, 1, c:c + 1] for c in range(DC)],
                     extra_wait=("x2c", "llmm"))

        @blk.gpsimd
        def _(g, tqb=tqb):
            for oc in range(DC):
                idx = tqb * DC + oc
                if idx >= 2:
                    g.wait_ge(p.sem("llmm"), idx - 1)
                p.inc(g.dma_start(out=wst2[:, idx % 2, :, :],
                                  in_=win_d.rearrange("(c p) m -> p c m", p=128)
                                  [:, :, oc * 128:(oc + 1) * 128]), "wil", 16)

        for oc in range(DC):
            idx = tqb * DC + oc

            @blk.tensor
            def _(t, tqb=tqb, oc=oc, idx=idx, tsl=tsl):
                p.wait(t, "ln_xn")
                t.wait_ge(p.sem("wil"), 16 * (idx + 1))
                if idx >= 2:
                    t.wait_ge(p.sem("ll_ev"), idx - 1)
                ps = P0 if idx % 2 == 0 else P1
                for c in range(DC):
                    mm = t.matmul(ps[:, :], wst2[:, idx % 2, c, :], xn2t[:, c, :],
                                  start=(c == 0), stop=(c == DC - 1))
                p.inc(mm, "llmm", 1)

            @blk.vector
            def _(v, oc=oc, idx=idx):
                v.wait_ge(p.sem("llmm"), idx + 1)
                if idx >= DC:
                    p.wait(v, "llw", 16 * (idx // DC))
                ps = P0 if idx % 2 == 0 else P1
                p.inc(v.tensor_scalar(xt[:, oc, :], ps[:, :], gb[:, 2, oc:oc + 1],
                                      None, ALU.add), "ll_ev", 1)

        @blk.sync
        def _(sy, tqb=tqb, tsl=tsl):
            p.wait(sy, "ll_ev")
            p.inc(sy.dma_start(out=ll_o.rearrange("(c p) t -> p c t", p=128)[:, :, tsl],
                               in_=xt[:, :, :]), "llw", 16)

    @blk.sync
    def _(sy):
        p.wait(sy, "x2w")
        p.wait(sy, "llw")
    return p



def build_launch_D():
    p = Prog()
    ll_d = p.din("ll", [256, T])
    cw_d = p.din("conv_w", [128, 2, 4])
    cb_d = p.din("conv_b", [128, 2])
    lc_o = p.dout("lc", [256, T])

    lin = p.sb("lin", [128, 3 + T])
    lout = p.sb("lout", [128, T])
    cw = p.sb("cw", [128, 2, 4])
    cb = p.sb("cb", [128, 2])

    blk = p.block()

    @blk.gpsimd
    def _(g):
        p.inc(g.dma_start(out=cw[:, :, :], in_=cw_d[:, :, :]), "wld", 16)
        p.inc(g.dma_start(out=cb[:, :], in_=cb_d[:, :]), "wld", 16)

    for ch in range(2):
        @blk.vector
        def _(v, ch=ch):
            if ch == 0:
                p.wait(v, "wld")
            p.inc(v.memset(lin[:, 0:3], 0.0), "mz", 1)

        @blk.sync
        def _(sy, ch=ch):
            p.wait(sy, "mz")
            if ch == 1:
                p.wait(sy, "cvw", 16)
            p.inc(sy.dma_start(out=lin[:, 3:], in_=ll_d[ch * 128:(ch + 1) * 128, :]),
                  "lld", 16)

        @blk.vector
        def _(v, ch=ch):
            v.wait_ge(p.sem("lld"), 16 * (ch + 1))
            v.tensor_scalar_mul(lout[:, :], lin[:, 0:T], cw[:, ch, 0:1])
            for k in range(1, 4):
                v.scalar_tensor_tensor(lout[:, :], lin[:, k:k + T], cw[:, ch, k:k + 1],
                                       lout[:, :], ALU.mult, ALU.add)
            p.inc(v.tensor_scalar_add(lout[:, :], lout[:, :], cb[:, ch:ch + 1]),
                  "cv", 1)

        @blk.scalar
        def _(s, ch=ch):
            s.wait_ge(p.sem("cv"), ch + 1)
            p.inc(s.activation(out=lout[:, :], in_=lout[:, :], func=AF.Silu), "cs", 1)

        @blk.sync
        def _(sy, ch=ch):
            sy.wait_ge(p.sem("cs"), ch + 1)
            p.inc(sy.dma_start(out=lc_o[ch * 128:(ch + 1) * 128, :], in_=lout[:, :]),
                  "cvw", 16)

    @blk.sync
    def _(sy):
        p.wait(sy, "cvw")
    return p



def build_launch_E():
    p = Prog()
    lcf_d = p.din("lcf", [D, T])
    wdt_d = p.din("wdt", [D, 256])
    bdt_d = p.din("bdt", [128, 2])
    wbc_d = p.din("wbc", [D, 32])
    A_d = p.din("A", [128, 2, 16])
    Dssm_d = p.din("Dssm", [128, 2])
    row0 = 0
    own_d = p.din("own", [256, T])

    y_o = p.dout("yssm", [256, T])

    SEG = 2048
    NSEG = T // SEG

    lcb = p.sb("lcb", [128, 2, DC, TT], BF16)
    wdt = p.sb("wdt_sb", [128, DC, 256], BF16)
    wbc = p.sb("wbc_sb", [128, DC, 32], BF16)
    bdt = p.sb("bdt_sb", [128, 2])
    A_sb = p.sb("A_sb", [128, 2, 16])
    Ds = p.sb("Ds_sb", [128, 2])
    cst = p.sb("cst", [128, 2])
    bc_sb = p.sb("bc_sb", [32, T])
    dtt = p.sb("dtt", [128, 2, SEG])
    lco = p.sb("lco", [128, 2, SEG])
    dl = p.sb("dl", [128, 2, SEG])
    Bb = p.sb("Bb", [128, SEG])
    Cb = p.sb("Cb", [128, SEG])
    dA = p.sb("dA", [128, SEG])
    uu = p.sb("uu", [128, SEG])
    hh = p.sb("hh", [128, SEG])
    yac = p.sb("yac", [128, 2, SEG])
    tmp = p.sb("tmp", [128, SEG])
    hini = p.sb("hini", [128, 2, 16])
    brow = p.sb("brow", [1, 2, SEG])
    sp1 = p.sb("sp1", [128, SEG])
    ones = p.sb("ones", [128, 1])

    bc_dram = p.dtmp("bcd", [32, T])

    P0 = p.ps("P0", [128, TT])
    P1 = p.ps("P1", [128, TT])
    P2 = p.ps("P2", [128, TT])
    P3 = p.ps("P3", [128, TT])

    blk = p.block()

    @blk.vector
    def _(v):
        v.memset(ones[:, :], 1.0)
        v.memset(cst[:, 0:1], 1.0)
        v.memset(hini[:, :, :], 0.0)
        p.inc(v.engine_nop(), "init", 1)

    @blk.scalar
    def _(s):
        p.wait(s, "init")

    @blk.tensor
    def _(t):
        p.wait(t, "init")

    @blk.gpsimd
    def _(g):
        p.inc(g.dma_start(out=wdt[:, :, :],
                          in_=wdt_d.rearrange("(c p) m -> p c m", p=128)), "wld", 16)
        p.inc(g.dma_start(out=wbc[:, :, :],
                          in_=wbc_d.rearrange("(c p) m -> p c m", p=128)), "wld", 16)
        p.inc(g.dma_start(out=bdt[:, :], in_=bdt_d[:, :]), "wld", 16)
        p.inc(g.dma_start(out=A_sb[:, :, :], in_=A_d[:, :, :]), "wld", 16)
        p.inc(g.dma_start(out=Ds[:, :], in_=Dssm_d[:, :]), "wld", 16)

    for it in range(T // TT):
        @blk.gpsimd
        def _(g, it=it):
            if it >= 2:
                g.wait_ge(p.sem("bcm"), it - 1)
            p.inc(g.dma_start(out=lcb[:, it % 2, :, :],
                              in_=lcf_d.rearrange("(c p) t -> p c t", p=128)
                              [:, :, it * TT:(it + 1) * TT]), "lcl", 16)

        @blk.tensor
        def _(t, it=it):
            t.wait_ge(p.sem("lcl"), 16 * (it + 1))
            p.wait(t, "wld")
            if it >= 2:
                t.wait_ge(p.sem("bce"), it - 1)
            ps = P2 if it % 2 == 0 else P3
            for c in range(DC):
                mm = t.matmul(ps[0:32, :], wbc[:, c, :], lcb[:, it % 2, c, :],
                              start=(c == 0), stop=(c == DC - 1))
            p.inc(mm, "bcm", 1)

        @blk.vector
        def _(v, it=it):
            v.wait_ge(p.sem("bcm"), it + 1)
            ps = P2 if it % 2 == 0 else P3
            p.inc(v.tensor_copy(bc_sb[:, it * TT:(it + 1) * TT], ps[0:32, :]), "bce", 1)

    @blk.sync
    def _(sy):
        p.wait(sy, "bce")
        p.inc(sy.dma_start(out=bc_dram[:, :], in_=bc_sb[0:32, :]), "bcw", 16)

    for sg in range(NSEG):
        ssl = slice(sg * SEG, (sg + 1) * SEG)

        @blk.sync
        def _(sy, sg=sg, ssl=ssl):
            if sg > 0:
                p.wait(sy, "yw", 16 * sg)
            p.inc(sy.dma_start(out=lco[:, :, :],
                               in_=own_d.rearrange("(c p) t -> p c t", p=128)[:, :, ssl]),
                  "lcol", 16)

        for st in range(SEG // TT):
            it = sg * (SEG // TT) + st

            @blk.gpsimd
            def _(g, sg=sg, st=st, it=it, ssl=ssl):
                g.wait_ge(p.sem("bcm"), 16)
                if it >= 2:
                    g.wait_ge(p.sem("dtm"), 2 * (it - 1))
                p.inc(g.dma_start(out=lcb[:, it % 2, :, :],
                                  in_=lcf_d.rearrange("(c p) t -> p c t", p=128)
                                  [:, :, sg * SEG + st * TT:sg * SEG + (st + 1) * TT]),
                      "lcl2", 16)

            @blk.tensor
            def _(t, it=it, st=st):
                t.wait_ge(p.sem("lcl2"), 16 * (it + 1))
                if it >= 1:
                    t.wait_ge(p.sem("dte"), 2 * it)
                for oc in range(2):
                    ps = P0 if oc == 0 else P1
                    for c in range(DC):
                        mm = t.matmul(ps[:, :], wdt[:, c, oc * 128:(oc + 1) * 128],
                                      lcb[:, it % 2, c, :],
                                      start=(c == 0), stop=(c == DC - 1))
                    p.inc(mm, "dtm", 1)

            @blk.scalar
            def _(s, it=it, st=st):
                for oc in range(2):
                    s.wait_ge(p.sem("dtm"), 2 * it + oc + 1)
                    if 2 * it + oc >= 1:
                        s.wait_ge(p.sem("dte"), 2 * it + oc)
                    s.activation(out=tmp[:, 0:TT], in_=(P0 if oc == 0 else P1)[:, :],
                                 func=AF.Abs, bias=bdt[:, oc:oc + 1])
                    s.activation(out=sp1[:, 0:TT], in_=tmp[:, 0:TT], func=AF.Exp,
                                 scale=-1.0)
                    p.inc(s.activation(out=sp1[:, 0:TT], in_=sp1[:, 0:TT], func=AF.Ln,
                                       bias=cst[:, 0:1]), "spl", 1)

            @blk.vector
            def _(v, it=it, st=st):
                for oc in range(2):
                    v.wait_ge(p.sem("spl"), 2 * it + oc + 1)
                    ps = P0 if oc == 0 else P1
                    v.tensor_scalar(tmp[:, 0:TT], ps[:, :], bdt[:, oc:oc + 1], 0.0,
                                    ALU.add, ALU.max)
                    v.tensor_add(tmp[:, 0:TT], tmp[:, 0:TT], sp1[:, 0:TT])
                    p.inc(v.tensor_copy(dtt[:, oc, st * TT:(st + 1) * TT], tmp[:, 0:TT]),
                          "dte", 1)

        @blk.vector
        def _(v, sg=sg):
            p.wait(v, "dte")
            v.wait_ge(p.sem("lcol"), 16 * (sg + 1))
            v.tensor_mul(dl[:, 0, :], dtt[:, 0, :], lco[:, 0, :])
            p.inc(v.tensor_mul(dl[:, 1, :], dtt[:, 1, :], lco[:, 1, :]), "dlv", 1)

        for n in range(16):
            nidx = sg * 16 + n

            @blk.sync
            def _(sy, sg=sg, n=n, ssl=ssl, nidx=nidx):
                if nidx >= 1:
                    p.wait(sy, "bbm", 4 * nidx)
                sy.wait_ge(p.sem("bcw"), 16)
                p.inc(sy.dma_start(out=brow[0:1, 0, :], in_=bc_dram[n:n + 1, ssl]),
                      "brl", 16)
                p.inc(sy.dma_start(out=brow[0:1, 1, :], in_=bc_dram[16 + n:17 + n, ssl]),
                      "brl", 16)

            for q4 in range(4):
                qidx = nidx * 4 + q4

                @blk.tensor
                def _(t, q4=q4, nidx=nidx, qidx=qidx):
                    t.wait_ge(p.sem("brl"), 32 * (nidx + 1))
                    if qidx >= 1:
                        t.wait_ge(p.sem("bbe"), qidx)
                    t.matmul(P2[:, :], ones[0:1, 0:1].to_broadcast([1, 128]),
                             brow[0:1, 0, q4 * TT:(q4 + 1) * TT], start=True, stop=True)
                    p.inc(t.matmul(P3[:, :], ones[0:1, 0:1].to_broadcast([1, 128]),
                                   brow[0:1, 1, q4 * TT:(q4 + 1) * TT],
                                   start=True, stop=True), "bbm", 1)

                @blk.vector
                def _(v, q4=q4, qidx=qidx):
                    v.wait_ge(p.sem("bbm"), qidx + 1)
                    if qidx >= 1:
                        p.wait(v, "yacc", 0) if False else None
                    v.tensor_copy(Bb[:, q4 * TT:(q4 + 1) * TT], P2[:, :])
                    p.inc(v.tensor_copy(Cb[:, q4 * TT:(q4 + 1) * TT], P3[:, :]),
                          "bbe", 1)

            for oc in range(2):
                oidx = (sg * 16 + n) * 2 + oc

                @blk.vector
                def _(v, sg=sg, n=n, oc=oc, oidx=oidx):
                    p.wait(v, "bbe")
                    p.wait(v, "dlv")
                    p.inc(v.tensor_scalar_mul(dA[:, :], dtt[:, oc, :],
                                              A_sb[:, oc, n:n + 1]), "dam", 1)

                @blk.scalar
                def _(s, oidx=oidx):
                    s.wait_ge(p.sem("dam"), oidx + 1)
                    p.inc(s.activation(out=dA[:, :], in_=dA[:, :], func=AF.Exp),
                          "dax", 1)

                @blk.vector
                def _(v, sg=sg, n=n, oc=oc, oidx=oidx):
                    v.wait_ge(p.sem("dax"), oidx + 1)
                    v.tensor_mul(uu[:, :], dl[:, oc, :], Bb[:, :])
                    v.tensor_tensor_scan(hh[:, :], dA[:, :], uu[:, :],
                                         hini[:, oc, n:n + 1], ALU.mult, ALU.add)
                    p.vsf(v, v.tensor_copy(hini[:, oc, n:n + 1], hh[:, SEG - 1:SEG]))
                    v.tensor_mul(tmp[:, :], hh[:, :], Cb[:, :])
                    if n == 0:
                        ya = v.tensor_copy(yac[:, oc, :], tmp[:, :])
                    else:
                        ya = v.tensor_add(yac[:, oc, :], yac[:, oc, :], tmp[:, :])
                    p.inc(ya, "yacc", 1)

        @blk.vector
        def _(v, sg=sg):
            p.wait(v, "yacc")
            for oc in range(2):
                p.inc(v.scalar_tensor_tensor(yac[:, oc, :], lco[:, oc, :],
                                             Ds[:, oc:oc + 1], yac[:, oc, :],
                                             ALU.mult, ALU.add), "yfin", 1)

        @blk.sync
        def _(sy, sg=sg, ssl=ssl):
            p.wait(sy, "yfin")
            p.inc(sy.dma_start(out=y_o.rearrange("(c p) t -> p c t", p=128)[:, :, ssl],
                               in_=yac[:, :, :]), "yw", 16)

    @blk.sync
    def _(sy):
        p.wait(sy, "yw")
    return p



def build_launch_F(debug_yg=False, skip_l3=False, skip_moe=False):
    p = Prog()
    x2_d = p.din("x2", [D, TPC])
    ys_d = p.din("yssm", [D, TPC])
    gw_d = p.din("gate_wT", [D, 8])
    ew_g = p.din("e_gate", [8, D, 1408])
    ew_u = p.din("e_up", [8, D, 1408])
    ew_d = p.din("e_down", [8, 1408, D])
    swg_d = p.din("sw_gate", [D, 2816])
    swu_d = p.din("sw_up", [D, 2816])
    swd_d = p.din("sw_down", [2816, D])
    wout_d = p.din("w_out", [D, D])
    bout_d = p.din("b_out", [128, DC])
    lng_d = p.din("ln_g", [128, DC])
    lnb_d = p.din("ln_b", [128, DC])
    wq3_d = p.din("wq3T", [D, D])
    wk3_d = p.din("wk3T", [D, D])
    wv3_d = p.din("wv3T", [D, D])
    wqi_d = p.din("wqiT", [D, D])
    wki_d = p.din("wkiT", [D, D])
    wvi_d = p.din("wviT", [D, D])
    bqi_d = p.din("bqi", [128, DC])
    bki_d = p.din("bki", [128, DC])
    bvi_d = p.din("bvi", [1, D])
    cos3_d = p.din("cos3", [D, TPC])
    sin3_d = p.din("sin3", [D, TPC])
    id_d = p.din("ident_in", [128, 128])

    x3_o = p.dout("x3", [D, TPC])
    xn3_o = p.dout("xn3", [D, TPC])
    q3_o = p.dout("q3h", [D, TPC], BF16)
    k3_o = p.dout("k3h", [D, TPC], BF16)
    v3_o = p.dout("v3h", [TPC, D], BF16)
    qn3_o = p.dout("qn3", [NH, TPC])
    kn3_o = p.dout("kn3", [NH, TPC])

    comb_dram = p.dtmp("combdram", [8, TPC])
    xn2_dram = p.dtmp("xn2d", [D, TPC])
    yg_dbg = p.dout("ygdbg", [D, TPC], BF16) if debug_yg else None
    s2_dram = p.dtmp("s2d", [D, TPC])
    dbgb_d = p.dtmp("dbgb", [256, TPC], BF16)
    g2_dram = p.dtmp("g2d", [D, TPC])

    xt = p.sb("xt", [128, DC, TT])
    xf = p.sb("xf", [128, DC, TT])
    yst = p.sb("yst", [128, DC, TT])
    xn = p.sb("xn", [128, DC, TPC], BF16)
    yg = p.sb("yg", [128, DC, TPC], BF16)
    hsb = p.sb("hsb", [128, 22, TT], BF16)
    stg = yst[:, 3:5, :]
    sq = p.sb("sq", [128, 1, TT])
    srow = p.sb("srow", [1, 2, TT])
    ones = p.sb("ones", [128, 2])
    cst = p.sb("cst", [128, 2])
    tmp = p.sb("tmpa", [128, TT])
    tmp2 = p.sb("tmpb", [128, 8])
    ident = p.sb("ident", [128, 128])
    wbuf = p.sb("wbuf", [128, 2, DC, 128], BF16)
    wbuf2 = p.sb("wbuf2", [128, 2, DC, 128], BF16)
    ewsb = p.sb("ewsb", [128, DC, 1408], BF16)
    dnsb = p.sb("dnsb", [128, 11, 1024], BF16)
    wbuf3 = dnsb.rearrange("p k m -> p (k m)").rearrange(
        "p (a b) -> p a b", b=128)[:, 0:22, :]
    zf = yst[:, 1:3, :]
    logit = p.sb("logit", [128, NTC, 8])
    lsc = p.sb("lsc", [128, 8])
    comb_tm = p.sb("comb_tm", [128, NTC, 8])
    comb_st = p.sb("comb_st", [8, 128])
    comb_row = p.sb("comb_row", [1, 1, TT])
    rwsb = p.sb("rwsb", [128, DC, 8], BF16)
    gb = p.sb("gb", [128, 5, DC])
    bvrow = p.sb("bvrow", [1, D])
    bvb = p.sb("bvb", [128, DC, 128])
    cs3 = stg
    eons = p.sb("eons", [128, 2])
    n2r = yst[0:2, 0, 0:TT]
    rpb = xf.rearrange("p c t -> p (c t)").bitcast(BF16)[:, 0:TT]

    P0 = p.ps("P0", [128, TT])
    P1 = p.ps("P1", [128, TT])
    P2 = p.ps("P2", [128, TT])
    P3 = p.ps("P3", [128, TT])
    P4 = p.ps("P4", [128, TT])
    P5 = p.ps("P5", [128, TT])
    PTf = p.ps("PTf", [8, 128])

    blk = p.block()

    @blk.vector
    def _(v):
        v.memset(ones[:, :], 1.0)
        v.memset(cst[:, 0:1], 1e-5)
        v.memset(cst[:, 1:2], 1e-20)
        v.memset(eons[:, :], 0.0)
        v.memset(eons[0:64, 0:1], 1.0)
        p.inc(v.memset(eons[64:128, 1:2], 1.0), "init", 1)

    @blk.scalar
    def _(s):
        p.wait(s, "init")

    @blk.tensor
    def _(t):
        p.wait(t, "init")

    @blk.sync
    def _(sy):
        p.inc(sy.dma_start(out=ident[:, :], in_=id_d[:, :]), "wlds", 16)

    @blk.gpsimd
    def _(g):
        p.inc(g.dma_start(out=gb[:, 0, :], in_=lng_d[:, :]), "wld", 16)
        p.inc(g.dma_start(out=gb[:, 1, :], in_=lnb_d[:, :]), "wld", 16)
        p.inc(g.dma_start(out=gb[:, 2, :], in_=bout_d[:, :]), "wld", 16)
        p.inc(g.dma_start(out=gb[:, 3, :], in_=bqi_d[:, :]), "wld", 16)
        p.inc(g.dma_start(out=gb[:, 4, :], in_=bki_d[:, :]), "wld", 16)
        p.inc(g.dma_start(out=bvrow[0:1, :], in_=bvi_d[:, :]), "wld", 16)
        p.inc(g.dma_start(out=rwsb[:, :, :],
                          in_=gw_d.rearrange("(c p) e -> p c e", p=128)), "wld", 16)

    for it in range(NTT):
        tsl = slice(it * TT, (it + 1) * TT)

        @blk.sync
        def _(sy, it=it, tsl=tsl):
            if it >= 1:
                p.wait(sy, "ygd", it)
                p.wait(sy, "xfw", 16 * it)
            p.inc(sy.dma_start(out=xt[:, :, :],
                               in_=x2_d.rearrange("(c p) t -> p c t", p=128)[:, :, tsl]),
                  "ln_xld", 16)
            p.inc(sy.dma_start(out=yst[:, :, :],
                               in_=ys_d.rearrange("(c p) t -> p c t", p=128)[:, :, tsl]),
                  "ysl", 16)

        emit_ln_tile(p, blk, f"f{it}", xt, [xn[:, c, tsl] for c in range(DC)],
                     sq, srow, P2, P3, P4, P5, ones, tmp=tmp, eps_ap=cst[0:1, 0:1],
                     xn_f32=[xf[:, c, :] for c in range(DC)])

        @blk.sync
        def _(sy, it=it, tsl=tsl):
            p.wait(sy, "ln_xn")
            p.inc(sy.dma_start(out=xn2_dram.rearrange("(c p) t -> p c t", p=128)[:, :, tsl],
                               in_=xf[:, :, :]), "xfw", 16)

        @blk.vector
        def _(v, it=it):
            p.wait(v, "ln_xn")
            v.wait_ge(p.sem("xfw"), 16 * (it + 1))
            for c in range(DC):
                cp = v.tensor_scalar(xf[:, c, :], xf[:, c, :], gb[:, 0, c:c + 1],
                                     gb[:, 1, c:c + 1], ALU.mult, ALU.add)
            p.inc(cp, "xiv", 1)

        @blk.scalar
        def _(s, it=it):
            p.wait(s, "xiv")
            s.wait_ge(p.sem("xfw"), 16 * (it + 1))
            for c in range(DC):
                ac = s.activation(out=xf[:, c, :], in_=xf[:, c, :], func=AF.Silu)
            p.inc(ac, "gsl", 1)

        @blk.vector
        def _(v, it=it, tsl=tsl):
            p.wait(v, "gsl")
            v.wait_ge(p.sem("ysl"), 16 * (it + 1))
            for c in range(DC):
                v.tensor_mul(tmp[:, :], yst[:, c, :], xf[:, c, :])
                cp = v.tensor_copy(yg[:, c, tsl], tmp[:, :])
            p.inc(cp, "ygd", 1)

        if debug_yg:
            @blk.sync
            def _(sy, it=it, tsl=tsl):
                p.wait(sy, "ygd")
                p.inc(sy.dma_start(out=yg_dbg.rearrange("(c p) t -> p c t", p=128)[:, :, tsl],
                                   in_=yg[:, :, tsl]), "ygw", 16)

    for it in range(NTT):
        @blk.tensor
        def _(t, it=it):
            p.wait(t, "ln_xn")
            p.wait(t, "wld")
            for tc in range(4):
                col0 = it * TT + tc * 128
                idx = it * 4 + tc
                if idx > 0:
                    t.wait_ge(p.sem("rtev"), idx)
                for c in range(DC):
                    mm = t.matmul(P4[:, 0:8], xn[:, c, col0:col0 + 128],
                                  rwsb[:, c, :], start=(c == 0), stop=(c == DC - 1))
                p.inc(mm, "rtmm", 1)

        @blk.vector
        def _(v, it=it):
            for tc in range(4):
                v.wait_ge(p.sem("rtmm"), it * 4 + tc + 1)
                p.inc(v.tensor_copy(logit[:, it * 4 + tc, :], P4[:, 0:8]), "rtev", 1)

    for tc in range(NTC):
        @blk.vector
        def _(v, tc=tc):
            if tc == 0:
                p.wait(v, "rtev")
            else:
                v.wait_ge(p.sem("rex"), tc)
            p.inc(v.reduce_max(lsc[:, 0:1], logit[:, tc, :], axis=AX.X, negate=True),
                  "rmx", 1)

        @blk.scalar
        def _(s, tc=tc):
            s.wait_ge(p.sem("rmx"), tc + 1)
            p.inc(s.activation(out=comb_tm[:, tc, :], in_=logit[:, tc, :], func=AF.Exp,
                               bias=lsc[:, 0:1], accum_out=lsc[:, 1:2]), "rex", 1)

        @blk.vector
        def _(v, tc=tc):
            v.wait_ge(p.sem("rex"), tc + 1)
            S = comb_tm[:, tc, :]
            M = logit[:, tc, :]
            p.vsf(v, v.reciprocal(lsc[:, 2:3], lsc[:, 1:2]))
            p.vsf(v, v.tensor_scalar_mul(S, S, lsc[:, 2:3]))
            p.vsf(v, v.reduce_max(lsc[:, 3:4], S, axis=AX.X, negate=True))
            p.vsf(v, v.tensor_scalar(M, S, lsc[:, 3:4], 0.0, ALU.add, ALU.is_ge))
            p.vsf(v, v.scalar_tensor_tensor(tmp[:, 0:8], M, -1e30, S, ALU.mult, ALU.add))
            p.vsf(v, v.reduce_max(lsc[:, 4:5], tmp[:, 0:8], axis=AX.X, negate=True))
            p.vsf(v, v.tensor_scalar(tmp2[:, 0:8], tmp[:, 0:8], lsc[:, 4:5], 0.0,
                                     ALU.add, ALU.is_ge))
            p.vsf(v, v.tensor_add(lsc[:, 5:6], lsc[:, 3:4], lsc[:, 4:5]))
            p.vsf(v, v.tensor_scalar(lsc[:, 5:6], lsc[:, 5:6], -1.0, 1e-20,
                                     ALU.mult, ALU.add))
            p.vsf(v, v.reciprocal(lsc[:, 5:6], lsc[:, 5:6]))
            p.vsf(v, v.tensor_scalar_mul(lsc[:, 5:6], lsc[:, 5:6], 2.5))
            p.vsf(v, v.tensor_scalar_mul(lsc[:, 6:7], lsc[:, 3:4], -1.0))
            p.vsf(v, v.tensor_mul(lsc[:, 6:7], lsc[:, 6:7], lsc[:, 5:6]))
            p.vsf(v, v.tensor_scalar_mul(M, M, lsc[:, 6:7]))
            p.vsf(v, v.tensor_scalar_mul(lsc[:, 6:7], lsc[:, 4:5], -1.0))
            p.vsf(v, v.tensor_mul(lsc[:, 6:7], lsc[:, 6:7], lsc[:, 5:6]))
            p.vsf(v, v.tensor_scalar_mul(tmp2[:, 0:8], tmp2[:, 0:8], lsc[:, 6:7]))
            p.inc(v.tensor_add(S, M, tmp2[:, 0:8]), "cmb", 1)

    @blk.tensor
    def _(t):
        p.wait(t, "cmb")
        p.wait(t, "wlds")
        for tc in range(NTC):
            if tc > 0:
                t.wait_ge(p.sem("ctv"), tc)
            p.inc(t.transpose(PTf[0:8, 0:128], comb_tm[:, tc, :], ident[:, :]), "ctm", 1)

    @blk.vector
    def _(v):
        for tc in range(NTC):
            v.wait_ge(p.sem("ctm"), tc + 1)
            if tc >= 1:
                v.wait_ge(p.sem("cbd"), 16 * tc)
            p.inc(v.tensor_copy(comb_st[0:8, :], PTf[0:8, 0:128]), "ctv", 1)

    @blk.sync
    def _(sy):
        for tc in range(NTC):
            sy.wait_ge(p.sem("ctv"), tc + 1)
            p.inc(sy.dma_start(out=comb_dram[:, tc * 128:(tc + 1) * 128],
                               in_=comb_st[0:8, :]), "cbd", 16)

    NF2 = 11
    NREG = NTT * DC

    for e in range((0 if skip_moe else 8)):
        @blk.gpsimd
        def _(g, e=e):
            if e > 0:
                g.wait_ge(p.sem("dmm"), e * NREG)
            p.inc(g.dma_start(out=ewsb[:, :, :],
                              in_=ew_g[e].rearrange("(c p) f -> p c f", p=128)),
                  "mwl", 16)
            p.inc(g.dma_start(out=dnsb[:, :, :],
                              in_=ew_d[e].rearrange("(k p) m -> p k m", p=128)),
                  "mdl", 16)
            for it in range(NTT):
                for fc in range(NF2):
                    idx = (e * NTT + it) * NF2 + fc
                    if idx >= 2:
                        g.wait_ge(p.sem("hmm"), idx - 1)
                    p.inc(g.dma_start(out=wbuf2[:, idx % 2, :, :],
                                      in_=ew_u[e].rearrange("(c p) f -> p c f", p=128)
                                      [:, :, fc * 128:(fc + 1) * 128]), "mul", 16)
                for oc in range(DC):
                    idx = (e * NTT + it) * DC + oc
                    g.wait_ge(p.sem("dev"), idx + 1)
                    if idx >= NREG:
                        g.wait_ge(p.sem("gad"), 16 * (idx - NREG + 1))
                    p.inc(g.dma_start(
                        out=g2_dram.rearrange("(c p) t -> p c t", p=128)
                        [:, oc, it * TT:(it + 1) * TT],
                        in_=stg[:, idx % 2, :],
                        accum_op=(ALU.bypass if e == 0 else ALU.add)), "gad", 16)


        @blk.sync
        def _(sy, e=e):
            for it in range(NTT):
                gi = e * NTT + it
                if gi >= 1:
                    sy.wait_ge(p.sem("cbc"), gi)
                sy.wait_ge(p.sem("cbd"), 16 * NTC)
                p.inc(sy.dma_start(out=comb_row[0:1, 0, :],
                                   in_=comb_dram[e:e + 1, it * TT:(it + 1) * TT]),
                      "crl", 16)

        @blk.tensor
        def _(t, e=e):
            t.wait_ge(p.sem("mwl"), 16 * (e + 1))
            t.wait_ge(p.sem("mdl"), 16 * (e + 1))
            if e == 0:
                p.wait(t, "ygd")
            for it in range(NTT):
                gi = e * NTT + it
                if gi * NF2 > 0:
                    t.wait_ge(p.sem("hev"), gi * NF2)
                t.wait_ge(p.sem("crl"), 16 * (gi + 1))
                p.inc(t.matmul(P5[:, 0:TT], ones[0:1, 0:1].to_broadcast([1, 128]),
                               comb_row[0:1, 0, :], start=True, stop=True), "cbc", 1)
                for fc in range(NF2):
                    idx = gi * NF2 + fc
                    if idx >= 2:
                        t.wait_ge(p.sem("hev"), idx - 1)
                    t.wait_ge(p.sem("mul"), 16 * (idx + 1))
                    psG = P0 if fc % 2 == 0 else P2
                    psU = P1 if fc % 2 == 0 else P3
                    for c in range(DC):
                        t.matmul(psG[:, :], ewsb[:, c, fc * 128:(fc + 1) * 128],
                                 xn[:, c, it * TT:(it + 1) * TT],
                                 start=(c == 0), stop=(c == DC - 1))
                    for c in range(DC):
                        mm = t.matmul(psU[:, :], wbuf2[:, idx % 2, c, :],
                                      xn[:, c, it * TT:(it + 1) * TT],
                                      start=(c == 0), stop=(c == DC - 1))
                    p.inc(mm, "hmm", 1)
                t.wait_ge(p.sem("hev"), (gi + 1) * NF2)
                for oc in range(DC):
                    idx = gi * DC + oc
                    if idx >= 1:
                        t.wait_ge(p.sem("dev"), idx)
                    for k in range(NF2):
                        mm = t.matmul(P4[:, :], dnsb[:, k, oc * 128:(oc + 1) * 128],
                                      hsb[:, k, :], start=(k == 0), stop=(k == NF2 - 1))
                    p.inc(mm, "dmm", 1)

        @blk.scalar
        def _(s, e=e):
            for it in range(NTT):
                for fc in range(NF2):
                    idx = (e * NTT + it) * NF2 + fc
                    s.wait_ge(p.sem("hmm"), idx + 1)
                    if idx >= 1:
                        s.wait_ge(p.sem("hev"), idx)
                    psG = P0 if fc % 2 == 0 else P2
                    p.inc(s.activation(out=sq[:, 0, :], in_=psG[:, :], func=AF.Silu),
                          "slu", 1)

        @blk.vector
        def _(v, e=e):
            for it in range(NTT):
                v.wait_ge(p.sem("cbc"), e * NTT + it + 1)
                for fc in range(NF2):
                    idx = (e * NTT + it) * NF2 + fc
                    v.wait_ge(p.sem("slu"), idx + 1)
                    psU = P1 if fc % 2 == 0 else P3
                    v.tensor_mul(tmp[:, :], sq[:, 0, :], psU[:, :])
                    v.tensor_mul(tmp[:, :], tmp[:, :], P5[:, 0:TT])
                    p.inc(v.tensor_copy(hsb[:, fc, :], tmp[:, :]), "hev", 1)
                for oc in range(DC):
                    idx = (e * NTT + it) * DC + oc
                    if idx >= 2:
                        v.wait_ge(p.sem("gad"), 16 * (idx - 1))
                    v.wait_ge(p.sem("dmm"), idx + 1)
                    p.inc(v.tensor_copy(stg[:, idx % 2, :], P4[:, :]), "dev", 1)


    for it in range(0 if skip_moe else NTT):
        @blk.gpsimd
        def _(g, it=it):
            if it == 0:
                g.wait_ge(p.sem("dmm"), 8 * NREG)
            for fc in range(22):
                idx = it * 22 + fc
                if idx >= 2:
                    g.wait_ge(p.sem("shmm"), idx - 1)
                p.inc(g.dma_start(
                    out=wbuf[:, idx % 2, :, :],
                    in_=swg_d.rearrange("(c p) f -> p c f", p=128)[:, :, fc * 128:(fc + 1) * 128]),
                    "swl", 16)
                p.inc(g.dma_start(
                    out=wbuf2[:, idx % 2, :, :],
                    in_=swu_d.rearrange("(c p) f -> p c f", p=128)[:, :, fc * 128:(fc + 1) * 128]),
                    "swl", 16)

        @blk.tensor
        def _(t, it=it):
            for fc in range(22):
                idx = it * 22 + fc
                t.wait_ge(p.sem("swl"), 32 * (idx + 1))
                if idx >= 2:
                    t.wait_ge(p.sem("shev"), idx - 1)
                psG = P0 if fc % 2 == 0 else P2
                psU = P1 if fc % 2 == 0 else P3
                for c in range(DC):
                    t.matmul(psG[:, :], wbuf[:, idx % 2, c, :],
                             xn[:, c, it * TT:(it + 1) * TT],
                             start=(c == 0), stop=(c == DC - 1))
                for c in range(DC):
                    mm = t.matmul(psU[:, :], wbuf2[:, idx % 2, c, :],
                                  xn[:, c, it * TT:(it + 1) * TT],
                                  start=(c == 0), stop=(c == DC - 1))
                p.inc(mm, "shmm", 1)

        @blk.scalar
        def _(s, it=it):
            for fc in range(22):
                idx = it * 22 + fc
                s.wait_ge(p.sem("shmm"), idx + 1)
                if idx >= 1:
                    s.wait_ge(p.sem("shev"), idx)
                psG = P0 if fc % 2 == 0 else P2
                p.inc(s.activation(out=sq[:, 0, :], in_=psG[:, :], func=AF.Silu),
                      "shsl", 1)

        @blk.vector
        def _(v, it=it):
            for fc in range(22):
                idx = it * 22 + fc
                v.wait_ge(p.sem("shsl"), idx + 1)
                if it >= 1 and fc == 0:
                    v.wait_ge(p.sem("sdmm"), it * DC)
                psU = P1 if fc % 2 == 0 else P3
                v.tensor_mul(tmp[:, :], sq[:, 0, :], psU[:, :])
                p.inc(v.tensor_copy(hsb[:, fc, :], tmp[:, :]), "shev", 1)

        @blk.gpsimd
        def _(g, it=it):
            for oc in range(DC):
                idx2 = it * DC + oc
                gidx = 8 * NREG + idx2
                if idx2 > 0:
                    g.wait_ge(p.sem("sdmm"), idx2)
                p.inc(g.dma_start(
                    out=wbuf3[:, :, :],
                    in_=swd_d.rearrange("(k p) m -> p k m", p=128)[:, :, oc * 128:(oc + 1) * 128]),
                    "sdl", 16)
                g.wait_ge(p.sem("sdev"), idx2 + 1)
                g.wait_ge(p.sem("gad"), 16 * (gidx - NREG + 1))
                p.inc(g.dma_start(
                    out=g2_dram.rearrange("(c p) t -> p c t", p=128)
                    [:, oc, it * TT:(it + 1) * TT],
                    in_=stg[:, gidx % 2, :], accum_op=ALU.add), "gad", 16)

        @blk.tensor
        def _(t, it=it):
            for oc in range(DC):
                idx2 = it * DC + oc
                t.wait_ge(p.sem("sdl"), 16 * (idx2 + 1))
                t.wait_ge(p.sem("shev"), it * 22 + 22)
                if idx2 >= 1:
                    t.wait_ge(p.sem("sdev"), idx2)
                for k in range(22):
                    mm = t.matmul(P4[:, :], wbuf3[:, k, :], hsb[:, k, :],
                                  start=(k == 0), stop=(k == 21))
                p.inc(mm, "sdmm", 1)

        @blk.vector
        def _(v, it=it):
            for oc in range(DC):
                idx2 = it * DC + oc
                v.wait_ge(p.sem("sdmm"), idx2 + 1)
                gidx = 8 * NREG + idx2
                if gidx >= 2:
                    v.wait_ge(p.sem("gad"), 16 * (gidx - 1))
                p.inc(v.tensor_copy(stg[:, gidx % 2, :], P4[:, :]), "sdev", 1)

    for it in range(NTT):
        tsl = slice(it * TT, (it + 1) * TT)

        @blk.sync
        def _(sy, it=it, tsl=tsl):
            p.wait(sy, "gad")
            p.wait(sy, "sdmm")
            if it >= 1:
                p.wait(sy, "x3w", 16 * it)
                p.wait(sy, "ln_xn")
            if it >= 1:
                p.wait(sy, "xnw", 16 * it)
            p.inc(sy.dma_start(out=xt[:, :, :],
                               in_=xn2_dram.rearrange("(c p) t -> p c t", p=128)[:, :, tsl]),
                  "f3a", 16)
            p.inc(sy.dma_start(out=yst[:, :, :],
                               in_=x2_d.rearrange("(c p) t -> p c t", p=128)[:, :, tsl]),
                  "f3b", 16)
            p.inc(sy.dma_start(out=xf[:, :, :],
                               in_=g2_dram.rearrange("(c p) t -> p c t", p=128)[:, :, tsl]),
                  "f3c", 16)

        @blk.scalar
        def _(s, it=it):
            s.wait_ge(p.sem("f3c"), 16 * (it + 1))
            if it >= 1:
                p.wait(s, "x3c")
            for c in range(DC):
                ac = s.activation(out=xf[:, c, :], in_=xf[:, c, :], func=AF.Silu)
            p.inc(ac, "g2s", 1)

        @blk.gpsimd
        def _(g, it=it):
            for oc in range(DC):
                idx = it * DC + oc
                if idx >= 2:
                    g.wait_ge(p.sem("s2mm"), idx - 1)
                p.inc(g.dma_start(out=wbuf[:, idx % 2, :, :],
                                  in_=wout_d.rearrange("(c p) m -> p c m", p=128)
                                  [:, :, oc * 128:(oc + 1) * 128]), "wotl", 16)

        for oc in range(DC):
            idx = it * DC + oc

            @blk.tensor
            def _(t, it=it, oc=oc, idx=idx, tsl=tsl):
                t.wait_ge(p.sem("wotl"), 16 * (idx + 1))
                if idx == 0:
                    p.wait(t, "sdmm")
                if idx >= 2:
                    t.wait_ge(p.sem("s2ev"), idx - 1)
                ps = P0 if idx % 2 == 0 else P1
                for c in range(DC):
                    mm = t.matmul(ps[:, :], wbuf[:, idx % 2, c, :], yg[:, c, tsl],
                                  start=(c == 0), stop=(c == DC - 1))
                p.inc(mm, "s2mm", 1)

            @blk.vector
            def _(v, it=it, oc=oc, idx=idx):
                v.wait_ge(p.sem("s2mm"), idx + 1)
                v.wait_ge(p.sem("f3a"), 16 * (it + 1))
                v.wait_ge(p.sem("f3b"), 16 * (it + 1))
                p.wait(v, "g2s")
                ps = P0 if idx % 2 == 0 else P1
                v.tensor_scalar(tmp[:, :], ps[:, :], gb[:, 2, oc:oc + 1], None, ALU.add)
                p.inc(v.engine_nop(), "s2ev", 1)
                v.tensor_add(tmp[:, :], tmp[:, :], xt[:, oc, :])
                v.tensor_mul(tmp[:, :], tmp[:, :], xf[:, oc, :])
                p.inc(v.tensor_add(yst[:, oc, :], yst[:, oc, :], tmp[:, :]), "x3c", 1)

        @blk.sync
        def _(sy, it=it, tsl=tsl):
            p.wait(sy, "x3c")
            p.inc(sy.dma_start(out=x3_o.rearrange("(c p) t -> p c t", p=128)[:, :, tsl],
                               in_=yst[:, :, :]), "x3w", 16)

        emit_ln_tile(p, blk, f"g{it}", yst, [xn[:, c, tsl] for c in range(DC)],
                     sq, srow, P2, P3, P4, P5, ones, tmp=tmp, eps_ap=cst[0:1, 0:1],
                     xn_f32=[xf[:, c, :] for c in range(DC)],
                     extra_wait=("x3c",))

        @blk.sync
        def _(sy, it=it, tsl=tsl):
            p.wait(sy, "ln_xn")
            p.inc(sy.dma_start(out=xn3_o.rearrange("(c p) t -> p c t", p=128)[:, :, tsl],
                               in_=xf[:, :, :]), "xnw", 16)

    if skip_l3:
        @blk.sync
        def _(sy):
            p.wait(sy, "x3w")
        return p

    @blk.tensor
    def _(t):
        p.wait(t, "wld")
        p.wait(t, "ln_xn")
        for half in range(2):
            if half == 1:
                p.wait(t, "bvc", 1)
            p.inc(t.matmul(P5[:, :], ones[0:1, 0:1].to_broadcast([1, 128]),
                           bvrow[0:1, half * TT:(half + 1) * TT], start=True, stop=True),
                  "bvm", 1)

    @blk.vector
    def _(v):
        for half in range(2):
            v.wait_ge(p.sem("bvm"), half + 1)
            p.inc(v.tensor_copy(bvb.rearrange("p c f -> p (c f)")[:, half * TT:(half + 1) * TT],
                                P5[:, :]), "bvc", 1)

    wvi_sb = ewsb[:, :, 0:1024]

    for which, w3_d, wi_d, bcol, out_d, n2_d in (
            ("q", wq3_d, wqi_d, 3, q3_o, qn3_o),
            ("k", wk3_d, wki_d, 4, k3_o, kn3_o),
            ("v", wv3_d, wvi_d, -1, v3_o, None)):

        @blk.gpsimd
        def _(g, which=which, w3_d=w3_d):
            for it in range(NTT):
                for cp in range(4):
                    for half in range(2):
                        idx = (it * 4 + cp) * 2 + half
                        if idx == 0 and which == "k":
                            g.wait_ge(p.sem("qpm"), 32)
                        if idx == 0 and which == "v":
                            g.wait_ge(p.sem("kpm"), 32)
                        if idx >= 2:
                            g.wait_ge(p.sem(which + "pm"), idx - 1)
                        oc = cp if half == 0 else cp + 4
                        p.inc(g.dma_start(out=wbuf2[:, idx % 2, :, :],
                                          in_=w3_d.rearrange("(c p) m -> p c m", p=128)
                                          [:, :, oc * 128:(oc + 1) * 128]),
                              which + "pl", 16)

        for it in range(NTT):
            tsl = slice(it * TT, (it + 1) * TT)
            for cp in range(4):
                pidx = it * 4 + cp

                @blk.sync
                def _(sy, it=it, cp=cp, pidx=pidx, tsl=tsl, which=which):
                    if pidx >= 1:
                        p.wait(sy, which + "rp", pidx)
                    elif which == "q":
                        p.wait(sy, "ln_xn")
                    elif which == "k":
                        p.wait(sy, "qrp")
                    else:
                        p.wait(sy, "krp")
                    p.inc(sy.dma_start(out=cs3[:, 0, :],
                                       in_=cos3_d[cp * 128:(cp + 1) * 128, tsl]),
                          which + "cs", 16)
                    p.inc(sy.dma_start(out=cs3[:, 1, :],
                                       in_=sin3_d[cp * 128:(cp + 1) * 128, tsl]),
                          which + "cs", 16)

                for half in range(2):
                    idx = pidx * 2 + half

                    @blk.tensor
                    def _(t, it=it, cp=cp, half=half, idx=idx, tsl=tsl, which=which):
                        t.wait_ge(p.sem(which + "pl"), 16 * (idx + 1))
                        if which == "q" and idx == 0:
                            p.wait(t, "ln_xn")
                            p.wait(t, "s2mm")
                        if which == "k" and idx <= 1:
                            t.wait_ge(p.sem("qpe"), 32)
                        if which == "v" and idx <= 1:
                            t.wait_ge(p.sem("kpe"), 32)
                        if idx >= 2:
                            t.wait_ge(p.sem(which + "pe"), idx - 1)
                        ps = P0 if idx % 2 == 0 else P1
                        for c in range(DC):
                            mm = t.matmul(ps[:, :], wbuf2[:, idx % 2, c, :],
                                          xn[:, c, tsl], start=(c == 0), stop=(c == DC - 1))
                        p.inc(mm, which + "pm", 1)

                    @blk.vector
                    def _(v, half=half, idx=idx, which=which):
                        v.wait_ge(p.sem(which + "pm"), idx + 1)
                        if idx == 0 and which == "q":
                            v.wait_ge(p.sem("x3w"), 16 * NTT)
                        ps = P0 if idx % 2 == 0 else P1
                        p.inc(v.tensor_copy(zf[:, half, :], ps[:, :]), which + "pe", 1)

                @blk.vector
                def _(v, it=it, cp=cp, pidx=pidx, tsl=tsl, which=which):
                    v.wait_ge(p.sem(which + "pe"), 2 * (pidx + 1))
                    v.wait_ge(p.sem(which + "cs"), 32 * (pidx + 1))
                    v.tensor_mul(tmp[:, :], zf[:, 0, :], cs3[:, 0, :])
                    v.tensor_mul(sq[:, 0, :], zf[:, 1, :], cs3[:, 1, :])
                    v.tensor_add(tmp[:, :], tmp[:, :], sq[:, 0, :])
                    v.tensor_copy(yg[:, cp, tsl], tmp[:, :])
                    v.tensor_mul(tmp[:, :], zf[:, 1, :], cs3[:, 0, :])
                    v.tensor_mul(sq[:, 0, :], zf[:, 0, :], cs3[:, 1, :])
                    v.tensor_sub(tmp[:, :], tmp[:, :], sq[:, 0, :])
                    p.inc(v.tensor_copy(yg[:, cp + 4, tsl], tmp[:, :]), which + "rp", 1)

        if debug_yg and which == "q":
            @blk.gpsimd
            def _(g):
                p.wait(g, "qrp")
                p.inc(g.dma_start(out=yg_dbg.rearrange("(c p) t -> p c t", p=128),
                                  in_=yg[:, :, :]), "ydq", 16)

        if which != "v":
            @blk.gpsimd
            def _(g, which=which, wi_d=wi_d):
                for oc in range(DC):
                    for it in range(NTT):
                        idx = oc * NTT + it
                        if idx == 0:
                            if which == "q":
                                g.wait_ge(p.sem("s2mm"), 32)
                            else:
                                g.wait_ge(p.sem("qim"), 32)
                        if idx >= 2:
                            g.wait_ge(p.sem(which + "im"), idx - 1)
                        p.inc(g.dma_start(out=wbuf[:, idx % 2, :, :],
                                          in_=wi_d.rearrange("(c p) m -> p c m", p=128)
                                          [:, :, oc * 128:(oc + 1) * 128]),
                              which + "il", 16)

            for oc in range(DC):
                for it in range(NTT):
                    idx = oc * NTT + it
                    tsl = slice(it * TT, (it + 1) * TT)

                    @blk.tensor
                    def _(t, oc=oc, it=it, idx=idx, tsl=tsl, which=which):
                        t.wait_ge(p.sem(which + "il"), 16 * (idx + 1))
                        if idx == 0:
                            p.wait(t, which + "rp")
                        if idx >= 2:
                            t.wait_ge(p.sem(which + "ie"), idx - 1)
                        ps = P2 if idx % 2 == 0 else P3
                        for c in range(DC):
                            mm = t.matmul(ps[:, :], wbuf[:, idx % 2, c, :],
                                          yg[:, c, tsl], start=(c == 0), stop=(c == DC - 1))
                        p.inc(mm, which + "im", 1)

                    @blk.vector
                    def _(v, oc=oc, idx=idx, which=which, bcol=bcol):
                        v.wait_ge(p.sem(which + "im"), idx + 1)
                        if idx >= 1:
                            p.wait(v, which + "sq", idx)
                            p.wait(v, which + "ow", 16 * idx)
                        ps = P2 if idx % 2 == 0 else P3
                        v.tensor_scalar(tmp[:, :], ps[:, :], gb[:, bcol, oc:oc + 1],
                                        None, ALU.add)
                        p.inc(v.engine_nop(), which + "ie", 1)
                        p.inc(v.tensor_copy(rpb[:, :], tmp[:, :]), which + "rb", 1)

                    @blk.scalar
                    def _(s, idx=idx, which=which):
                        s.wait_ge(p.sem(which + "rb"), idx + 1)
                        p.inc(s.activation(out=sq[:, 0, :], in_=tmp[:, :],
                                           func=AF.Square), which + "sq", 1)

                    @blk.tensor
                    def _(t, idx=idx, which=which):
                        t.wait_ge(p.sem(which + "sq"), idx + 1)
                        if idx >= 1:
                            p.wait(t, which + "nc", idx)
                        p.inc(t.matmul(P4[0:2, :], eons[:, :], sq[:, 0, :],
                                       start=True, stop=True), which + "n2", 1)

                    @blk.vector
                    def _(v, idx=idx, which=which):
                        v.wait_ge(p.sem(which + "n2"), idx + 1)
                        if idx >= 1:
                            p.wait(v, which + "nw", 16 * idx)
                        p.inc(v.tensor_copy(n2r[:, :], P4[0:2, :]), which + "nc", 1)

                    @blk.scalar
                    def _(s, oc=oc, idx=idx, tsl=tsl, which=which, out_d=out_d, n2_d=n2_d):
                        s.wait_ge(p.sem(which + "nc"), idx + 1)
                        p.inc(s.dma_start(
                            out=n2_d.rearrange("(o j) t -> j o t", j=2)[:, oc, tsl],
                            in_=n2r[:, :]), which + "nw", 16)
                        s.wait_ge(p.sem(which + "rb"), idx + 1)
                        p.inc(s.dma_start(
                            out=out_d.rearrange("(c p) t -> p c t", p=128)[:, oc, tsl],
                            in_=rpb[:, :]), which + "ow", 16)
        else:
            @blk.gpsimd
            def _(g):
                g.wait_ge(p.sem("vrp"), 16)
                p.inc(g.dma_start(out=wvi_sb,
                                  in_=wvi_d.rearrange("(c p) m -> p c m", p=128)),
                      "vil", 16)

            for tc in range(NTC):
                for half in range(2):
                    idx = tc * 2 + half

                    @blk.tensor
                    def _(t, tc=tc, half=half, idx=idx):
                        t.wait_ge(p.sem("vil"), 16)
                        if idx == 0:
                            p.wait(t, "vrp")
                            p.wait(t, "bvc")
                        if idx >= 2:
                            t.wait_ge(p.sem("vte"), idx - 1)
                        ps = P2 if idx % 2 == 0 else P3
                        for c in range(DC):
                            mm = t.matmul(ps[:, :], yg[:, c, tc * 128:(tc + 1) * 128],
                                          wvi_sb[:, c, half * TT:(half + 1) * TT],
                                          start=(c == 0), stop=(c == DC - 1))
                        p.inc(mm, "vtm", 1)

                    @blk.vector
                    def _(v, half=half, idx=idx):
                        v.wait_ge(p.sem("vtm"), idx + 1)
                        if idx == 0:
                            v.wait_ge(p.sem("kow"), 16 * 32)
                        if idx >= 1:
                            p.wait(v, "vtw", 16 * idx)
                        ps = P2 if idx % 2 == 0 else P3
                        p.inc(v.tensor_tensor(rpb[:, :], ps[:, :],
                                              bvb.rearrange("p c f -> p (c f)")
                                              [:, half * TT:(half + 1) * TT], ALU.add),
                              "vte", 1)

                    @blk.scalar
                    def _(s, tc=tc, half=half, idx=idx):
                        s.wait_ge(p.sem("vte"), idx + 1)
                        p.inc(s.dma_start(out=v3_o[tc * 128:(tc + 1) * 128,
                                                   half * TT:(half + 1) * TT],
                                          in_=rpb[:, :]), "vtw", 16)

    @blk.sync
    def _(sy):
        p.wait(sy, "x3w")
        p.wait(sy, "xnw")
        p.wait(sy, "qow")
        p.wait(sy, "kow")
        p.wait(sy, "qnw")
        p.wait(sy, "knw")
        p.wait(sy, "vtw")
    return p



def build_launch_G():
    p = Prog()
    NCK = T // 128
    q3_d = p.din("q3", [NH, DH, TPC], BF16)
    kf_d = p.din("k3f", [NH, DH, T], BF16)
    vf_d = p.din("v3f", [NH, T, DH], BF16)
    qn_d = p.din("qn3", [NH, TPC])
    kn_d = p.din("kn3f", [NH, T])
    wo_d = p.din("woT", [D, D])
    bo_d = p.din("bo", [128, DC])
    msk_d = p.din("mask", [NTT, NCK, 128, TT], BF16)

    s3_o = p.dout("s3", [D, TPC])
    mb_dram = p.dtmp("mbd", [1, NH])

    khb = p.sb("khb", [64, T], BF16)
    vh = p.sb("vh", [128, NCK, DH + 1], BF16)
    qh = p.sb("qh", [64, TT], BF16)
    pbs = p.sb("pbs", [128, 2, TT], BF16)
    ofm = p.sb("ofm", [128, DC, TT], BF16)
    mskb = p.sb("mskb", [128, NCK, TT], BF16)
    onr = p.sb("onr", [1, TT])
    oev = p.sb("oev", [64, TT])
    s1t = p.sb("s1t", [128, TT])
    xf = p.sb("xf", [128, DC, TT])
    wst2 = p.sb("wst2", [128, 2, DC, 128], BF16)
    ones = p.sb("ones", [128, 2])
    cst = p.sb("cst", [128, 2])
    mrow = p.sb("mrow", [1, NH])
    mb = p.sb("mb", [128, NH])
    nsc = p.sb("nsc", [16, 4])
    gb = p.sb("gb", [128, 1, DC])
    qn_sb = p.sb("qn_sb", [16, TPC])
    kn_sb = p.sb("kn_sb", [16, TPC])

    P0 = p.ps("P0", [128, TT])
    P1 = p.ps("P1", [128, TT])
    P3 = p.ps("P3", [128, TT])
    P4 = p.ps("P4", [128, TT])
    PTf = p.ps("PTf", [128, 128])

    blk = p.block()

    @blk.vector
    def _(v):
        v.memset(ones[:, :], 1.0)
        v.memset(cst[:, 1:2], 1e-38)
        p.inc(v.memset(vh[:, :, DH:DH + 1], 1.0), "init", 1)

    @blk.scalar
    def _(s):
        p.wait(s, "init")

    @blk.tensor
    def _(t):
        p.wait(t, "init")

    @blk.gpsimd
    def _(g):
        p.wait(g, "init")
        p.inc(g.dma_start(out=gb[:, 0, :], in_=bo_d[:, :]), "wld", 16)

    @blk.sync
    def _(sy):
        p.inc(sy.dma_start(out=qn_sb[:, :], in_=qn_d[:, :]), "nld", 16)

    @blk.vector
    def _(v):
        p.wait(v, "nld")
        p.vsf(v, v.reduce_max(nsc[:, 0:1], qn_sb[:, :], axis=AX.X))

    for kc4 in range(4):
        @blk.sync
        def _(sy, kc4=kc4):
            if kc4 > 0:
                p.wait(sy, "nkm", kc4)
            p.inc(sy.dma_start(out=kn_sb[:, :],
                               in_=kn_d[:, kc4 * TPC:(kc4 + 1) * TPC]), "nld2", 16)

        @blk.vector
        def _(v, kc4=kc4):
            v.wait_ge(p.sem("nld2"), 16 * (kc4 + 1))
            p.vsf(v, v.reduce_max(nsc[:, 3:4], kn_sb[:, :], axis=AX.X))
            if kc4 == 0:
                p.inc(v.tensor_copy(nsc[:, 1:2], nsc[:, 3:4]), "nkm", 1)
            else:
                p.inc(v.tensor_max(nsc[:, 1:2], nsc[:, 1:2], nsc[:, 3:4]), "nkm", 1)

    @blk.vector
    def _(v):
        p.wait(v, "nkm")
        p.vsf(v, v.engine_nop())
        p.inc(v.tensor_mul(nsc[:, 2:3], nsc[:, 0:1], nsc[:, 1:2]), "nm1", 1)

    @blk.scalar
    def _(s):
        p.wait(s, "nm1")
        p.inc(s.activation(out=nsc[:, 3:4], in_=nsc[:, 2:3], func=AF.Sqrt), "nm2", 1)

    @blk.vector
    def _(v):
        p.wait(v, "nm2")
        p.inc(v.tensor_scalar_mul(nsc[:, 2:3], nsc[:, 3:4], -0.125), "nm3", 1)

    @blk.sync
    def _(sy):
        p.wait(sy, "nm3")
        p.inc(sy.dma_start(out=mb_dram.rearrange("o h -> (o h)")[None, :],
                           in_=nsc[:, 2:3]), "nm4", 16)
        sy.wait_ge(p.sem("nm4"), 16)
        p.inc(sy.dma_start(out=mrow[0:1, :], in_=mb_dram[0:1, :]), "nm5", 16)

    @blk.tensor
    def _(t):
        t.wait_ge(p.sem("nm5"), 16)
        p.inc(t.matmul(PTf[:, 0:NH], ones[0:1, 0:1].to_broadcast([1, 128]),
                       mrow[0:1, :], start=True, stop=True), "nm6", 1)

    @blk.vector
    def _(v):
        p.wait(v, "nm6")
        p.inc(v.tensor_copy(mb[:, :], PTf[:, 0:NH]), "nm7", 1)

    for tqb in range(NTT):
        tsl = slice(tqb * TT, (tqb + 1) * TT)

        @blk.sync
        def _(sy, tqb=tqb):
            if tqb >= 1:
                p.wait(sy, "mke", tqb * NH * NCK)
            p.inc(sy.dma_start(out=mskb[:, :, :],
                               in_=msk_d[tqb].rearrange("c p t -> p c t")), "mskl", 16)

        for h in range(NH):
            hidx = tqb * NH + h

            @blk.gpsimd
            def _(g, h=h, tqb=tqb, hidx=hidx):
                if hidx >= 1:
                    g.wait_ge(p.sem("avm"), hidx * NCK)
                else:
                    p.wait(g, "init")
                p.inc(g.dma_start(out=khb[:, :], in_=kf_d[h]), "khl", 16)
                p.inc(g.dma_start(out=vh[:, :, 0:DH],
                                  in_=vf_d[h].rearrange("(c p) d -> p c d", p=128)),
                      "khl", 16)
                p.inc(g.dma_start(out=qh[:, 0:TT],
                                  in_=q3_d[h, :, tqb * TT:(tqb + 1) * TT]), "khl", 16)

            @blk.tensor
            def _(t, h=h, tqb=tqb, hidx=hidx):
                t.wait_ge(p.sem("khl"), 48 * (hidx + 1))
                if hidx >= 1:
                    t.wait_ge(p.sem("ocp"), hidx)
                if hidx == 0:
                    p.wait(t, "nm7")
                if h == 0 and tqb > 0:
                    p.wait(t, "s1e")
                base = hidx * NCK

                def smm(ck):
                    psB = P0 if ck % 2 == 0 else P1
                    if base + ck >= 2:
                        t.wait_ge(p.sem("pex"), base + ck - 1)
                    p.inc(t.matmul(psB[:, :], khb[:, ck * 128:(ck + 1) * 128],
                                   qh[:, 0:TT], start=True, stop=True), "smm", 1)

                smm(0)
                smm(1)
                for ck in range(NCK):
                    t.wait_ge(p.sem("pex"), base + ck + 1)
                    p.inc(t.matmul(P3[0:65, :], vh[:, ck, :], pbs[:, ck % 2, :],
                                   start=(ck == 0), stop=(ck == NCK - 1)), "avm", 1)
                    if ck + 2 < NCK:
                        smm(ck + 2)

            @blk.vector
            def _(v, h=h, tqb=tqb, hidx=hidx):
                base = hidx * NCK
                for ck in range(NCK):
                    v.wait_ge(p.sem("smm"), base + ck + 1)
                    if h == 0 and ck == 0:
                        v.wait_ge(p.sem("mskl"), 16 * (tqb + 1))
                    psB = P0 if ck % 2 == 0 else P1
                    p.inc(v.tensor_tensor(psB[:, :], psB[:, :], mskb[:, ck, :],
                                          ALU.add), "mke", 1)

            @blk.scalar
            def _(s, h=h, hidx=hidx):
                base = hidx * NCK
                for ck in range(NCK):
                    s.wait_ge(p.sem("mke"), base + ck + 1)
                    if base + ck >= 2:
                        s.wait_ge(p.sem("avm"), base + ck - 1)
                    psB = P0 if ck % 2 == 0 else P1
                    p.inc(s.activation(out=pbs[:, ck % 2, :], in_=psB[:, :], func=AF.Exp,
                                       bias=mb[:, h:h + 1], scale=0.125), "pex", 1)

            @blk.vector
            def _(v, h=h, hidx=hidx):
                v.wait_ge(p.sem("avm"), (hidx + 1) * NCK)
                p.wait(v, "s1mm")
                p.vsf(v, v.tensor_scalar_add(onr[0:1, :], P3[64:65, :], cst[0:1, 1:2]))
                p.vsf(v, v.reciprocal(onr[0:1, :], onr[0:1, :]))
                p.inc(v.tensor_copy(oev[:, :], P3[0:64, :]), "ocp", 1)

            @blk.tensor
            def _(t, hidx=hidx):
                t.wait_ge(p.sem("ocp"), hidx + 1)
                if hidx >= 1:
                    t.wait_ge(p.sem("oev"), hidx)
                p.inc(t.matmul(P4[0:64, :], ones[0:1, 0:1].to_broadcast([1, 64]),
                               onr[0:1, :], start=True, stop=True), "obc", 1)

            @blk.vector
            def _(v, h=h, hidx=hidx):
                v.wait_ge(p.sem("obc"), hidx + 1)
                p.inc(v.tensor_mul(ofm[(h % 2) * 64:(h % 2) * 64 + 64, h // 2, :],
                                   oev[:, :], P4[0:64, :]), "oev", 1)

        @blk.gpsimd
        def _(g, tqb=tqb):
            for oc in range(DC):
                idx = tqb * DC + oc
                if idx >= 2:
                    g.wait_ge(p.sem("s1mm"), idx - 1)
                p.inc(g.dma_start(out=wst2[:, idx % 2, :, :],
                                  in_=wo_d.rearrange("(c p) m -> p c m", p=128)
                                  [:, :, oc * 128:(oc + 1) * 128]), "wol", 16)

        for oc in range(DC):
            idx = tqb * DC + oc

            @blk.tensor
            def _(t, tqb=tqb, oc=oc, idx=idx):
                t.wait_ge(p.sem("oev"), (tqb + 1) * NH)
                t.wait_ge(p.sem("wol"), 16 * (idx + 1))
                t.wait_ge(p.sem("avm"), (tqb + 1) * NH * NCK)
                if idx >= 2:
                    t.wait_ge(p.sem("s1e"), idx - 1)
                ps = P0 if idx % 2 == 0 else P1
                for c in range(DC):
                    mm = t.matmul(ps[:, :], wst2[:, idx % 2, c, :], ofm[:, c, :],
                                  start=(c == 0), stop=(c == DC - 1))
                p.inc(mm, "s1mm", 1)

            @blk.vector
            def _(v, tqb=tqb, oc=oc, idx=idx):
                v.wait_ge(p.sem("s1mm"), idx + 1)
                p.wait(v, "wld")
                if tqb >= 1:
                    p.wait(v, "s3w", 16 * tqb)
                ps = P0 if idx % 2 == 0 else P1
                p.inc(v.tensor_scalar(s1t[:, :], ps[:, :], gb[:, 0, oc:oc + 1],
                                      None, ALU.add), "s1e", 1)
                p.inc(v.tensor_copy(xf[:, oc, :], s1t[:, :]), "s3c", 1)

        @blk.sync
        def _(sy, tqb=tqb, tsl=tsl):
            p.wait(sy, "s3c")
            p.inc(sy.dma_start(out=s3_o.rearrange("(c p) t -> p c t", p=128)[:, :, tsl],
                               in_=xf[:, :, :]), "s3w", 16)

    @blk.sync
    def _(sy):
        p.wait(sy, "s3w")
    return p



SLW = 2560
NTI = 5
NQC = 20
NHC = 22


def build_launch_H(final=False):
    p = Prog()
    xin_d = p.din("xin", [D, SLW])
    wqkv_d = p.din("wqkv", [D, 3072], BF16)
    wo_d = p.din("wo", [D, D], BF16)
    w1_d = p.din("w1p", [D, 5632], BF16)
    w2_d = p.din("w2p", [2816, D], BF16)
    b1a_d = p.din("b1a", [128, NHC])
    b1g_d = p.din("b1g", [128, NHC])
    b2_d = p.din("b2", [128, DC])
    lng_d = p.din("lng", [128, DC])
    lnb_d = p.din("lnb", [128, DC])
    cs_d = p.din("cs4", [4, 128, SLW], BF16)
    msk_d = p.din("mskh", [6, 128, TT], BF16)
    id_d = p.din("identb", [128, 128], BF16)
    eop_d = p.din("eonsP", [128, 2])
    ebc_d = p.din("ebc", [2, 128])
    if final:
        x3_d = p.din("x3f", [D, TPC])
        s3_d = p.din("s3f", [D, TPC])
        x4_o = p.dout("x4", [D, TPC])
    xo_o = p.dout("xout", [D, SLW])
    o_dram = p.dtmp("od", [D, SLW], BF16)

    x = p.sb("x", [128, DC, SLW])
    qsb = p.sb("qsb", [128, SLW], BF16)
    ksb = p.sb("ksb", [128, SLW], BF16)
    vsb = p.sb("vsb", [128, SLW], BF16)
    vt = p.sb("vt", [128, NQC, 128], BF16)
    mskb = p.sb("mskb", [128, 6, TT], BF16)
    h_sb = p.sb("h_sb", [128, NHC, TT], BF16)
    xt = h_sb.rearrange("p k w -> p (k w)").bitcast(F32)[:, 0:DC * TT].rearrange(
        "p (c w) -> p c w", w=TT)
    xsb = p.sb("xsb", [128, 2, DC, TT], BF16)
    xnb = p.sb("xnb", [128, DC, TT], BF16)
    wqkv_sb = p.sb("wqkv_sb", [128, 2, 24, 128], BF16)
    wf_sb = p.sb("wf_sb", [128, 2, 16, 128], BF16)
    wd_sb = p.sb("wd_sb", [128, 2, NHC, 128], BF16)
    ob_sb = p.sb("ob_sb", [128, 1, DC, TT], BF16)
    cs_sb = p.sb("cs_sb", [128, 1, 4, TT], BF16)
    sqf = p.sb("sqf", [128, TT])
    fbb = p.sb("fbb", [128, TT], BF16)
    t1b = p.sb("t1b", [128, TT], BF16)
    t2b = p.sb("t2b", [128, TT], BF16)
    pbs = p.sb("pbs", [128, TT], BF16)
    obq = p.sb("obq", [128, 128], BF16)
    mxc = p.sb("mxc", [128, 4])
    gg = p.sb("gg", [128, TT])
    tmp = p.sb("tmpa", [128, TT])
    t3b = tmp.bitcast(BF16)[:, 0:TT]
    stb = gg.bitcast(BF16)[:, 0:TT].rearrange("p (a w) -> p a w", w=128)
    sq = ob_sb.rearrange("p b c w -> p (b c w)").bitcast(F32).rearrange(
        "p (a w) -> p a w", w=TT)[:, 0:2, :]
    nsr = sq[0:2, :, :]
    srow = h_sb.rearrange("p k w -> p (k w)").bitcast(F32)[:, DC * TT:(DC + 2) * TT]        .rearrange("p (a w) -> p a w", w=TT)[0:1, :, :]
    ones = p.sb("ones", [128, 2])
    cst = p.sb("cst", [128, 2])
    identb = p.sb("idh", [128, 128], BF16)
    eonsP = p.sb("eoph", [128, 2])
    ebc = p.sb("ebch", [2, 128])
    gb = p.sb("gb", [128, 5, DC])
    b1s = p.sb("b1s", [128, 2, NHC])

    P0 = p.ps("P0", [128, TT])
    P1 = p.ps("P1", [128, TT])
    P2 = p.ps("P2", [128, TT])
    P3 = p.ps("P3", [128, TT])
    PN = p.ps("PN", [128, TT])
    PT = p.ps("PT", [128, TT])
    PO = p.ps("PO", [128, TT])
    PB = p.ps("PB", [128, TT], BF16)

    blk = p.block()

    @blk.vector
    def _(v):
        v.memset(ones[:, :], 1.0)
        v.memset(cst[:, 0:1], 1e-5)
        p.inc(v.memset(cst[:, 1:2], 1e-12), "init", 1)

    @blk.scalar
    def _(s):
        p.wait(s, "init")

    @blk.tensor
    def _(t):
        p.wait(t, "init")

    @blk.gpsimd
    def _(g):
        p.wait(g, "init")
        p.inc(g.dma_start(out=identb[:, :], in_=id_d[:, :]), "wld", 16)
        p.inc(g.dma_start(out=eonsP[:, :], in_=eop_d[:, :]), "wld", 16)
        p.inc(g.dma_start(out=ebc[:, :], in_=ebc_d[:, :]), "wld", 16)
        p.inc(g.dma_start(out=gb[:, 0, :], in_=lng_d[:, :]), "wld", 16)
        p.inc(g.dma_start(out=gb[:, 1, :], in_=lnb_d[:, :]), "wld", 16)
        p.inc(g.dma_start(out=gb[:, 2, :], in_=b2_d[:, :]), "wld", 16)
        p.inc(g.dma_start(out=b1s[:, 0, :], in_=b1a_d[:, :]), "wld", 16)
        p.inc(g.dma_start(out=b1s[:, 1, :], in_=b1g_d[:, :]), "wld", 16)
        p.inc(g.dma_start(out=mskb[:, :, :],
                          in_=msk_d.rearrange("q p t -> p q t")), "wld", 16)

    @blk.sync
    def _(sy):
        p.inc(sy.dma_start(out=x[:, :, :],
                           in_=xin_d.rearrange("(c p) t -> p c t", p=128)), "xld", 16)

    for hp in range(8):
        @blk.gpsimd
        def _(g, hp=hp):
            if hp >= 2:
                g.wait_ge(p.sem("qkm"), (hp - 1) * 15)
            for s3i in range(3):
                p.inc(g.dma_start(
                    out=wqkv_sb[:, hp % 2, s3i * 8:(s3i + 1) * 8, :],
                    in_=wqkv_d.rearrange("(c p) m -> p c m", p=128)
                    [:, :, s3i * 1024 + hp * 128:s3i * 1024 + (hp + 1) * 128]),
                    "qkl", 16)

        for ti in range(NTI):
            csl = slice(ti * TT, (ti + 1) * TT)

            @blk.vector
            def _(v, hp=hp, ti=ti, csl=csl):
                idx = hp * NTI + ti
                if idx == 0:
                    p.wait(v, "xld")
                if idx >= 2:
                    p.wait(v, "qkm", 3 * (idx - 1))
                for c in range(4):
                    v.tensor_copy(xsb[:, idx % 2, c, :], x[:, c, csl])
                if ti == 0:
                    v.memset(xsb[:, idx % 2, 4:8, 0:1], 0.0)
                    for c in range(4, 8):
                        cp = v.tensor_copy(xsb[:, idx % 2, c, 1:TT],
                                           x[:, c, 0:TT - 1])
                else:
                    for c in range(4, 8):
                        cp = v.tensor_copy(xsb[:, idx % 2, c, :],
                                           x[:, c, ti * TT - 1:(ti + 1) * TT - 1])
                p.inc(cp, "xst", 1)

            @blk.tensor
            def _(t, hp=hp, ti=ti, csl=csl):
                idx = hp * NTI + ti
                t.wait_ge(p.sem("qkl"), 48 * (hp + 1))
                t.wait_ge(p.sem("xst"), idx + 1)
                if idx >= 1:
                    p.wait(t, "qke", 3 * idx)
                for s3i, ps in ((0, P0), (1, P1), (2, P2)):
                    for c in range(DC):
                        mm = t.matmul(ps[:, :], wqkv_sb[:, hp % 2, s3i * 8 + c, :],
                                      xsb[:, idx % 2, c, :],
                                      start=(c == 0), stop=(c == DC - 1))
                    p.inc(mm, "qkm", 1)

            @blk.vector
            def _(v, hp=hp, ti=ti, csl=csl):
                idx = hp * NTI + ti
                v.wait_ge(p.sem("qkm"), 3 * (idx + 1))
                if ti == 0 and hp >= 1:
                    p.wait(v, "avm")
                v.tensor_copy(qsb[:, csl], P0[:, :])
                v.tensor_copy(ksb[:, csl], P1[:, :])
                p.inc(v.tensor_copy(vsb[:, csl], P2[:, :]), "qke", 3)

        for ti in range(NTI):
            csl = slice(ti * TT, (ti + 1) * TT)
            idx = hp * NTI + ti

            @blk.sync
            def _(sy, hp=hp, ti=ti, idx=idx, csl=csl):
                if idx >= 1:
                    p.wait(sy, "rpk", idx)
                p.inc(sy.dma_start(out=cs_sb[:, 0, :, :],
                                   in_=cs_d[:, :, csl].rearrange("f p t -> p f t")),
                      "csl", 16)

            for which, strip, tcol in (("q", qsb, 0), ("k", ksb, 2)):
                @blk.scalar
                def _(s, which=which, strip=strip, idx=idx, csl=csl):
                    s.wait_ge(p.sem("qke"), 3 * (idx + 1))
                    if idx >= 1 or which == "k":
                        p.wait(s, "fme", 2 * idx + (1 if which == "k" else 0))
                    p.inc(s.activation(out=sqf[:, :], in_=strip[:, csl],
                                       func=AF.Square), which + "sq", 1)

                @blk.tensor
                def _(t, which=which, idx=idx, hp=hp, ti=ti):
                    t.wait_ge(p.sem(which + "sq"), idx + 1)
                    p.wait(t, "wld")
                    if hp >= 1 and ti == 0 and which == "q":
                        p.wait(t, "pex")
                    p.inc(t.matmul(PN[0:2, :], eonsP[:, :], sqf[:, :],
                                   start=True, stop=True), which + "n2", 1)

                @blk.scalar
                def _(s, which=which, idx=idx):
                    s.wait_ge(p.sem(which + "n2"), idx + 1)
                    p.inc(s.activation(out=nsr[:, 0, :], in_=PN[0:2, :],
                                       func=AF.Sqrt), which + "sr", 1)

                @blk.vector
                def _(v, which=which, idx=idx):
                    v.wait_ge(p.sem(which + "sr"), idx + 1)
                    p.vsf(v, v.tensor_scalar(nsr[:, 1, :], nsr[:, 0, :],
                                             cst[0:2, 1:2], None, ALU.max))
                    p.inc(v.reciprocal(nsr[:, 1, :], nsr[:, 1, :]), which + "rc", 1)

                @blk.tensor
                def _(t, which=which, idx=idx, hp=hp, ti=ti):
                    t.wait_ge(p.sem(which + "rc"), idx + 1)
                    if hp >= 1 and ti == 0 and which == "q":
                        p.wait(t, "ste")
                        p.wait(t, "ote")
                    p.inc(t.matmul(PT[:, :], ebc[:, :], nsr[:, 1, :],
                                   start=True, stop=True), which + "fb", 1)

                @blk.vector
                def _(v, which=which, strip=strip, tcol=tcol, idx=idx, csl=csl):
                    v.wait_ge(p.sem(which + "fb"), idx + 1)
                    p.vsf(v, v.tensor_copy(fbb[:, :], PT[:, :]))
                    p.inc(v.tensor_mul(strip[:, csl], strip[:, csl], fbb[:, :]),
                          "fme", 1)
                    v.wait_ge(p.sem("csl"), 16 * (idx + 1))
                    v.tensor_mul(t1b[:, :], strip[:, csl], cs_sb[:, 0, tcol, :])
                    cp = v.tensor_mul(t2b[:, :], strip[:, csl],
                                      cs_sb[:, 0, tcol + 1, :])
                    p.vsf(v, cp)
                    for bb in range(4):
                        sl_o = slice(bb * 32, bb * 32 + 32)
                        sl_i = slice((bb ^ 1) * 32, (bb ^ 1) * 32 + 32)
                        v.tensor_copy(t3b[sl_o, :], t2b[sl_i, :])
                    p.vsf(v, v.engine_nop())
                    p.inc(v.tensor_add(strip[:, csl], t1b[:, :], t3b[:, :]),
                          "rp" + which, 1)

            @blk.tensor
            def _(t, hp=hp, ti=ti, idx=idx):
                t.wait_ge(p.sem("qke"), 3 * (idx + 1))
                if ti == 0 and hp >= 1:
                    p.wait(t, "avm")
                    p.wait(t, "onv")
                    p.wait(t, "ote")
                for ck in range(4):
                    for b in range(2):
                        if idx * 8 + ck * 2 + b >= 1:
                            p.wait(t, "vte", idx * 8 + ck * 2 + b)
                        p.inc(t.transpose(PB[0:128, 0:64],
                                          vsb[64 * b:64 * b + 64,
                                              ti * TT + ck * 128:ti * TT + (ck + 1) * 128],
                                          identb[64 * b:64 * b + 64, 64 * b:64 * b + 64]),
                              "vtm", 1)

            @blk.vector
            def _(v, hp=hp, ti=ti, idx=idx):
                for ck in range(4):
                    for b in range(2):
                        v.wait_ge(p.sem("vtm"), idx * 8 + ck * 2 + b + 1)
                        p.inc(v.tensor_copy(vt[:, ti * 4 + ck, 64 * b:64 * b + 64],
                                            PB[0:128, 0:64]), "vte", 1)

        for b in range(2):
            for qc in range(NQC):
                aidx = (hp * 2 + b) * NQC + qc
                kw0 = max(0, (qc // 2) * 256 - 256)
                qrow = slice(64 * b, 64 * b + 64)

                @blk.tensor
                def _(t, hp=hp, b=b, qc=qc, aidx=aidx, kw0=kw0, qrow=qrow):
                    p.wait(t, "rpq", (hp + 1) * NTI)
                    p.wait(t, "rpk", (hp + 1) * NTI)
                    p.wait(t, "vte", (hp + 1) * NTI * 8)
                    if aidx >= 1:
                        p.wait(t, "mka", aidx)
                    p.inc(t.matmul(PN[:, :],
                                   qsb[qrow, qc * 128:(qc + 1) * 128],
                                   ksb[qrow, kw0:kw0 + TT],
                                   start=True, stop=True), "smm", 1)

                @blk.vector
                def _(v, b=b, qc=qc, aidx=aidx):
                    v.wait_ge(p.sem("smm"), aidx + 1)
                    ms = {0: 0, 1: 1, 4: 2, 5: 3}.get(qc, 4 + (qc % 2))
                    p.vsf(v, v.tensor_tensor(PN[:, :], PN[:, :], mskb[:, ms, :],
                                             ALU.add))
                    p.vsf(v, v.reduce_max(mxc[:, 0:1], PN[:, :], axis=AX.X,
                                          negate=True))
                    p.inc(v.tensor_scalar_mul(mxc[:, 1:2], mxc[:, 0:1], 8.0),
                          "mka", 1)

                @blk.scalar
                def _(s, aidx=aidx):
                    s.wait_ge(p.sem("mka"), aidx + 1)
                    if aidx >= 1:
                        s.wait_ge(p.sem("stm"), aidx * 4)
                    p.inc(s.activation(out=pbs[:, :], in_=PN[:, :], func=AF.Exp,
                                       scale=8.0, bias=mxc[:, 1:2],
                                       accum_out=mxc[:, 2:3]), "pex", 1)

                @blk.vector
                def _(v, aidx=aidx):
                    v.wait_ge(p.sem("pex"), aidx + 1)
                    p.vsf(v, v.tensor_scalar_add(mxc[:, 2:3], mxc[:, 2:3],
                                                 cst[:, 1:2]))
                    p.inc(v.reciprocal(mxc[:, 3:4], mxc[:, 2:3]), "den", 1)

                @blk.tensor
                def _(t, aidx=aidx):
                    for ck in range(4):
                        sidx = aidx * 4 + ck
                        t.wait_ge(p.sem("pex"), aidx + 1)
                        if ck == 0 and aidx >= 1:
                            t.wait_ge(p.sem("ote"), aidx)
                        if sidx >= 1:
                            p.wait(t, "ste", sidx)
                        p.inc(t.transpose(PB[:, 0:128],
                                          pbs[:, ck * 128:(ck + 1) * 128],
                                          identb[:, :]), "stm", 1)

                @blk.vector
                def _(v, aidx=aidx):
                    for ck in range(4):
                        sidx = aidx * 4 + ck
                        v.wait_ge(p.sem("stm"), sidx + 1)
                        if ck == 0 and aidx >= 1:
                            v.wait_ge(p.sem("avm"), aidx)
                        p.inc(v.tensor_copy(stb[:, ck, :], PB[:, 0:128]), "ste", 1)

                @blk.tensor
                def _(t, b=b, qc=qc, aidx=aidx, kw0=kw0):
                    t.wait_ge(p.sem("ste"), (aidx + 1) * 4)
                    if aidx >= 1:
                        p.wait(t, "onv", aidx)
                    for ck in range(4):
                        mm = t.matmul(PO[:, 0:64], stb[:, ck, :],
                                      vt[:, kw0 // 128 + ck, 64 * b:64 * b + 64],
                                      start=(ck == 0), stop=(ck == 3))
                    p.inc(mm, "avm", 1)

                @blk.vector
                def _(v, aidx=aidx):
                    v.wait_ge(p.sem("avm"), aidx + 1)
                    v.wait_ge(p.sem("den"), aidx + 1)
                    p.inc(v.tensor_scalar_mul(obq[:, 0:64], PO[:, 0:64],
                                              mxc[:, 3:4]), "onv", 1)

                @blk.tensor
                def _(t, aidx=aidx):
                    t.wait_ge(p.sem("onv"), aidx + 1)
                    if aidx >= 1:
                        p.wait(t, "ote", aidx)
                    p.inc(t.transpose(PB[0:64, 0:128], obq[:, 0:64], identb[:, :]),
                          "otm", 1)

                @blk.vector
                def _(v, aidx=aidx):
                    v.wait_ge(p.sem("otm"), aidx + 1)
                    if aidx >= 1:
                        p.wait(v, "oww", 16 * aidx)
                    p.inc(v.tensor_copy(obq[0:64, 0:128], PB[0:64, 0:128]), "ote", 1)

                @blk.sync
                def _(sy, hp=hp, b=b, qc=qc, aidx=aidx):
                    p.wait(sy, "ote", aidx + 1)
                    h = hp * 2 + b
                    p.inc(sy.dma_start(
                        out=o_dram[h * 64:(h + 1) * 64, qc * 128:(qc + 1) * 128],
                        in_=obq[0:64, 0:128]), "oww", 16)

    for ti in range(NTI):
        csl = slice(ti * TT, (ti + 1) * TT)

        @blk.sync
        def _(sy, ti=ti, csl=csl):
            if ti == 0:
                p.wait(sy, "oww")
            else:
                p.wait(sy, "wom", ti * DC)
            p.inc(sy.dma_start(out=ob_sb[:, 0, :, :],
                               in_=o_dram.rearrange("(c p) t -> p c t", p=128)
                               [:, :, csl]), "obl", 16)

        @blk.gpsimd
        def _(g, ti=ti):
            for oc in range(DC):
                idx = ti * DC + oc
                if idx >= 2:
                    g.wait_ge(p.sem("wom"), idx - 1)
                p.inc(g.dma_start(out=wd_sb[:, idx % 2, 0:DC, :],
                                  in_=wo_d.rearrange("(c p) m -> p c m", p=128)
                                  [:, :, oc * 128:(oc + 1) * 128]), "wol", 16)

        for oc in range(DC):
            idx = ti * DC + oc

            @blk.tensor
            def _(t, ti=ti, oc=oc, idx=idx):
                t.wait_ge(p.sem("obl"), 16 * (ti + 1))
                t.wait_ge(p.sem("wol"), 16 * (idx + 1))
                if idx == 0:
                    p.wait(t, "avm")
                if idx >= 2:
                    t.wait_ge(p.sem("xr1"), idx - 1)
                ps = P0 if idx % 2 == 0 else P1
                for c in range(DC):
                    mm = t.matmul(ps[:, :], wd_sb[:, idx % 2, c, :],
                                  ob_sb[:, 0, c, :],
                                  start=(c == 0), stop=(c == DC - 1))
                p.inc(mm, "wom", 1)

            @blk.vector
            def _(v, ti=ti, oc=oc, idx=idx, csl=csl):
                v.wait_ge(p.sem("wom"), idx + 1)
                ps = P0 if idx % 2 == 0 else P1
                p.inc(v.tensor_add(x[:, oc, csl], x[:, oc, csl], ps[:, :]),
                      "xr1", 1)

    for ti in range(NTI):
        csl = slice(ti * TT, (ti + 1) * TT)

        @blk.vector
        def _(v, ti=ti, csl=csl):
            p.wait(v, "xr1", (ti + 1) * DC)
            if ti >= 1:
                p.wait(v, "xr2", ti * DC)
            for c in range(4):
                v.tensor_copy(xt[:, c, :], x[:, c, csl])
            if ti == 0:
                v.memset(xt[:, 4:8, 0:1], 0.0)
                for c in range(4, 8):
                    cp = v.tensor_copy(xt[:, c, 1:TT], x[:, c, 0:TT - 1])
            else:
                for c in range(4, 8):
                    cp = v.tensor_copy(xt[:, c, :],
                                       x[:, c, ti * TT - 1:(ti + 1) * TT - 1])
            p.inc(cp, "ln_xld", 1)

        emit_ln_tile(p, blk, f"h{ti}", xt, [xnb[:, c, :] for c in range(DC)],
                     sq, srow, P0, P1, P2, P3, ones, tmp=tmp, eps_ap=cst[0:1, 0:1],
                     gain_ap=[gb[:, 0, c:c + 1] for c in range(DC)],
                     bias_ap=[gb[:, 1, c:c + 1] for c in range(DC)])

        @blk.gpsimd
        def _(g, ti=ti):
            for hc in range(NHC):
                idx = ti * NHC + hc
                if idx >= 2:
                    g.wait_ge(p.sem("agm"), 2 * (idx - 1))
                p.inc(g.dma_start(out=wf_sb[:, idx % 2, 0:8, :],
                                  in_=w1_d.rearrange("(c p) m -> p c m", p=128)
                                  [:, :, hc * 128:(hc + 1) * 128]), "w1l", 16)
                p.inc(g.dma_start(out=wf_sb[:, idx % 2, 8:16, :],
                                  in_=w1_d.rearrange("(c p) m -> p c m", p=128)
                                  [:, :, (NHC + hc) * 128:(NHC + hc + 1) * 128]),
                      "w1l", 16)

        for hc in range(NHC):
            idx = ti * NHC + hc

            @blk.tensor
            def _(t, ti=ti, hc=hc, idx=idx):
                p.wait(t, "ln_xn")
                t.wait_ge(p.sem("w1l"), 32 * (idx + 1))
                if idx >= 2:
                    p.wait(t, "hev", idx - 1)
                psA = P0 if idx % 2 == 0 else P2
                psG = P1 if idx % 2 == 0 else P3
                for c in range(DC):
                    t.matmul(psA[:, :], wf_sb[:, idx % 2, c, :], xnb[:, c, :],
                             start=(c == 0), stop=(c == DC - 1))
                for c in range(DC):
                    mm = t.matmul(psG[:, :], wf_sb[:, idx % 2, 8 + c, :],
                                  xnb[:, c, :], start=(c == 0), stop=(c == DC - 1))
                p.inc(mm, "agm", 2)

            @blk.scalar
            def _(s, hc=hc, idx=idx):
                s.wait_ge(p.sem("agm"), 2 * (idx + 1))
                if idx >= 1:
                    s.wait_ge(p.sem("hev"), idx)
                psG = P1 if idx % 2 == 0 else P3
                p.inc(s.activation(out=gg[:, :], in_=psG[:, :], func=AF.Gelu,
                                   bias=b1s[:, 1, hc:hc + 1]), "glu", 1)

            @blk.vector
            def _(v, hc=hc, idx=idx):
                v.wait_ge(p.sem("glu"), idx + 1)
                psA = P0 if idx % 2 == 0 else P2
                v.tensor_scalar(tmp[:, :], psA[:, :], b1s[:, 0, hc:hc + 1],
                                None, ALU.add)
                v.tensor_mul(tmp[:, :], tmp[:, :], gg[:, :])
                p.inc(v.tensor_copy(h_sb[:, hc, :], tmp[:, :]), "hev", 1)

        @blk.gpsimd
        def _(g, ti=ti):
            for oc in range(DC):
                idx = ti * DC + oc
                if idx <= 1:
                    g.wait_ge(p.sem("wom"), NTI * DC)
                if idx >= 2:
                    g.wait_ge(p.sem("w2m"), idx - 1)
                p.inc(g.dma_start(out=wd_sb[:, idx % 2, :, :],
                                  in_=w2_d.rearrange("(k p) m -> p k m", p=128)
                                  [:, :, oc * 128:(oc + 1) * 128]), "w2l", 16)

        for oc in range(DC):
            idx = ti * DC + oc

            @blk.tensor
            def _(t, ti=ti, oc=oc, idx=idx):
                p.wait(t, "hev", (ti + 1) * NHC)
                t.wait_ge(p.sem("w2l"), 16 * (idx + 1))
                if idx >= 1:
                    p.wait(t, "w2e", idx)
                ps = P2 if idx % 2 == 0 else P3
                for k in range(NHC):
                    mm = t.matmul(ps[:, :], wd_sb[:, idx % 2, k, :], h_sb[:, k, :],
                                  start=(k == 0), stop=(k == NHC - 1))
                p.inc(mm, "w2m", 1)

            @blk.vector
            def _(v, ti=ti, oc=oc, idx=idx, csl=csl):
                v.wait_ge(p.sem("w2m"), idx + 1)
                if ti >= 1:
                    p.wait(v, "xow", 16 * ti)
                ps = P2 if idx % 2 == 0 else P3
                v.tensor_scalar(tmp[:, :], ps[:, :], gb[:, 2, oc:oc + 1],
                                None, ALU.add)
                p.inc(v.engine_nop(), "w2e", 1)
                p.inc(v.tensor_add(x[:, oc, csl], x[:, oc, csl], tmp[:, :]),
                      "xr2", 1)

        @blk.sync
        def _(sy, ti=ti, csl=csl):
            p.wait(sy, "xr2")
            p.inc(sy.dma_start(out=xo_o.rearrange("(c p) t -> p c t", p=128)[:, :, csl],
                               in_=x[:, :, csl]), "xow", 16)

    if final:
        for ti in range(1, NTI):
            csl = slice(ti * TT, (ti + 1) * TT)
            osl = slice((ti - 1) * TT, ti * TT)

            @blk.sync
            def _(sy, ti=ti, osl=osl):
                if ti == 1:
                    p.wait(sy, "w2m")
                if ti >= 2:
                    p.wait(sy, "x4c", (ti - 1) * DC)
                p.inc(sy.dma_start(out=xt[:, :, :], in_=x3_d.rearrange(
                    "(c p) t -> p c t", p=128)[:, :, osl]), "x3l", 16)
                p.inc(sy.dma_start(out=ob_sb.rearrange("p b c w -> p (b c w)")
                                   .bitcast(F32).rearrange("p (c w) -> p c w", w=TT)
                                   [:, 0:4, :], in_=s3_d.rearrange(
                                       "(c p) t -> p c t", p=128)[:, 0:4, osl]),
                      "s3l", 16)
                p.inc(sy.dma_start(out=xnb.rearrange("p c w -> p (c w)")
                                   .bitcast(F32).rearrange("p (c w) -> p c w", w=TT)
                                   [:, 0:4, :], in_=s3_d.rearrange(
                                       "(c p) t -> p c t", p=128)[:, 4:8, osl]),
                      "s3l", 16)

            for oc in range(DC):
                idx = (ti - 1) * DC + oc

                @blk.scalar
                def _(s, ti=ti, oc=oc, idx=idx, csl=csl):
                    s.wait_ge(p.sem("xow"), 16 * (ti + 1))
                    if idx >= 1:
                        s.wait_ge(p.sem("x4c"), idx)
                    p.inc(s.activation(out=gg[:, :], in_=x[:, oc, csl],
                                       func=AF.Silu), "fsl", 1)

                @blk.vector
                def _(v, ti=ti, oc=oc, idx=idx):
                    v.wait_ge(p.sem("fsl"), idx + 1)
                    v.wait_ge(p.sem("x3l"), 16 * ti)
                    v.wait_ge(p.sem("s3l"), 32 * ti)
                    if idx >= 1:
                        p.wait(v, "x4w", 16 * idx)
                    if oc < 4:
                        sb3 = ob_sb.rearrange("p b c w -> p (b c w)").bitcast(F32)
                        sb3 = sb3.rearrange("p (c w) -> p c w", w=TT)[:, oc, :]
                    else:
                        sb3 = xnb.rearrange("p c w -> p (c w)").bitcast(F32)
                        sb3 = sb3.rearrange("p (c w) -> p c w", w=TT)[:, oc - 4, :]
                    v.tensor_mul(tmp[:, :], gg[:, :], sb3)
                    p.inc(v.tensor_add(tmp[:, :], tmp[:, :], xt[:, oc, :]),
                          "x4c", 1)

                @blk.sync
                def _(sy, ti=ti, oc=oc, idx=idx, osl=osl):
                    p.wait(sy, "x4c", idx + 1)
                    p.inc(sy.dma_start(out=x4_o.rearrange("(c p) t -> p c t", p=128)
                                       [:, oc, osl], in_=tmp[:, :]), "x4w", 16)

        @blk.sync
        def _(sy):
            p.wait(sy, "x4w")

    @blk.sync
    def _(sy):
        p.wait(sy, "xow")
    return p



def host_h_masks(r):
    out = np.zeros((6, 128, TT), np.float32)
    qcs = [0, 1, 4, 5, 6, 7]
    for si, qc in enumerate(qcs):
        kw0 = max(0, (qc // 2) * 256 - 256)
        qp = qc * 128 + np.arange(128)[:, None]
        kj = kw0 + np.arange(TT)[None, :]
        d = qp - kj
        m = np.where((d < 0) | (d > 256), -1e30, 0.0).astype(np.float32)
        if r == 0:
            m = np.where(kj < 512, -1e30, m)
        out[si] = m
    return out.astype(ml_dtypes.bfloat16)


def host_h_tables(q0, qs, ks):
    t = np.arange(q0 - 512, q0 + TPC, dtype=np.float64)
    t = np.maximum(t, 0.0)
    half = np.arange(0, 64, 2, dtype=np.float64)
    invf = 1.0 / 10000.0 ** (half / 64.0)
    bs = (half + 0.4 * 64) / (1.4 * 64)
    power = (t - 4096.0) / 128.0
    ang = t[None, :] * invf[:, None]
    sc = bs[:, None] ** power[None, :]
    cosv = np.cos(ang)
    sinv = np.sin(ang)
    d64 = np.arange(64)
    i32 = d64 % 32
    sig = np.where(d64 < 32, 1.0, -1.0)
    qs = np.asarray(qs, np.float64)
    ks = np.asarray(ks, np.float64)
    csS = qs[d64, None] * cosv[i32] * sc[i32]
    ssS = sig[:, None] * qs[d64, None] * sinv[i32] * sc[i32]
    csI = ks[d64, None] * cosv[i32] / sc[i32]
    ssI = sig[:, None] * ks[d64, None] * sinv[i32] / sc[i32]
    out = np.stack([np.tile(csS, (2, 1)), np.tile(ssS, (2, 1)),
                    np.tile(csI, (2, 1)), np.tile(ssI, (2, 1))])
    return np.ascontiguousarray(out.astype(ml_dtypes.bfloat16))


def host_h_weights(Wqkv, Wo, W1, b1, W2, b2, lng, lnb):
    BF = ml_dtypes.bfloat16
    w1p = np.zeros((D, 5632), np.float32)
    w1p[:, 0:2730] = W1[:, 0:2730]
    w1p[:, 2816:2816 + 2730] = W1[:, 2730:5460]
    b1a = np.zeros(2816, np.float32)
    b1g = np.zeros(2816, np.float32)
    b1a[0:2730] = b1[0:2730]
    b1g[0:2730] = b1[2730:5460]
    w2p = np.zeros((2816, D), np.float32)
    w2p[0:2730] = W2

    def b128(b):
        return np.ascontiguousarray(np.asarray(b, np.float32).reshape(-1, 128).T)
    return dict(
        wqkv=np.ascontiguousarray(Wqkv.astype(BF)),
        wo=np.ascontiguousarray(Wo.astype(BF)),
        w1p=np.ascontiguousarray(w1p.astype(BF)),
        w2p=np.ascontiguousarray(w2p.astype(BF)),
        b1a=b128(b1a), b1g=b128(b1g), b2=b128(b2),
        lng=b128(lng), lnb=b128(lnb))


def host_h_consts():
    eonsP = np.zeros((128, 2), np.float32)
    eonsP[0:64, 0] = 1.0
    eonsP[64:128, 1] = 1.0
    ebc = np.zeros((2, 128), np.float32)
    ebc[0, 0:64] = 1.0
    ebc[1, 64:128] = 1.0
    return dict(identb=IDENT128.astype(ml_dtypes.bfloat16), eonsP=eonsP, ebc=ebc)


# version 02
# speedup vs baseline: 1.1687x; 1.1687x over previous
import numpy as np
import ml_dtypes
from contextlib import ExitStack

import concourse.bass as bass
import concourse.mybir as mybir
from concourse.bass_utils import run_bass_kernel_spmd

F32 = mybir.dt.float32
BF16 = mybir.dt.bfloat16
AF = mybir.ActivationFunctionType
ALU = mybir.AluOpType
AX = mybir.AxisListType
BF = ml_dtypes.bfloat16

NCORES = 8
B = 2
T = 8192
D = 1024
TPC = 2048
DC = D // 128
TT = 512
NTT = TPC // TT
NTC = TPC // 128
NH = 16
DH = 64


class Prog:
    def __init__(self):
        self.nc = bass.Bass("TRN2", target_bir_lowering=False, debug=False,
                            detect_race_conditions=False)
        self.st = ExitStack()
        self._sems = {}
        self.cnt = {}

    def sem(self, name):
        if name not in self._sems:
            self._sems[name] = self.st.enter_context(self.nc.semaphore(name))
            self.cnt[name] = 0
        return self._sems[name]

    def inc(self, inst, name, n=1):
        inst.then_inc(self.sem(name), n)
        self.cnt[name] += n
        return self.cnt[name]

    def wait(self, eng, name, val=None):
        v = self.cnt.get(name, 0) if val is None else val
        if v > 0:
            eng.wait_ge(self.sem(name), v)

    def sb(self, name, shape, dtype=F32):
        return self.st.enter_context(self.nc.sbuf_tensor(name, shape, dtype))

    def ps(self, name, shape, dtype=F32):
        return self.st.enter_context(self.nc.psum_tensor(name, shape, dtype))

    def din(self, name, shape, dtype=F32):
        return self.nc.dram_tensor(name, list(shape), dtype, kind="ExternalInput").ap()

    def dout(self, name, shape, dtype=F32):
        return self.nc.dram_tensor(name, list(shape), dtype, kind="ExternalOutput").ap()

    def dtmp(self, name, shape, dtype=F32):
        return self.nc.dram_tensor(name, list(shape), dtype).ap()

    def block(self):
        return self.st.enter_context(self.nc.Block())

    def vsf(self, v, inst):
        n = self.inc(inst, "vsf", 1)
        v.wait_ge(self.sem("vsf"), n)

    def close(self):
        self.st.close()


def _run(p, in_maps):
    res = run_bass_kernel_spmd(p.nc, in_maps, core_ids=list(range(NCORES)))
    p.close()
    return res.results


def host_fm(a2d):
    return np.ascontiguousarray(np.asarray(a2d, dtype=np.float32).T)


IDENT128 = np.eye(128, dtype=np.float32)



def emit_ln_tile(p, blk, pre, xt, xn_bf, sq, srow, psA, psB, psC, psD, ones,
                 nchunk=DC, tmp=None, xn_f32=None, dmodel=D,
                 gain_ap=None, bias_ap=None, eps_ap=None, extra_wait=()):
    @blk.tensor
    def _(t):
        p.wait(t, "ln_xld")
        for w in extra_wait:
            p.wait(t, w)
        for c in range(nchunk):
            mm = t.matmul(psA[0:1, :], ones[:, 0:1], xt[:, c, :],
                          start=(c == 0), stop=(c == nchunk - 1))
        p.inc(mm, "ln_s1", 1)

    sqg0 = p.cnt.get("sqg", 0)

    @blk.scalar
    def _(s):
        p.wait(s, "ln_xld")
        for w in extra_wait:
            p.wait(s, w)
        for c in range(nchunk):
            g = sqg0 + c
            if g >= 1:
                s.wait_ge(p.sem("sqc"), g)
            p.inc(s.activation(out=sq[:, 0, :], in_=xt[:, c, :], func=AF.Square),
                  "sqg", 1)

    @blk.tensor
    def _(t):
        for c in range(nchunk):
            g = sqg0 + c
            t.wait_ge(p.sem("sqg"), g + 1)
            mm = t.matmul(psB[0:1, :], ones[:, 0:1], sq[:, 0, :],
                          start=(c == 0), stop=(c == nchunk - 1))
            p.inc(mm, "sqc", 1)

    @blk.vector
    def _(v):
        p.wait(v, "ln_s1")
        p.wait(v, "sqc")
        p.vsf(v, v.tensor_scalar_mul(srow[0:1, 0, :], psA[0:1, :], 1.0 / dmodel))
        p.vsf(v, v.tensor_scalar_mul(srow[0:1, 1, :], psB[0:1, :], 1.0 / dmodel))
        p.vsf(v, v.tensor_mul(psA[0:1, :], srow[0:1, 0, :], srow[0:1, 0, :]))
        p.inc(v.tensor_sub(srow[0:1, 1, :], srow[0:1, 1, :], psA[0:1, :]),
              "ln_va", 1)

    @blk.scalar
    def _(s):
        p.wait(s, "ln_va")
        s.activation(out=psB[0:1, :], in_=srow[0:1, 1, :], func=AF.Ln, bias=eps_ap)
        p.inc(s.activation(out=srow[0:1, 1, :], in_=psB[0:1, :], func=AF.Exp,
                           scale=-0.5), "ln_rm", 1)

    @blk.vector
    def _(v):
        p.wait(v, "ln_rm")
        p.vsf(v, v.tensor_mul(srow[0:1, 0, :], srow[0:1, 0, :], srow[0:1, 1, :]))
        p.inc(v.tensor_scalar_mul(srow[0:1, 0, :], srow[0:1, 0, :], -1.0),
              "ln_rw", 1)

    @blk.tensor
    def _(t):
        p.wait(t, "ln_rw")
        t.matmul(psC[:, :], ones[0:1, 0:1].to_broadcast([1, 128]),
                 srow[0:1, 0, :], start=True, stop=True)
        p.inc(t.matmul(psD[:, :], ones[0:1, 0:1].to_broadcast([1, 128]),
                       srow[0:1, 1, :], start=True, stop=True), "ln_bc", 1)

    @blk.vector
    def _(v):
        p.wait(v, "ln_bc")
        for c in range(nchunk):
            dst = xn_f32[c] if xn_f32 is not None else tmp[:, :]
            v.tensor_mul(dst, xt[:, c, :], psD[:, :])
            v.tensor_add(dst, dst, psC[:, :])
            if gain_ap is not None:
                v.tensor_scalar(dst, dst, gain_ap[c], bias_ap[c], ALU.mult, ALU.add)
            cp = v.tensor_copy(xn_bf[c], dst)
        p.inc(cp, "ln_xn", 1)



def emit_fp4_qdq(p, blk, xin, qa, qsc, qrc, qc1, qc2, qout, rows, width):
    nb = width // 16

    @blk.scalar
    def _(s):
        p.wait(s, "qqin")
        s.activation(out=qa[0:rows, 0:width], in_=xin, func=AF.Abs)
        p.inc(s.sign(qc2[0:rows, 0:width], xin), "qqab", 1)

    @blk.vector
    def _(v):
        p.wait(v, "qqab")
        a3 = qa[0:rows, 0:width].rearrange("p (b k) -> p b k", k=16)
        p.vsf(v, v.tensor_reduce(qsc[0:rows, 0:nb, 0:1], a3, axis=AX.X, op=ALU.max))
        p.vsf(v, v.tensor_scalar(qsc[0:rows, 0:nb, 0:1], qsc[0:rows, 0:nb, 0:1],
                                 1.0 / 6.0, 1e-12, ALU.mult, ALU.max))
        p.vsf(v, v.reciprocal(qrc[0:rows, 0:nb, 0:1], qsc[0:rows, 0:nb, 0:1]))
        v.tensor_tensor(a3, a3, qrc[0:rows, 0:nb, 0:1].to_broadcast([rows, nb, 16]),
                        ALU.mult)
        z = qa[0:rows, 0:width]
        c1 = qc1[0:rows, 0:width]
        v.tensor_scalar(c1, z, 0.25, None, ALU.is_gt)
        for th in (0.75, 1.25, 1.75):
            v.scalar_tensor_tensor(c1, z, th, c1, ALU.is_gt, ALU.add)
        v.tensor_scalar_mul(c1, c1, 0.5)
        for th in (2.5, 3.5):
            v.scalar_tensor_tensor(c1, z, th, c1, ALU.is_gt, ALU.add)
        v.scalar_tensor_tensor(c1, z, 5.0, c1, ALU.is_gt, ALU.add)
        v.scalar_tensor_tensor(c1, z, 5.0, c1, ALU.is_gt, ALU.add)
        c13 = qc1[0:rows, 0:width].rearrange("p (b k) -> p b k", k=16)
        v.tensor_tensor(c13, c13, qsc[0:rows, 0:nb, 0:1].to_broadcast([rows, nb, 16]),
                        ALU.mult)
        p.inc(v.tensor_mul(qout[0:rows, 0:width], qc1[0:rows, 0:width],
                           qc2[0:rows, 0:width]), "qqou", 1)



def build_launch_A(debug_comb=False):
    p = Prog()
    x_d = p.din("x", [D, TPC])
    id_d = p.din("ident_in", [128, 128])
    wqkv_d = p.din("wqkv", [D, 3 * D])
    rf_d = p.din("rf", [NH, DH, DH])
    rw_d = p.din("router_w", [D, 8])
    gw_d = p.din("gate_w", [8, D, D])
    uw_d = p.din("up_w", [8, D, D])
    dw_d = p.din("down_w", [8, D, D])
    sgw_d = p.din("sgw", [D, 2048])
    suw_d = p.din("suw", [D, 2048])
    sdw_d = p.din("sdw", [2048, D])
    qg_d = p.din("qg", [352, 1024])
    qu_d = p.din("qu", [352, 1024])
    qd_d = p.din("qd", [128, 2816])

    phiq_o = p.dout("phiq", [NH, DH, TPC], BF16)
    kvp_o = p.dout("kvp", [NH, DH, DH + 1])
    g0_o = p.dout("g0", [D, TPC])
    qgq_o = p.dout("qgq", [352, 1024], BF16)
    quq_o = p.dout("quq", [352, 1024], BF16)
    qdq_o = p.dout("qdq", [128, 2816], BF16)

    comb_dram = (p.dout if debug_comb else p.dtmp)("combdram", [8, TPC])

    xt = p.sb("xt", [128, DC, TT])
    xn = p.sb("xn", [128, DC, TPC], BF16)
    sq = p.sb("sq", [128, 1, TT])
    srow = p.sb("srow", [1, 2, TT])
    ones = p.sb("ones", [128, 2])
    tmp = p.sb("tmpa", [128, TT])
    wbuf = p.sb("wbuf", [128, 2, DC, 128], BF16)
    wbuf2 = p.sb("wbuf2", [128, 2, DC, 128], BF16)
    wbuf3 = p.sb("wbuf3", [128, 16, 128], BF16)
    rf_sb = p.sb("rf_sb", [128, NH, DH], BF16)
    ar1 = p.sb("ar1", [128, DC, TPC], BF16)
    ar2 = p.sb("ar2", [128, DC, TPC], BF16)
    vstage = p.sb("vstage", [128, 1, TT], BF16)
    vtm = p.sb("vtm", [128, 2, DH + 1], BF16)
    fkst = p.sb("fkst", [128, 2, DH], BF16)
    fqst = p.sb("fqst", [64, 1, TT], BF16)
    kvsb = p.sb("kvsb", [64, 2, DH + 1])
    logit = p.sb("logit", [128, NTC, 8])
    lsc = p.sb("lsc", [128, 8])
    comb_tm = p.sb("comb_tm", [128, NTC, 8])
    comb_st = p.sb("comb_st", [8, 128])
    comb_row = p.sb("comb_row", [1, 1, TT])
    g0acc = p.sb("g0acc", [128, DC, TPC])
    sg = p.sb("sg", [128, 1, TT])
    rwsb = p.sb("rwsb", [128, DC, 8], BF16)
    ident = p.sb("ident", [128, 128])
    identb = p.sb("identb", [128, 128], BF16)
    qout = None
    qin = sq[:, 0, :]
    qa = g0acc[:, 2, 0:TT]
    qc1 = tmp
    qc2 = g0acc[:, 0, 0:TT]
    qsc = g0acc[:, 1, 0:TT // 16].rearrange("p (x o) -> p x o", o=1)
    qrc = g0acc[:, 1, TT // 16:2 * (TT // 16)].rearrange("p (x o) -> p x o", o=1)
    qout = ar1[:, 0, 0:TT]

    P0 = p.ps("P0", [128, TT])
    P1 = p.ps("P1", [128, TT])
    P2 = p.ps("P2", [128, TT])
    P3 = p.ps("P3", [128, TT])
    P4 = p.ps("P4", [128, TT])
    P5 = p.ps("P5", [128, TT])
    PT = p.ps("PT", [128, 128], BF16)
    PTf = p.ps("PTf", [8, 128])

    blk = p.block()

    cst = p.sb("cst", [128, 2])

    @blk.vector
    def _(v):
        v.memset(ones[:, :], 1.0)
        v.memset(vtm[:, 0, DH:DH + 1], 1.0)
        v.memset(vtm[:, 1, DH:DH + 1], 1.0)
        v.memset(cst[:, 0:1], 1e-5)
        p.inc(v.memset(cst[:, 1:2], 1.0), "init", 1)

    @blk.scalar
    def _(s):
        p.wait(s, "init")

    @blk.tensor
    def _(t):
        p.wait(t, "init")

    @blk.sync
    def _(sy):
        p.inc(sy.dma_start(out=ident[:, :], in_=id_d[:, :]), "wlds", 16)

    @blk.gpsimd
    def _(g):
        p.inc(g.dma_start(out=rf_sb[0:64, :, :],
                          in_=rf_d.rearrange("h d r -> d h r")), "wld", 16)
        p.inc(g.dma_start(out=rf_sb[64:128, :, :],
                          in_=rf_d.rearrange("h d r -> d h r")), "wld", 16)
        p.inc(g.dma_start(out=rwsb[:, :, :],
                          in_=rw_d.rearrange("(c p) e -> p c e", p=128)), "wld", 16)
        p.inc(g.dma_start(out=identb[:, :], in_=id_d[:, :]), "wld", 16)

    for it in range(NTT):
        pre = f"ln{it}_"
        tsl = slice(it * TT, (it + 1) * TT)

        @blk.sync
        def _(sy, it=it):
            if it >= 1:
                p.wait(sy, "ln_xn")
            p.inc(sy.dma_start(
                out=xt[:, :, :],
                in_=x_d.rearrange("(c p) t -> p c t", p=128)[:, :, it * TT:(it + 1) * TT]),
                "ln_xld", 16)

        emit_ln_tile(p, blk, pre, xt, [xn[:, c, tsl] for c in range(DC)],
                     sq, srow, P0, P1, P2, P3, ones, tmp=tmp, eps_ap=cst[0:1, 0:1])

        @blk.tensor
        def _(t, it=it):
            p.wait(t, "ln_xn")
            p.wait(t, "wld")
            for tc in range(4):
                col0 = it * TT + tc * 128
                idx = it * 4 + tc
                if idx > 0:
                    t.wait_ge(p.sem("rtev"), idx)
                for c in range(DC):
                    mm = t.matmul(P4[:, 0:8], xn[:, c, col0:col0 + 128],
                                  rwsb[:, c, :], start=(c == 0), stop=(c == DC - 1))
                p.inc(mm, "rtmm", 1)

        @blk.vector
        def _(v, it=it):
            for tc in range(4):
                v.wait_ge(p.sem("rtmm"), it * 4 + tc + 1)
                p.inc(v.tensor_copy(logit[:, it * 4 + tc, :], P4[:, 0:8]), "rtev", 1)

    for tc in range(NTC):
        @blk.vector
        def _(v, tc=tc):
            if tc == 0:
                p.wait(v, "rtev")
            else:
                v.wait_ge(p.sem("rex"), tc)
            p.inc(v.reduce_max(lsc[:, 0:1], logit[:, tc, :], axis=AX.X, negate=True),
                  "rmx", 1)

        @blk.scalar
        def _(s, tc=tc):
            s.wait_ge(p.sem("rmx"), tc + 1)
            p.inc(s.activation(out=comb_tm[:, tc, :], in_=logit[:, tc, :], func=AF.Exp,
                               bias=lsc[:, 0:1], accum_out=lsc[:, 1:2]), "rex", 1)

        @blk.vector
        def _(v, tc=tc):
            v.wait_ge(p.sem("rex"), tc + 1)
            S = comb_tm[:, tc, :]
            M = logit[:, tc, :]
            p.vsf(v, v.reciprocal(lsc[:, 2:3], lsc[:, 1:2]))
            p.vsf(v, v.tensor_scalar_mul(S, S, lsc[:, 2:3]))
            p.vsf(v, v.reduce_max(lsc[:, 3:4], S, axis=AX.X, negate=True))
            p.vsf(v, v.tensor_scalar(M, S, lsc[:, 3:4], 0.0, ALU.add, ALU.is_ge))
            p.vsf(v, v.scalar_tensor_tensor(qa[:, 0:8], M, -1e30, S, ALU.mult, ALU.add))
            p.vsf(v, v.reduce_max(lsc[:, 4:5], qa[:, 0:8], axis=AX.X, negate=True))
            p.vsf(v, v.tensor_scalar(qc1[:, 0:8], qa[:, 0:8], lsc[:, 4:5], 0.0,
                                     ALU.add, ALU.is_ge))
            p.vsf(v, v.tensor_scalar_mul(lsc[:, 5:6], lsc[:, 3:4], -1.0))
            p.vsf(v, v.tensor_scalar_mul(lsc[:, 6:7], lsc[:, 4:5], -1.0))
            p.vsf(v, v.tensor_scalar_mul(M, M, lsc[:, 5:6]))
            p.vsf(v, v.tensor_scalar_mul(qc1[:, 0:8], qc1[:, 0:8], lsc[:, 6:7]))
            p.inc(v.tensor_add(S, M, qc1[:, 0:8]), "cmb", 1)

    @blk.tensor
    def _(t):
        p.wait(t, "cmb")
        p.wait(t, "wld")
        p.wait(t, "wlds")
        for tc in range(NTC):
            if tc > 0:
                t.wait_ge(p.sem("ctv"), tc)
            p.inc(t.transpose(PTf[0:8, 0:128], comb_tm[:, tc, :], ident[:, :]),
                  "ctm", 1)

    @blk.vector
    def _(v):
        for tc in range(NTC):
            v.wait_ge(p.sem("ctm"), tc + 1)
            if tc >= 1:
                v.wait_ge(p.sem("cbd"), 16 * tc)
            p.inc(v.tensor_copy(comb_st[0:8, :], PTf[0:8, 0:128]), "ctv", 1)

    @blk.sync
    def _(sy):
        for tc in range(NTC):
            sy.wait_ge(p.sem("ctv"), tc + 1)
            p.inc(sy.dma_start(out=comb_dram[:, tc * 128:(tc + 1) * 128],
                               in_=comb_st[0:8, :]), "cbd", 16)
        sy.wait_ge(p.sem("cbd"), p.cnt["cbd"])
        p.inc(sy.nop() if False else sy.dma_start(
            out=comb_row[0:1, 0:1, 0:1], in_=comb_dram[0:1, 0:1]), "cbl", 16)

    def qkv_w_slice(oc):
        return wqkv_d.rearrange("(c p) m -> p c m", p=128)[:, :, oc * 128:(oc + 1) * 128]

    @blk.gpsimd
    def _(g):
        for oc in range(16):
            if oc >= 2:
                g.wait_ge(p.sem("qev"), 4 * (oc - 1))
            p.inc(g.dma_start(out=wbuf[:, oc % 2, :, :], in_=qkv_w_slice(oc)),
                  "wq", 16)

    @blk.tensor
    def _(t):
        p.wait(t, "ln_xn")
        for oc in range(16):
            t.wait_ge(p.sem("wq"), 16 * (oc + 1))
            for it in range(NTT):
                gdx = 4 * oc + it
                if gdx >= 2:
                    t.wait_ge(p.sem("qev"), gdx - 1)
                ps = P0 if gdx % 2 == 0 else P1
                for c in range(DC):
                    mm = t.matmul(ps[:, :], wbuf[:, oc % 2, c, :],
                                  xn[:, c, it * TT:(it + 1) * TT],
                                  start=(c == 0), stop=(c == DC - 1))
                p.inc(mm, "qmm", 1)

    @blk.vector
    def _(v):
        for oc in range(16):
            dst = ar1 if oc < 8 else ar2
            for it in range(NTT):
                gdx = 4 * oc + it
                v.wait_ge(p.sem("qmm"), gdx + 1)
                ps = P0 if gdx % 2 == 0 else P1
                p.inc(v.tensor_copy(dst[:, oc % 8, it * TT:(it + 1) * TT], ps[:, :]),
                      "qev", 1)

    @blk.tensor
    def _(t):
        p.wait(t, "qev", 32)
        for h in range(NH):
            b0 = (h % 2) * 64
            for it in range(NTT):
                idx = h * NTT + it
                if idx > 0:
                    t.wait_ge(p.sem("fqev"), idx)
                p.inc(t.matmul(P2[0:64, :], rf_sb[b0:b0 + 64, h, :],
                               ar1[b0:b0 + 64, h // 2, it * TT:(it + 1) * TT],
                               start=True, stop=True), "fqmm", 1)

    @blk.vector
    def _(v):
        for h in range(NH):
            for it in range(NTT):
                idx = h * NTT + it
                v.wait_ge(p.sem("fqmm"), idx + 1)
                if idx >= 1:
                    v.wait_ge(p.sem("fqdm"), 16 * idx)
                p.inc(v.tensor_relu(fqst[:, 0, :], P2[0:64, :]), "fqev", 1)

    @blk.scalar
    def _(s):
        for h in range(NH):
            for it in range(NTT):
                idx = h * NTT + it
                s.wait_ge(p.sem("fqev"), idx + 1)
                p.inc(s.dma_start(out=phiq_o[h, :, it * TT:(it + 1) * TT],
                                  in_=fqst[:, 0, :]), "fqdm", 16)

    @blk.gpsimd
    def _(g):
        for vc in range(8):
            if vc == 0:
                g.wait_ge(p.sem("qev"), 60)
            elif vc == 1:
                g.wait_ge(p.sem("qev"), 64)
            else:
                g.wait_ge(p.sem("vev"), 4 * (vc - 1))
            p.inc(g.dma_start(out=wbuf[:, vc % 2, :, :], in_=qkv_w_slice(16 + vc)),
                  "wv", 16)

    for vc in range(8):
        h0 = 2 * vc
        for it in range(NTT):
            gdx = 4 * vc + it

            @blk.tensor
            def _(t, vc=vc, it=it, gdx=gdx):
                if it == 0:
                    t.wait_ge(p.sem("wv"), 16 * (vc + 1))
                    if vc > 0:
                        t.wait_ge(p.sem("kvev"), vc)
                if gdx >= 2:
                    t.wait_ge(p.sem("vev"), gdx - 1)
                ps = P0 if gdx % 2 == 0 else P1
                for c in range(DC):
                    mm = t.matmul(ps[:, :], wbuf[:, vc % 2, c, :],
                                  xn[:, c, it * TT:(it + 1) * TT],
                                  start=(c == 0), stop=(c == DC - 1))
                p.inc(mm, "vmm", 1)

            @blk.vector
            def _(v, gdx=gdx):
                v.wait_ge(p.sem("vmm"), gdx + 1)
                if gdx > 0:
                    v.wait_ge(p.sem("kvmm"), gdx * 8)
                ps = P0 if gdx % 2 == 0 else P1
                p.inc(v.tensor_copy(vstage[:, 0, :], ps[:, :]), "vev", 1)

            for tc in range(4):
                for j in range(2):
                    h = h0 + j
                    kb = (h % 2) * 64
                    step = ((gdx * 4 + tc) * 2 + j)

                    @blk.tensor
                    def _(t, gdx=gdx, tc=tc, j=j, step=step):
                        t.wait_ge(p.sem("vev"), gdx + 1)
                        if step >= 2:
                            t.wait_ge(p.sem("vtev"), step - 1)
                        p.inc(t.transpose(PT[:, j * 64:(j + 1) * 64],
                                          vstage[j * 64:(j + 1) * 64, 0,
                                                 tc * 128:(tc + 1) * 128],
                                          identb[j * 64:j * 64 + 64, j * 64:j * 64 + 64]),
                              "vtmm", 1)

                    @blk.vector
                    def _(v, j=j, step=step):
                        v.wait_ge(p.sem("vtmm"), step + 1)
                        if step >= 2:
                            v.wait_ge(p.sem("kvmm"), step - 1)
                        p.inc(v.tensor_copy(vtm[:, j, 0:DH], PT[:, j * 64:(j + 1) * 64]),
                              "vtev", 1)

                    @blk.tensor
                    def _(t, h=h, j=j, kb=kb, it=it, tc=tc, step=step):
                        t.wait_ge(p.sem("vtev"), step + 1)
                        if step >= 2:
                            t.wait_ge(p.sem("fkev"), step - 1)
                        p.inc(t.matmul(P3[:, j * 64:(j + 1) * 64],
                                       ar2[kb:kb + 64, h // 2,
                                           it * TT + tc * 128:it * TT + (tc + 1) * 128],
                                       rf_sb[kb:kb + 64, h, :], start=True, stop=True),
                              "fkmm", 1)

                    @blk.vector
                    def _(v, j=j, step=step):
                        v.wait_ge(p.sem("fkmm"), step + 1)
                        p.inc(v.tensor_relu(fkst[:, j, :], P3[:, j * 64:(j + 1) * 64]),
                              "fkev", 1)

                    @blk.tensor
                    def _(t, j=j, step=step, it=it, tc=tc, vc=vc):
                        t.wait_ge(p.sem("fkev"), step + 1)
                        if vc == 0 and it == 0 and tc == 0 and j == 1:
                            t.wait_ge(p.sem("fqev"), 64)
                        first = (it == 0 and tc == 0)
                        last = (it == NTT - 1 and tc == 3)
                        psKV = P5 if j == 0 else P2
                        p.inc(t.matmul(psKV[0:64, 0:65], fkst[:, j, :], vtm[:, j, :],
                                       start=first, stop=last), "kvmm", 1)

        @blk.vector
        def _(v, vc=vc, h0=h0):
            v.wait_ge(p.sem("kvmm"), (4 * vc + 4) * 8)
            if vc > 0:
                v.wait_ge(p.sem("kvdm"), 32 * vc)
            v.tensor_copy(kvsb[:, 0, :], P5[0:64, 0:65])
            p.inc(v.tensor_copy(kvsb[:, 1, :], P2[0:64, 0:65]), "kvev", 1)

        @blk.scalar
        def _(s, vc=vc, h0=h0):
            s.wait_ge(p.sem("kvev"), vc + 1)
            p.inc(s.dma_start(out=kvp_o[h0, :, :], in_=kvsb[:, 0, :]), "kvdm", 16)
            p.inc(s.dma_start(out=kvp_o[h0 + 1, :, :], in_=kvsb[:, 1, :]), "kvdm", 16)

    @blk.gpsimd
    def _(g):
        for e in range(8):
            if e == 0:
                g.wait_ge(p.sem("fkmm"), 256)
            else:
                g.wait_ge(p.sem("dmm"), e * 32)
            p.inc(g.dma_start(out=ar2[:, :, 0:1024],
                              in_=gw_d[e].rearrange("(c p) f -> p c f", p=128)),
                  "mwl", 16)
            p.inc(g.dma_start(out=ar2[:, :, 1024:2048],
                              in_=uw_d[e].rearrange("(c p) f -> p c f", p=128)),
                  "mwl", 16)
            g.wait_ge(p.sem("hmm"), (e + 1) * 32)
            p.inc(g.dma_start(out=ar2[:, :, 0:1024],
                              in_=dw_d[e].rearrange("(c p) f -> p c f", p=128)),
                  "mdl", 16)

    for e in range(8):
        hev_base = p.cnt.get("hev", 0)

        @blk.sync
        def _(sy, e=e):
            for it in range(NTT):
                gi = e * NTT + it
                if gi >= 1:
                    sy.wait_ge(p.sem("cbc"), gi)
                p.inc(sy.dma_start(out=comb_row[0:1, 0, :],
                                   in_=comb_dram[e:e + 1, it * TT:(it + 1) * TT]),
                      "crl", 16)

        @blk.tensor
        def _(t, e=e, hev_base=hev_base):
            t.wait_ge(p.sem("mwl"), 32 * (e + 1))
            if e == 0:
                p.wait(t, "cbl")
                p.wait(t, "qev")
                p.wait(t, "vev")
                p.wait(t, "kvev")
            for it in range(NTT):
                cur_base = e * 32 + it * 8
                if cur_base > 0:
                    t.wait_ge(p.sem("hev"), cur_base)
                gi = e * NTT + it
                t.wait_ge(p.sem("crl"), 16 * (gi + 1))
                p.inc(t.matmul(P5[:, 0:TT], ones[0:1, 0:1].to_broadcast([1, 128]),
                               comb_row[0:1, 0, :], start=True, stop=True), "cbc", 1)
                for fc in range(DC):
                    if fc >= 2:
                        t.wait_ge(p.sem("hev"), cur_base + fc - 1)
                    psG = P0 if fc % 2 == 0 else P2
                    psU = P1 if fc % 2 == 0 else P3
                    for c in range(DC):
                        t.matmul(psG[:, :], ar2[:, c, fc * 128:(fc + 1) * 128],
                                 xn[:, c, it * TT:(it + 1) * TT],
                                 start=(c == 0), stop=(c == DC - 1))
                    for c in range(DC):
                        mm = t.matmul(psU[:, :], ar2[:, c, 1024 + fc * 128:1024 + (fc + 1) * 128],
                                      xn[:, c, it * TT:(it + 1) * TT],
                                      start=(c == 0), stop=(c == DC - 1))
                    p.inc(mm, "hmm", 1)

        @blk.scalar
        def _(s, e=e):
            for it in range(NTT):
                for fc in range(DC):
                    idx = p.cnt.get("slu", 0)
                    s.wait_ge(p.sem("hmm"), e * 32 + it * 8 + fc + 1)
                    gidx = e * 32 + it * 8 + fc
                    if gidx >= 1:
                        s.wait_ge(p.sem("hev"), gidx)
                    psG = P0 if fc % 2 == 0 else P2
                    p.inc(s.activation(out=sg[:, 0, :], in_=psG[:, :], func=AF.Silu),
                          "slu", 1)

        @blk.vector
        def _(v, e=e):
            for it in range(NTT):
                v.wait_ge(p.sem("cbc"), e * 4 + it + 1)
                for fc in range(DC):
                    v.wait_ge(p.sem("slu"), e * 32 + it * 8 + fc + 1)
                    psU = P1 if fc % 2 == 0 else P3
                    v.tensor_mul(tmp[:, :], sg[:, 0, :], psU[:, :])
                    v.tensor_mul(tmp[:, :], tmp[:, :], P5[:, 0:TT])
                    p.inc(v.tensor_copy(ar1[:, fc, it * TT:(it + 1) * TT], tmp[:, :]),
                          "hev", 1)

        @blk.tensor
        def _(t, e=e):
            t.wait_ge(p.sem("mdl"), 16 * (e + 1))
            t.wait_ge(p.sem("hev"), (e + 1) * 32)
            if e == 0:
                p.wait(t, "rtev")
            for it in range(NTT):
                for oc in range(DC):
                    idx = e * 32 + it * 8 + oc
                    if idx > 0:
                        t.wait_ge(p.sem("dev"), idx)
                    for fc in range(DC):
                        mm = t.matmul(P4[:, :], ar2[:, fc, oc * 128:(oc + 1) * 128],
                                      ar1[:, fc, it * TT:(it + 1) * TT],
                                      start=(fc == 0), stop=(fc == DC - 1))
                    p.inc(mm, "dmm", 1)

        @blk.vector
        def _(v, e=e):
            for it in range(NTT):
                for oc in range(DC):
                    idx = e * 32 + it * 8 + oc
                    v.wait_ge(p.sem("dmm"), idx + 1)
                    if e == 0:
                        ev = v.tensor_copy(g0acc[:, oc, it * TT:(it + 1) * TT], P4[:, :])
                    else:
                        ev = v.tensor_add(g0acc[:, oc, it * TT:(it + 1) * TT],
                                          g0acc[:, oc, it * TT:(it + 1) * TT], P4[:, :])
                    p.inc(ev, "dev", 1)

    for it in range(NTT):
        @blk.gpsimd
        def _(g, it=it):
            if it == 0:
                g.wait_ge(p.sem("dmm"), 256)
            for fc in range(16):
                idx = it * 16 + fc
                if idx >= 2:
                    g.wait_ge(p.sem("shmm"), idx - 1)
                p.inc(g.dma_start(
                    out=wbuf[:, idx % 2, :, :],
                    in_=sgw_d.rearrange("(c p) f -> p c f", p=128)[:, :, fc * 128:(fc + 1) * 128]),
                    "swl", 16)
                p.inc(g.dma_start(
                    out=wbuf2[:, idx % 2, :, :],
                    in_=suw_d.rearrange("(c p) f -> p c f", p=128)[:, :, fc * 128:(fc + 1) * 128]),
                    "swl", 16)

        @blk.tensor
        def _(t, it=it):
            for fc in range(16):
                idx = it * 16 + fc
                t.wait_ge(p.sem("swl"), 32 * (idx + 1))
                if idx >= 2:
                    t.wait_ge(p.sem("shev"), idx - 1)
                psG = P0 if fc % 2 == 0 else P2
                psU = P1 if fc % 2 == 0 else P3
                for c in range(DC):
                    t.matmul(psG[:, :], wbuf[:, idx % 2, c, :],
                             xn[:, c, it * TT:(it + 1) * TT],
                             start=(c == 0), stop=(c == DC - 1))
                for c in range(DC):
                    mm = t.matmul(psU[:, :], wbuf2[:, idx % 2, c, :],
                                  xn[:, c, it * TT:(it + 1) * TT],
                                  start=(c == 0), stop=(c == DC - 1))
                p.inc(mm, "shmm", 1)

        @blk.scalar
        def _(s, it=it):
            for fc in range(16):
                idx = it * 16 + fc
                s.wait_ge(p.sem("shmm"), idx + 1)
                if idx >= 1:
                    s.wait_ge(p.sem("shev"), idx)
                psG = P0 if fc % 2 == 0 else P2
                p.inc(s.activation(out=sg[:, 0, :], in_=psG[:, :], func=AF.Silu),
                      "shsl", 1)

        @blk.vector
        def _(v, it=it):
            for fc in range(16):
                idx = it * 16 + fc
                v.wait_ge(p.sem("shsl"), idx + 1)
                if it >= 1 and fc == 0:
                    v.wait_ge(p.sem("sdmm"), it * 8)
                psU = P1 if fc % 2 == 0 else P3
                v.tensor_mul(tmp[:, :], sg[:, 0, :], psU[:, :])
                p.inc(v.tensor_copy(ar1.rearrange("p c t -> p (c t)")
                                    [:, fc * TT:(fc + 1) * TT], tmp[:, :]), "shev", 1)

        @blk.gpsimd
        def _(g, it=it):
            g.wait_ge(p.sem("shev"), it * 16 + 16)
            for oc in range(DC):
                idx2 = it * 8 + oc
                if idx2 > 0:
                    g.wait_ge(p.sem("sdmm"), idx2)
                p.inc(g.dma_start(
                    out=wbuf3[:, :, :],
                    in_=sdw_d.rearrange("(k p) m -> p k m", p=128)[:, :, oc * 128:(oc + 1) * 128]),
                    "sdl", 16)

        @blk.tensor
        def _(t, it=it):
            hsv = ar1.rearrange("p c t -> p (c t)")
            for oc in range(DC):
                idx2 = it * 8 + oc
                t.wait_ge(p.sem("sdl"), 16 * (idx2 + 1))
                if idx2 == 0:
                    p.wait(t, "dev")
                else:
                    t.wait_ge(p.sem("sdev"), idx2)
                for kc in range(16):
                    mm = t.matmul(P4[:, :], wbuf3[:, kc, :], hsv[:, kc * TT:(kc + 1) * TT],
                                  start=(kc == 0), stop=(kc == 15))
                p.inc(mm, "sdmm", 1)

        @blk.vector
        def _(v, it=it):
            for oc in range(DC):
                idx2 = it * 8 + oc
                v.wait_ge(p.sem("sdmm"), idx2 + 1)
                p.inc(v.tensor_add(g0acc[:, oc, it * TT:(it + 1) * TT],
                                   g0acc[:, oc, it * TT:(it + 1) * TT], P4[:, :]),
                      "sdev", 1)

    @blk.scalar
    def _(s):
        p.wait(s, "sdev")
        for c in range(DC):
            ac = s.activation(out=g0acc[:, c, :], in_=g0acc[:, c, :], func=AF.Silu)
        p.inc(ac, "g0s", 1)

    @blk.sync
    def _(sy):
        p.wait(sy, "g0s")
        for c in range(DC):
            p.inc(sy.dma_start(out=g0_o[c * 128:(c + 1) * 128, :], in_=g0acc[:, c, :]),
                  "g0d", 16)

    qjobs = []
    for (src, dst, rows_total, width) in ((qg_d, qgq_o, 352, 1024),
                                          (qu_d, quq_o, 352, 1024),
                                          (qd_d, qdq_o, 128, 2816)):
        r0 = 0
        while r0 < rows_total:
            rr = min(128, rows_total - r0)
            c0 = 0
            while c0 < width:
                cc = min(TT, width - c0)
                qjobs.append((src, dst, r0, rr, c0, cc))
                c0 += cc
            r0 += rr

    for qi, (srcd, dst, r0, rr, c0, cc) in enumerate(qjobs):

        @blk.sync
        def _(sy, srcd=srcd, r0=r0, rr=rr, c0=c0, cc=cc, qi=qi):
            if qi == 0:
                p.wait(sy, "g0d")
            else:
                p.wait(sy, "qdm")
            p.inc(sy.dma_start(out=qin[0:rr, 0:cc], in_=srcd[r0:r0 + rr, c0:c0 + cc]),
                  "qqin", 16)

        emit_fp4_qdq(p, blk, qin[0:rr, 0:cc], qa, qsc, qrc, qc1, qc2, qout, rr, cc)

        @blk.sync
        def _(sy, dst=dst, r0=r0, rr=rr, c0=c0, cc=cc):
            p.wait(sy, "qqou")
            p.inc(sy.dma_start(out=dst[r0:r0 + rr, c0:c0 + cc], in_=qout[0:rr, 0:cc]),
                  "qdm", 16)

    @blk.sync
    def _(sy):
        p.wait(sy, "g0d")
        p.wait(sy, "kvdm")
        p.wait(sy, "fqdm")
        p.wait(sy, "qdm")

    return p


def kernel(**inputs):
    z = inputs
    BF = ml_dtypes.bfloat16

    def b128(b):
        return np.ascontiguousarray(np.asarray(b, np.float32).reshape(DC, 128).T)

    x_full = np.asarray(z["L0_attn_x"], np.float32).reshape(16384, 1024)

    pA = build_launch_A()
    inA = [dict(
        x=host_fm(x_full[c * 2048:(c + 1) * 2048]), ident_in=IDENT128,
        wqkv=z["L0_attn_W_qkv"], rf=z["L0_attn_RF"], router_w=z["L0_ffn_router_w"],
        gate_w=z["L0_ffn_gate_w"], up_w=z["L0_ffn_up_w"], down_w=z["L0_ffn_down_w"],
        sgw=z["L0_ffn_shared_gate_w"], suw=z["L0_ffn_shared_up_w"],
        sdw=z["L0_ffn_shared_down_w"],
        qg=np.ascontiguousarray(z["L1_ffn_w_gate"][c * 352:(c + 1) * 352]),
        qu=np.ascontiguousarray(z["L1_ffn_w_up"][c * 352:(c + 1) * 352]),
        qd=np.ascontiguousarray(z["L1_ffn_w_down"][c * 128:(c + 1) * 128]))
        for c in range(8)]
    rA = _run(pA, inA)

    kvp_batch = [np.stack([rA[b * 4 + i]["kvp"] for i in range(4)]).astype(np.float32)
                 for b in range(2)]
    theta = (10000.0 ** (-np.arange(32, dtype=np.float32) / 32.0)).astype(np.float32)

    pB = build_launch_B()
    inB = []
    for c in range(8):
        fr = (np.arange(2048, dtype=np.float32) + (c % 4) * 2048)[:, None] * theta[None, :]
        inB.append(dict(
            x=host_fm(x_full[c * 2048:(c + 1) * 2048]), g0=rA[c]["g0"],
            phiq=rA[c]["phiq"], kvp4=kvp_batch[c // 4],
            wout=z["L0_attn_W_out"], bout=b128(z["L0_attn_b_out"]),
            wqT=np.ascontiguousarray(z["L1_attn_Wq"].T),
            wkT=np.ascontiguousarray(z["L1_attn_Wk"].T),
            wvT=np.ascontiguousarray(z["L1_attn_Wv"].T),
            bq=b128(z["L1_attn_bq"]), bk=b128(z["L1_attn_bk"]),
            bv=np.asarray(z["L1_attn_bv"])[None, :],
            cosT=np.ascontiguousarray(np.tile(np.cos(fr).T, (4, 1))),
            sinT=np.ascontiguousarray(np.tile(np.sin(fr).T, (4, 1)))))
    rB = _run(pB, inB)

    wgq = np.concatenate([rA[c]["qgq"] for c in range(8)], 0)
    wuq = np.concatenate([rA[c]["quq"] for c in range(8)], 0)
    wdq = np.concatenate([rA[c]["qdq"] for c in range(8)], 0)
    kfull = [np.concatenate([rB[b * 4 + i]["k1"].reshape(16, 64, 2048)
                             for i in range(4)], axis=2) for b in range(2)]
    vfull = [np.concatenate([rB[b * 4 + i]["v1"].reshape(2048, 16, 64).transpose(1, 0, 2)
                             for i in range(4)], axis=1) for b in range(2)]
    kn2f = [np.concatenate([rB[b * 4 + i]["kn2"] for i in range(4)], axis=1)
            for b in range(2)]

    pC = build_launch_C()
    inC = [dict(
        x1=rB[c]["x1"], q1=np.ascontiguousarray(rB[c]["q1"].reshape(16, 64, 2048)),
        kfull=np.ascontiguousarray(kfull[c // 4]),
        vfull=np.ascontiguousarray(vfull[c // 4]),
        qn2=rB[c]["qn2"], kn2f=np.ascontiguousarray(kn2f[c // 4]),
        woT=np.ascontiguousarray(z["L1_attn_Wo"].T), bo=b128(z["L1_attn_bo"]),
        wgqT=np.ascontiguousarray(wgq.T), wuqT=np.ascontiguousarray(wuq.T),
        wdqT=np.ascontiguousarray(wdq.T), ident_in=IDENT128,
        ln_g=b128(z["L2_attn_ln_g"]), ln_b=b128(z["L2_attn_ln_b"]),
        w_in=z["L2_attn_w_in"], b_in=b128(z["L2_attn_b_in"])) for c in range(8)]
    rC = _run(pC, inC)

    ll = [np.concatenate([rC[b * 4 + i]["left_lin"] for i in range(4)], axis=1)
          for b in range(2)]

    pD = build_launch_D()
    cw = z["L2_attn_conv_w"][:, 0, :]
    cb = z["L2_attn_conv_b"]
    inD = []
    for c in range(8):
        b, s = c // 4, c % 4
        inD.append(dict(
            ll=np.ascontiguousarray(ll[b][s * 256:(s + 1) * 256]),
            conv_w=np.ascontiguousarray(
                cw[s * 256:(s + 1) * 256].reshape(2, 128, 4).transpose(1, 0, 2)),
            conv_b=np.ascontiguousarray(cb[s * 256:(s + 1) * 256].reshape(2, 128).T)))
    rD = _run(pD, inD)
    lc_dev = [np.concatenate([rD[b * 4 + i]["lc"] for i in range(4)], axis=0)
              for b in range(2)]

    pE = build_launch_E()
    Aval = -np.exp(np.asarray(z["L2_attn_A_log"], np.float64)).astype(np.float32)
    inE = []
    for c in range(8):
        b, s = c // 4, c % 4
        inE.append(dict(
            lcf=np.ascontiguousarray(lc_dev[b]),
            own=np.ascontiguousarray(lc_dev[b][s * 256:(s + 1) * 256]),
            wdt=np.ascontiguousarray(z["L2_attn_w_dt"][:, s * 256:(s + 1) * 256]),
            bdt=np.ascontiguousarray(
                z["L2_attn_b_dt"][s * 256:(s + 1) * 256].reshape(2, 128).T),
            wbc=z["L2_attn_w_bc"],
            A=np.ascontiguousarray(
                Aval[s * 256:(s + 1) * 256].reshape(2, 128, 16).transpose(1, 0, 2)),
            Dssm=np.ascontiguousarray(
                z["L2_attn_D_ssm"][s * 256:(s + 1) * 256].reshape(2, 128).T)))
    rE = _run(pE, inE)
    y_dev = [np.concatenate([rE[b * 4 + i]["yssm"] for i in range(4)], axis=0)
             for b in range(2)]

    perm = np.concatenate([np.arange(0, 1024, 2), np.arange(1, 1024, 2)])
    i_ = np.arange(512, dtype=np.float32)
    theta3 = (10000.0 ** (-2.0 * (i_ - 1.0) / 1024.0)).astype(np.float32)
    wqi, wki, wvi = np.split(z["L3_attn_in_proj_w"], 3, axis=0)
    bqi, bki, bvi = np.split(z["L3_attn_in_proj_b"], 3, axis=0)
    pF = build_launch_F()
    inF = []
    for c in range(8):
        b, s = c // 4, c % 4
        pos = np.arange(2048, dtype=np.float32) + s * 2048
        ang = pos[:, None] * theta3[None, :]
        inF.append(dict(
            x2=rC[c]["x2"], yssm=np.ascontiguousarray(
                y_dev[b][:, s * 2048:(s + 1) * 2048]),
            gate_wT=np.ascontiguousarray(z["L2_ffn_gate_w"].T),
            e_gate=z["L2_ffn_w_gate"], e_up=z["L2_ffn_w_up"],
            e_down=z["L2_ffn_w_down"],
            sw_gate=z["L2_ffn_sw_gate"], sw_up=z["L2_ffn_sw_up"],
            sw_down=z["L2_ffn_sw_down"],
            w_out=z["L2_attn_w_out"], b_out=b128(z["L2_attn_b_out"]),
            ln_g=b128(z["L2_attn_ln_g"]), ln_b=b128(z["L2_attn_ln_b"]),
            wq3T=np.ascontiguousarray(z["L3_attn_wq"].T[:, perm]),
            wk3T=np.ascontiguousarray(z["L3_attn_wk"].T[:, perm]),
            wv3T=np.ascontiguousarray(z["L3_attn_wv"].T[:, perm]),
            wqiT=np.ascontiguousarray(wqi.T[perm, :]),
            wkiT=np.ascontiguousarray(wki.T[perm, :]),
            wviT=np.ascontiguousarray(wvi.T[perm, :]),
            bqi=b128(bqi), bki=b128(bki),
            bvi=np.ascontiguousarray(bvi[None, :]),
            cos3=np.ascontiguousarray(np.tile(np.cos(ang).T.astype(np.float32), (2, 1))),
            sin3=np.ascontiguousarray(np.tile(np.sin(ang).T.astype(np.float32), (2, 1))),
            ident_in=IDENT128))
    rF = _run(pF, inF)

    k3f = [np.ascontiguousarray(
        np.concatenate([rF[b * 4 + i]["k3h"] for i in range(4)], axis=1)
        .reshape(16, 64, 8192)) for b in range(2)]
    v3f = [np.ascontiguousarray(
        np.concatenate([rF[b * 4 + i]["v3h"] for i in range(4)], axis=0)
        .reshape(8192, 16, 64).transpose(1, 0, 2)) for b in range(2)]
    kn3f = [np.concatenate([rF[b * 4 + i]["kn3"] for i in range(4)], axis=1)
            for b in range(2)]
    NCK = T // 128
    gmask = []
    for r in range(4):
        qoff = r * 2048
        m = np.zeros((4, NCK, 128, 512), np.float32)
        for tqb in range(4):
            for ck in range(NCK):
                k0 = ck * 128
                q0 = qoff + tqb * 512
                if k0 + 127 <= q0:
                    continue
                if k0 > q0 + 511:
                    m[tqb, ck] = -1e30
                else:
                    ki = k0 + np.arange(128)[:, None]
                    qi = q0 + np.arange(512)[None, :]
                    m[tqb, ck] = np.where(ki <= qi, 0.0, -1e30)
        gmask.append(np.ascontiguousarray(m.astype(BF)))
    pG = build_launch_G()
    inG = []
    for c in range(8):
        b, s = c // 4, c % 4
        kn = np.array(kn3f[b], np.float32)
        kn[:, (s + 1) * 2048:] = 0.0
        inG.append(dict(
            q3=np.ascontiguousarray(rF[c]["q3h"].reshape(16, 64, 2048)),
            k3f=k3f[b], v3f=v3f[b], qn3=rF[c]["qn3"],
            kn3f=np.ascontiguousarray(kn),
            woT=np.ascontiguousarray(z["L3_attn_out_w"].T.astype(np.float32)),
            bo=b128(z["L3_attn_out_b"]), mask=gmask[s]))
    rG = _run(pG, inG)

    hconsts = host_h_consts()
    hmasks = [host_h_masks(r) for r in range(4)]
    xcur = [np.concatenate([rF[b * 4 + i]["xn3"] for i in range(4)], axis=1)
            for b in range(2)]
    x4 = [None] * 8
    for li in range(6):
        wts = host_h_weights(z["L3_ffn_Wqkv"][li], z["L3_ffn_Wo"][li],
                             z["L3_ffn_W1"][li], z["L3_ffn_b1"][li],
                             z["L3_ffn_W2"][li], z["L3_ffn_b2"][li],
                             z["L3_ffn_ln_g"][li], z["L3_ffn_ln_b"][li])
        tabs = [host_h_tables(r * 2048, z["L3_ffn_q_scale"][li],
                              z["L3_ffn_k_scale"][li]) for r in range(4)]
        final = (li == 5)
        pH = build_launch_H(final=final)
        xpad = [np.concatenate([np.zeros((1024, 512), np.float32), xcur[b]], axis=1)
                for b in range(2)]
        inH = []
        for c in range(8):
            b, s = c // 4, c % 4
            d = dict(xin=np.ascontiguousarray(xpad[b][:, s * 2048:s * 2048 + 2560]),
                     cs4=tabs[s], mskh=hmasks[s], **wts, **hconsts)
            if final:
                d["x3f"] = rF[c]["x3"]
                d["s3f"] = rG[c]["s3"]
            inH.append(d)
        rH = _run(pH, inH)
        if final:
            x4 = [rH[c]["x4"] for c in range(8)]
        else:
            xcur = [np.concatenate([rH[b * 4 + i]["xout"][:, 512:2560]
                                    for i in range(4)], axis=1) for b in range(2)]

    out = np.empty((2, 8192, 1024), np.float32)
    for c in range(8):
        b, s = c // 4, c % 4
        out[b, s * 2048:(s + 1) * 2048, :] = x4[c].T
    return out



def build_launch_B():
    p = Prog()
    x_d = p.din("x", [D, TPC])
    g0_d = p.din("g0", [D, TPC])
    phiq_d = p.din("phiq", [NH, DH, TPC], BF16)
    kvp_d = p.din("kvp4", [4, NH, DH, DH + 1])
    wout_d = p.din("wout", [D, D])
    bout_d = p.din("bout", [128, DC])
    wq_d = p.din("wqT", [D, D])
    wk_d = p.din("wkT", [D, D])
    wv_d = p.din("wvT", [D, D])
    bq_d = p.din("bq", [128, DC])
    bk_d = p.din("bk", [128, DC])
    bv_d = p.din("bv", [1, D])
    cos_d = p.din("cosT", [128, TPC])
    sin_d = p.din("sinT", [128, TPC])

    x1_o = p.dout("x1", [D, TPC])
    q1_o = p.dout("q1", [D, TPC], BF16)
    k1_o = p.dout("k1", [D, TPC], BF16)
    v1_o = p.dout("v1", [TPC, D], BF16)
    qn2_o = p.dout("qn2", [NH, TPC])
    kn2_o = p.dout("kn2", [NH, TPC])

    xt = p.sb("xt", [128, DC, TT])
    gt = p.sb("gt", [128, DC, TT])
    xn = p.sb("xn", [128, DC, TPC], BF16)
    sq = p.sb("sq", [128, 1, TT])
    srow = p.sb("srow", [1, 2, TT])
    ones = p.sb("ones", [128, 2])
    tmp = p.sb("tmpa", [128, TT])
    cst = p.sb("cst", [128, 2])
    kvt = p.sb("kvt", [64, NH, DH + 1])
    kvb = p.sb("kvb", [64, NH, DH + 1], BF16)
    fqs = p.sb("fqs", [64, 2, TT], BF16)
    ofm = p.sb("ofm", [128, DC, TPC], BF16)
    nrow = p.sb("nrow", [1, TT])
    wo_sb = p.sb("wo_sb", [128, DC, D], BF16)
    wq_sb = p.sb("wq_sb", [128, DC, D], BF16)
    wk_sb = p.sb("wk_sb", [128, DC, D], BF16)
    wv_sb = p.sb("wv_sb", [128, DC, D], BF16)
    bias = p.sb("bias", [128, 3, DC])
    bvb = p.sb("bvb", [128, DC, 128])
    bvrow = p.sb("bvrow", [1, D])
    s0t = p.sb("s0t", [128, TT])
    rps = p.sb("rps", [128, TT])
    rpb = p.sb("rpb", [128, TT], BF16)
    cs_t = p.sb("cs_t", [128, 2, TT])
    eons = p.sb("eons", [128, 2])
    n2r = p.sb("n2r", [2, TT])
    vtile = rpb

    P0 = p.ps("P0", [128, TT])
    P1 = p.ps("P1", [128, TT])
    P2 = p.ps("P2", [128, TT])
    P3 = p.ps("P3", [128, TT])
    P4 = p.ps("P4", [128, TT])
    P5 = p.ps("P5", [128, TT])

    blk = p.block()

    @blk.vector
    def _(v):
        v.memset(ones[:, :], 1.0)
        v.memset(cst[:, 0:1], 1e-5)
        v.memset(cst[:, 1:2], 1e-6)
        v.memset(eons[:, :], 0.0)
        v.memset(eons[0:64, 0:1], 1.0)
        p.inc(v.memset(eons[64:128, 1:2], 1.0), "init", 1)

    @blk.scalar
    def _(s):
        p.wait(s, "init")

    @blk.tensor
    def _(t):
        p.wait(t, "init")

    @blk.gpsimd
    def _(g):
        for i in range(4):
            if i > 0:
                g.wait_ge(p.sem("kvl"), 16 * i)
            p.inc(g.dma_start(out=kvt[:, :, :],
                              in_=kvp_d[i].rearrange("h r c -> r h c"),
                              accum_op=(ALU.add if i > 0 else ALU.bypass)), "kvl", 16)
        p.inc(g.dma_start(out=bias[:, 0, :], in_=bout_d[:, :]), "bls", 16)
        p.inc(g.dma_start(out=bias[:, 1, :], in_=bq_d[:, :]), "bls", 16)
        p.inc(g.dma_start(out=bias[:, 2, :], in_=bk_d[:, :]), "bls", 16)
        p.inc(g.dma_start(out=bvrow[0:1, :], in_=bv_d[:, :]), "bls", 16)
        p.inc(g.dma_start(out=wo_sb[:, :, :],
                          in_=wout_d.rearrange("(c p) m -> p c m", p=128)), "wls", 16)
        p.inc(g.dma_start(out=wq_sb[:, :, :],
                          in_=wq_d.rearrange("(c p) m -> p c m", p=128)), "wls", 16)
        p.inc(g.dma_start(out=wk_sb[:, :, :],
                          in_=wk_d.rearrange("(c p) m -> p c m", p=128)), "wls", 16)
        p.inc(g.dma_start(out=wv_sb[:, :, :],
                          in_=wv_d.rearrange("(c p) m -> p c m", p=128)), "wls", 16)

    @blk.vector
    def _(v):
        p.wait(v, "kvl")
        p.inc(v.tensor_copy(kvb[:, :, :], kvt[:, :, :]), "kvc", 1)

    @blk.tensor
    def _(t):
        p.wait(t, "bls")
        for half in range(2):
            if half == 1:
                p.wait(t, "bvc", 1)
            p.inc(t.matmul(P5[:, :], ones[0:1, 0:1].to_broadcast([1, 128]),
                           bvrow[0:1, half * TT:(half + 1) * TT], start=True, stop=True),
                  "bvm", 1)

    @blk.vector
    def _(v):
        for half in range(2):
            v.wait_ge(p.sem("bvm"), half + 1)
            p.inc(v.tensor_copy(bvb.rearrange("p c f -> p (c f)")[:, half * TT:(half + 1) * TT],
                                P5[:, :]), "bvc", 1)

    for h in range(NH):
        for it in range(NTT):
            idx = h * NTT + it
            tsl = slice(it * TT, (it + 1) * TT)

            @blk.sync
            def _(sy, h=h, idx=idx, tsl=tsl):
                if idx >= 2:
                    p.wait(sy, "po_mm", idx - 1)
                p.inc(sy.dma_start(out=fqs[:, idx % 2, :], in_=phiq_d[h, :, tsl]),
                      "fql", 16)

            @blk.tensor
            def _(t, h=h, idx=idx):
                t.wait_ge(p.sem("fql"), 16 * (idx + 1))
                if idx == 0:
                    p.wait(t, "kvc")
                    p.wait(t, "bvc")
                if idx >= 2:
                    p.wait(t, "po_ev", idx - 1)
                ps = P0 if idx % 2 == 0 else P1
                p.inc(t.matmul(ps[0:65, :], kvb[:, h, :], fqs[:, idx % 2, :],
                               start=True, stop=True), "po_mm", 1)

            @blk.vector
            def _(v, idx=idx):
                v.wait_ge(p.sem("po_mm"), idx + 1)
                if idx >= 1:
                    p.wait(v, "po_bc", idx)
                ps = P0 if idx % 2 == 0 else P1
                p.vsf(v, v.tensor_scalar_add(nrow[0:1, :], ps[64:65, :], cst[0:1, 1:2]))
                p.vsf(v, v.reciprocal(nrow[0:1, :], nrow[0:1, :]))
                p.inc(v.tensor_copy(s0t[0:64, :], ps[0:64, :]), "po_cp", 1)

            @blk.tensor
            def _(t, idx=idx):
                t.wait_ge(p.sem("po_cp"), idx + 1)
                if idx >= 1:
                    p.wait(t, "po_ev", idx)
                p.inc(t.matmul(P2[0:64, :], ones[0:1, 0:1].to_broadcast([1, 64]),
                               nrow[0:1, :], start=True, stop=True), "po_bc", 1)

            @blk.vector
            def _(v, h=h, idx=idx, tsl=tsl):
                v.wait_ge(p.sem("po_bc"), idx + 1)
                p.inc(v.tensor_mul(ofm[(h % 2) * 64:(h % 2) * 64 + 64, h // 2, tsl],
                                   s0t[0:64, :], P2[0:64, :]), "po_ev", 1)

    for it in range(NTT):
        tsl = slice(it * TT, (it + 1) * TT)

        @blk.sync
        def _(sy, it=it, tsl=tsl):
            if it >= 1:
                p.wait(sy, "x1w", 16 * it)
                p.wait(sy, "ln_xn", it)
            p.inc(sy.dma_start(out=xt[:, :, :],
                               in_=x_d.rearrange("(c p) t -> p c t", p=128)[:, :, tsl]),
                  "ln_xld", 16)
            p.inc(sy.dma_start(out=gt[:, :, :],
                               in_=g0_d.rearrange("(c p) t -> p c t", p=128)[:, :, tsl]),
                  "g0l", 16)

        for oc in range(DC):
            idx = it * DC + oc

            @blk.tensor
            def _(t, it=it, oc=oc, idx=idx, tsl=tsl):
                if idx == 0:
                    p.wait(t, "wls")
                    p.wait(t, "po_ev")
                if idx >= 2:
                    t.wait_ge(p.sem("s0ev"), idx - 1)
                ps = P0 if idx % 2 == 0 else P1
                for c in range(DC):
                    mm = t.matmul(ps[:, :], wo_sb[:, c, oc * 128:(oc + 1) * 128],
                                  ofm[:, c, tsl], start=(c == 0), stop=(c == DC - 1))
                p.inc(mm, "s0mm", 1)

            @blk.vector
            def _(v, it=it, oc=oc, idx=idx):
                v.wait_ge(p.sem("s0mm"), idx + 1)
                v.wait_ge(p.sem("ln_xld"), 16 * (it + 1))
                v.wait_ge(p.sem("g0l"), 16 * (it + 1))
                ps = P0 if idx % 2 == 0 else P1
                p.inc(v.tensor_scalar(s0t[:, :], ps[:, :], bias[:, 0, oc:oc + 1],
                                      None, ALU.add), "s0ev", 1)
                v.tensor_mul(s0t[:, :], s0t[:, :], gt[:, oc, :])
                p.inc(v.tensor_add(xt[:, oc, :], xt[:, oc, :], s0t[:, :]), "x1c", 1)

        @blk.sync
        def _(sy, it=it, tsl=tsl):
            p.wait(sy, "x1c")
            p.inc(sy.dma_start(out=x1_o.rearrange("(c p) t -> p c t", p=128)[:, :, tsl],
                               in_=xt[:, :, :]), "x1w", 16)

        emit_ln_tile(p, blk, f"b{it}", xt, [xn[:, c, tsl] for c in range(DC)],
                     sq, srow, P2, P3, P4, P5, ones, tmp=tmp, eps_ap=cst[0:1, 0:1])

    for which, w_sb, bcol, out_d, n2_d in (("q", wq_sb, 1, q1_o, qn2_o),
                                           ("k", wk_sb, 2, k1_o, kn2_o)):
        for oc in range(DC):
            for it in range(NTT):
                idx = oc * NTT + it
                tsl = slice(it * TT, (it + 1) * TT)

                @blk.sync
                def _(sy, idx=idx, tsl=tsl, which=which):
                    if which == "k" and idx == 0:
                        p.wait(sy, "qev", 32)
                    if idx >= 1:
                        p.wait(sy, which + "ev", idx)
                    p.inc(sy.dma_start(out=cs_t[:, 0, :], in_=cos_d[:, tsl]), which + "cs", 16)
                    p.inc(sy.dma_start(out=cs_t[:, 1, :], in_=sin_d[:, tsl]), which + "cs", 16)

                @blk.tensor
                def _(t, oc=oc, idx=idx, tsl=tsl, which=which, w_sb=w_sb):
                    if which == "q" and idx == 0:
                        p.wait(t, "ln_xn")
                    if idx >= 2:
                        p.wait(t, which + "pe", idx - 1)
                    ps = P0 if idx % 2 == 0 else P1
                    for c in range(DC):
                        mm = t.matmul(ps[:, :], w_sb[:, c, oc * 128:(oc + 1) * 128],
                                      xn[:, c, tsl], start=(c == 0), stop=(c == DC - 1))
                    p.inc(mm, which + "mm", 1)

                @blk.vector
                def _(v, oc=oc, idx=idx, which=which, bcol=bcol):
                    v.wait_ge(p.sem(which + "mm"), idx + 1)
                    v.wait_ge(p.sem(which + "cs"), 32 * (idx + 1))
                    if which == "k" and idx == 0:
                        p.wait(v, "qsq", 32)
                        p.wait(v, "qqw", 16 * 32)
                    if idx >= 1:
                        p.wait(v, which + "sq", idx)
                        p.wait(v, which + "qw", 16 * idx)
                    ps = P0 if idx % 2 == 0 else P1
                    p.inc(v.tensor_scalar(rps[:, :], ps[:, :], bias[:, bcol, oc:oc + 1],
                                          None, ALU.add), which + "pe", 1)
                    for j in range(2):
                        lo = slice(64 * j, 64 * j + 32)
                        hi = slice(64 * j + 32, 64 * j + 64)
                        v.tensor_mul(tmp[lo, :], rps[lo, :], cs_t[lo, 0, :])
                        v.tensor_mul(s0t[lo, :], rps[hi, :], cs_t[hi, 1, :])
                        v.tensor_mul(tmp[hi, :], rps[hi, :], cs_t[hi, 0, :])
                        v.tensor_mul(s0t[hi, :], rps[lo, :], cs_t[lo, 1, :])
                        v.tensor_sub(rps[lo, :], tmp[lo, :], s0t[lo, :])
                        v.tensor_add(rps[hi, :], tmp[hi, :], s0t[hi, :])
                    p.inc(v.tensor_copy(rpb[:, :], rps[:, :]), which + "ev", 1)

                @blk.scalar
                def _(s, idx=idx, which=which):
                    s.wait_ge(p.sem(which + "ev"), idx + 1)
                    p.inc(s.activation(out=sq[:, 0, :], in_=rps[:, :], func=AF.Square),
                          which + "sq", 1)

                @blk.tensor
                def _(t, idx=idx, which=which):
                    t.wait_ge(p.sem(which + "sq"), idx + 1)
                    if which == "k" and idx == 0:
                        p.wait(t, "qnc", 32)
                    if idx >= 1:
                        p.wait(t, which + "nc", idx)
                    p.inc(t.matmul(P2[0:2, :], eons[:, :], sq[:, 0, :],
                                   start=True, stop=True), which + "n2", 1)

                @blk.vector
                def _(v, idx=idx, which=which):
                    v.wait_ge(p.sem(which + "n2"), idx + 1)
                    if idx >= 1:
                        p.wait(v, which + "nw", 16 * idx)
                    p.inc(v.tensor_copy(n2r[:, :], P2[0:2, :]), which + "nc", 1)

                @blk.scalar
                def _(s, oc=oc, idx=idx, tsl=tsl, which=which, out_d=out_d, n2_d=n2_d):
                    s.wait_ge(p.sem(which + "nc"), idx + 1)
                    p.inc(s.dma_start(
                        out=n2_d.rearrange("(o j) t -> j o t", j=2)[:, oc, tsl],
                        in_=n2r[:, :]), which + "nw", 16)
                    s.wait_ge(p.sem(which + "ev"), idx + 1)
                    p.inc(s.dma_start(
                        out=out_d.rearrange("(c p) t -> p c t", p=128)[:, oc, tsl],
                        in_=rpb[:, :]), which + "qw", 16)

    for tc in range(NTC):
        for half in range(2):
            idx = tc * 2 + half

            @blk.tensor
            def _(t, tc=tc, half=half, idx=idx):
                if idx == 0:
                    p.wait(t, "kev")
                if idx >= 2:
                    p.wait(t, "vtev", idx - 1)
                ps = P3 if idx % 2 == 0 else P4
                for c in range(DC):
                    mm = t.matmul(ps[:, :], xn[:, c, tc * 128:(tc + 1) * 128],
                                  wv_sb[:, c, half * TT:(half + 1) * TT],
                                  start=(c == 0), stop=(c == DC - 1))
                p.inc(mm, "vtmm", 1)

            @blk.vector
            def _(v, half=half, idx=idx):
                v.wait_ge(p.sem("vtmm"), idx + 1)
                if idx >= 1:
                    p.wait(v, "vtw", 16 * idx)
                ps = P3 if idx % 2 == 0 else P4
                p.inc(v.tensor_tensor(vtile[:, :], ps[:, :],
                                      bvb.rearrange("p c f -> p (c f)")
                                      [:, half * TT:(half + 1) * TT], ALU.add),
                      "vtev", 1)

            @blk.scalar
            def _(s, tc=tc, half=half, idx=idx):
                s.wait_ge(p.sem("vtev"), idx + 1)
                p.inc(s.dma_start(out=v1_o[tc * 128:(tc + 1) * 128,
                                           half * TT:(half + 1) * TT],
                                  in_=vtile[:, :]), "vtw", 16)

    @blk.sync
    def _(sy):
        p.wait(sy, "x1w")
        p.wait(sy, "qqw")
        p.wait(sy, "kqw")
        p.wait(sy, "qnw")
        p.wait(sy, "knw")
        p.wait(sy, "vtw")
    return p



def build_launch_C():
    p = Prog()
    x1_d = p.din("x1", [D, TPC])
    q1_d = p.din("q1", [NH, DH, TPC], BF16)
    kf_d = p.din("kfull", [NH, DH, T], BF16)
    vf_d = p.din("vfull", [NH, T, DH], BF16)
    qn2_d = p.din("qn2", [NH, TPC])
    kn2_d = p.din("kn2f", [NH, T])
    wo_d = p.din("woT", [D, D])
    bo_d = p.din("bo", [128, DC])
    wgq_d = p.din("wgqT", [D, 2816], BF16)
    wuq_d = p.din("wuqT", [D, 2816], BF16)
    wdq_d = p.din("wdqT", [2816, D], BF16)
    id_d = p.din("ident_in", [128, 128])
    lng_d = p.din("ln_g", [128, DC])
    lnb_d = p.din("ln_b", [128, DC])
    win_d = p.din("w_in", [D, D])
    bin_d = p.din("b_in", [128, DC])

    x2_o = p.dout("x2", [D, TPC])
    ll_o = p.dout("left_lin", [D, TPC])

    xn_dram = p.dtmp("xnf32d", [D, TPC])
    mb_dram = p.dtmp("mbd", [1, NH])

    xt = p.sb("xt", [128, DC, TT])
    xf = p.sb("xf", [128, DC, TT])
    xn2t = p.sb("xn2t", [128, DC, TT], BF16)
    g1 = p.sb("g1", [128, DC, TPC], BF16)
    sq = p.sb("sq", [128, 1, TT])
    srow = p.sb("srow", [1, 2, TT])
    ones = p.sb("ones", [128, 2])
    cst = p.sb("cst", [128, 2])
    tmp = p.sb("tmpa", [128, TT])
    identb = p.sb("identb", [128, 128], BF16)
    xnq = p.sb("xnq", [128, DC, 128], BF16)
    xtm = p.sb("xtm", [128, D])
    gtm = p.sb("gtm", [128, 2816])
    utm = p.sb("utm", [128, 2816])
    qa = p.sb("qa", [128, 2816])
    qc1 = p.sb("qc1", [128, 2816])
    qc2 = p.sb("qc2", [128, 2816])
    qsc = p.sb("qsc", [128, 2816 // 16, 1])
    qrc = p.sb("qrc", [128, 2816 // 16, 1])
    hq = p.sb("hq", [128, 2816], BF16)
    khb = p.sb("khb", [128, T], BF16)
    wst = khb.rearrange("p (c f) -> p c f", f=TT)[:, 0:DC, :]
    wstU = khb.rearrange("p (c f) -> p c f", f=TT)[:, DC:2 * DC, :]
    kh = khb[0:64, :]
    wst2 = p.sb("wst2", [128, 2, DC, 128], BF16)
    vh = gtm.bitcast(BF16)[:, 0:(T // 128) * (DH + 1)].rearrange(
        "p (c d) -> p c d", d=DH + 1)
    qh = utm.bitcast(BF16)[0:64, 0:TT]
    pbs = utm.bitcast(BF16)[:, 1024:1024 + 2 * TT].rearrange("p (b f) -> p b f", f=TT)
    ofm = qc1.bitcast(BF16)[:, 0:DC * TT].rearrange("p (c f) -> p c f", f=TT)
    onr = qc2[0:1, 0:TT]
    oev = qc2[0:64, TT:2 * TT]
    s1t = qc2[:, 2 * TT:3 * TT]
    mrow = p.sb("mrow", [1, NH])
    mb = p.sb("mb", [128, NH])
    nsc = p.sb("nsc", [16, 4])
    gb = p.sb("gb", [128, 4, DC])
    qn_sb = p.sb("qn_sb", [16, TPC])
    kn_sb = p.sb("kn_sb", [16, TPC])

    P0 = p.ps("P0", [128, TT])
    P1 = p.ps("P1", [128, TT])
    P2 = p.ps("P2", [128, TT])
    P3 = p.ps("P3", [128, TT])
    P4 = p.ps("P4", [128, TT])
    P5 = p.ps("P5", [128, TT])
    PTf = p.ps("PTf", [128, 128])
    PTb = p.ps("PTb", [128, 128], BF16)

    blk = p.block()

    @blk.vector
    def _(v):
        v.memset(ones[:, :], 1.0)
        v.memset(cst[:, 0:1], 1e-5)
        p.inc(v.memset(cst[:, 1:2], 1e-38), "init", 1)

    @blk.scalar
    def _(s):
        p.wait(s, "init")

    @blk.tensor
    def _(t):
        p.wait(t, "init")

    @blk.gpsimd
    def _(g):
        p.inc(g.dma_start(out=identb[:, :], in_=id_d[:, :]), "wld", 16)
        p.inc(g.dma_start(out=gb[:, 0, :], in_=lng_d[:, :]), "wld", 16)
        p.inc(g.dma_start(out=gb[:, 1, :], in_=lnb_d[:, :]), "wld", 16)
        p.inc(g.dma_start(out=gb[:, 2, :], in_=bin_d[:, :]), "wld", 16)
        p.inc(g.dma_start(out=gb[:, 3, :], in_=bo_d[:, :]), "wld", 16)

    @blk.sync
    def _(sy):
        p.inc(sy.dma_start(out=qn_sb[:, :], in_=qn2_d[:, :]), "nld", 16)

    @blk.vector
    def _(v):
        p.wait(v, "nld")
        p.vsf(v, v.reduce_max(nsc[:, 0:1], qn_sb[:, :], axis=AX.X))

    for kc4 in range(4):
        @blk.sync
        def _(sy, kc4=kc4):
            if kc4 > 0:
                p.wait(sy, "nkm", kc4)
            p.inc(sy.dma_start(out=kn_sb[:, :],
                               in_=kn2_d[:, kc4 * TPC:(kc4 + 1) * TPC]), "nld2", 16)

        @blk.vector
        def _(v, kc4=kc4):
            v.wait_ge(p.sem("nld2"), 16 * (kc4 + 1))
            p.vsf(v, v.reduce_max(nsc[:, 3:4], kn_sb[:, :], axis=AX.X))
            if kc4 == 0:
                p.inc(v.tensor_copy(nsc[:, 1:2], nsc[:, 3:4]), "nkm", 1)
            else:
                p.inc(v.tensor_max(nsc[:, 1:2], nsc[:, 1:2], nsc[:, 3:4]), "nkm", 1)

    @blk.vector
    def _(v):
        p.wait(v, "nkm")
        p.vsf(v, v.engine_nop())
        p.inc(v.tensor_mul(nsc[:, 2:3], nsc[:, 0:1], nsc[:, 1:2]), "nm1", 1)

    @blk.scalar
    def _(s):
        p.wait(s, "nm1")
        p.inc(s.activation(out=nsc[:, 3:4], in_=nsc[:, 2:3], func=AF.Sqrt), "nm2", 1)

    @blk.vector
    def _(v):
        p.wait(v, "nm2")
        p.inc(v.tensor_scalar_mul(nsc[:, 2:3], nsc[:, 3:4], -0.125), "nm3", 1)

    @blk.sync
    def _(sy):
        p.wait(sy, "nm3")
        p.inc(sy.dma_start(out=mb_dram.rearrange("o h -> (o h)")[None, :],
                           in_=nsc[:, 2:3]), "nm4", 16)
        sy.wait_ge(p.sem("nm4"), 16)
        p.inc(sy.dma_start(out=mrow[0:1, :], in_=mb_dram[0:1, :]), "nm5", 16)

    @blk.tensor
    def _(t):
        t.wait_ge(p.sem("nm5"), 16)
        p.inc(t.matmul(PTf[:, 0:NH], ones[0:1, 0:1].to_broadcast([1, 128]),
                       mrow[0:1, :], start=True, stop=True), "nm6", 1)

    @blk.vector
    def _(v):
        p.wait(v, "nm6")
        p.inc(v.tensor_copy(mb[:, :], PTf[:, 0:NH]), "nm7", 1)

    for it in range(NTT):
        tsl = slice(it * TT, (it + 1) * TT)

        @blk.sync
        def _(sy, it=it, tsl=tsl):
            if it >= 1:
                p.wait(sy, "ln_xn", it)
                p.wait(sy, "xfw", 16 * it)
            p.inc(sy.dma_start(out=xt[:, :, :],
                               in_=x1_d.rearrange("(c p) t -> p c t", p=128)[:, :, tsl]),
                  "ln_xld", 16)

        emit_ln_tile(p, blk, f"c{it}", xt, [g1[:, c, tsl] for c in range(DC)],
                     sq, srow, P2, P3, P4, P5, ones, tmp=tmp, eps_ap=cst[0:1, 0:1],
                     xn_f32=[xf[:, c, :] for c in range(DC)])

        @blk.sync
        def _(sy, it=it, tsl=tsl):
            p.wait(sy, "ln_xn")
            p.inc(sy.dma_start(out=xn_dram.rearrange("(c p) t -> p c t", p=128)[:, :, tsl],
                               in_=xf[:, :, :]), "xfw", 16)

    ident = p.sb("ident", [128, 128])

    @blk.sync
    def _(sy):
        p.inc(sy.dma_start(out=ident[:, :], in_=id_d[:, :]), "wlds", 16)

    fcw = [TT] * 5 + [256]
    fco = [0, 512, 1024, 1536, 2048, 2560]
    wst3 = p.sb("wst3", [128, 2, 22, 128], BF16)
    hfm = p.sb("hfm", [128, 22, 128], BF16)

    for tc in range(NTC):
        csl = slice(tc * 128, (tc + 1) * 128)

        @blk.sync
        def _(sy, tc=tc, csl=csl):
            if tc == 0:
                p.wait(sy, "xfw")
            else:
                p.wait(sy, "xqt", tc * DC)
            p.inc(sy.dma_start(out=xt[:, :, 0:128],
                               in_=xn_dram.rearrange("(c p) t -> p c t", p=128)[:, :, csl]),
                  "xcl", 16)

        @blk.tensor
        def _(t, tc=tc):
            t.wait_ge(p.sem("xcl"), 16 * (tc + 1))
            if tc == 0:
                p.wait(t, "wlds")
                p.wait(t, "wld")
            for c in range(DC):
                idx = tc * DC + c
                if idx >= 1:
                    p.wait(t, "xte", idx)
                p.inc(t.transpose(PTf[:, :], xt[:, c, 0:128], ident[:, :]), "xqt", 1)

        @blk.vector
        def _(v, tc=tc):
            for c in range(DC):
                idx = tc * DC + c
                v.wait_ge(p.sem("xqt"), idx + 1)
                cp = v.tensor_copy(xtm[:, c * 128:(c + 1) * 128], PTf[:, :])
                p.inc(cp, "xte", 1)
            p.inc(v.engine_nop(), "qqin", 1)

        emit_fp4_qdq(p, blk, xtm[:, :], qa[:, 0:1024], qsc, qrc, qc1[:, 0:1024],
                     qc2[:, 0:1024], hq[:, 0:1024], 128, 1024)

        @blk.tensor
        def _(t, tc=tc):
            p.wait(t, "qqou")
            for c in range(DC):
                idx = tc * DC + c
                if idx >= 1:
                    p.wait(t, "xqe", idx)
                p.inc(t.transpose(PTb[:, :], hq[:, c * 128:(c + 1) * 128], identb[:, :]),
                      "xqm", 1)

        @blk.vector
        def _(v, tc=tc):
            for c in range(DC):
                idx = tc * DC + c
                v.wait_ge(p.sem("xqm"), idx + 1)
                p.inc(v.tensor_copy(xnq[:, c, :], PTb[:, :]), "xqe", 1)

        @blk.gpsimd
        def _(g, tc=tc):
            for fc in range(6):
                idx = tc * 6 + fc
                if idx >= 1:
                    g.wait_ge(p.sem("gumm"), 2 * idx)
                p.inc(g.dma_start(out=wst[:, :, 0:fcw[fc]].rearrange("p c f -> p c f"),
                                  in_=wgq_d.rearrange("(c p) f -> p c f", p=128)
                                  [:, :, fco[fc]:fco[fc] + fcw[fc]]), "gul", 16)
                p.inc(g.dma_start(out=wstU[:, :, 0:fcw[fc]].rearrange("p c f -> p c f"),
                                  in_=wuq_d.rearrange("(c p) f -> p c f", p=128)
                                  [:, :, fco[fc]:fco[fc] + fcw[fc]]), "gul", 16)

        @blk.tensor
        def _(t, tc=tc):
            p.wait(t, "xqe")
            for fc in range(6):
                idx = tc * 6 + fc
                t.wait_ge(p.sem("gul"), 32 * (idx + 1))
                if idx >= 2:
                    t.wait_ge(p.sem("guev"), 2 * (idx - 1))
                for c in range(DC):
                    mmG = t.matmul(P0[:, 0:fcw[fc]], xnq[:, c, :],
                                   wst[:, c, 0:fcw[fc]],
                                   start=(c == 0), stop=(c == DC - 1))
                for c in range(DC):
                    mmU = t.matmul(P1[:, 0:fcw[fc]], xnq[:, c, :],
                                   wstU[:, c, 0:fcw[fc]],
                                   start=(c == 0), stop=(c == DC - 1))
                p.inc(mmU, "gumm", 2)

        @blk.vector
        def _(v, tc=tc):
            for fc in range(6):
                idx = tc * 6 + fc
                v.wait_ge(p.sem("gumm"), 2 * (idx + 1))
                v.tensor_copy(gtm[:, fco[fc]:fco[fc] + fcw[fc]], P0[:, 0:fcw[fc]])
                p.inc(v.tensor_copy(utm[:, fco[fc]:fco[fc] + fcw[fc]], P1[:, 0:fcw[fc]]),
                      "guev", 2)

        @blk.scalar
        def _(s, tc=tc):
            p.wait(s, "guev")
            p.inc(s.activation(out=qa[:, :], in_=gtm[:, :], func=AF.Silu), "gsl", 1)

        @blk.vector
        def _(v, tc=tc):
            p.wait(v, "gsl")
            p.inc(v.tensor_mul(gtm[:, :], qa[:, :], utm[:, :]), "qqin", 1)

        emit_fp4_qdq(p, blk, gtm[:, :], qa, qsc, qrc, qc1, qc2, hq, 128, 2816)

        @blk.tensor
        def _(t, tc=tc):
            p.wait(t, "qqou")
            for k in range(22):
                idx = tc * 22 + k
                if idx >= 1:
                    p.wait(t, "hfe", idx)
                p.inc(t.transpose(PTb[:, :], hq[:, k * 128:(k + 1) * 128], identb[:, :]),
                      "hfm", 1)

        @blk.vector
        def _(v, tc=tc):
            for k in range(22):
                idx = tc * 22 + k
                v.wait_ge(p.sem("hfm"), idx + 1)
                p.inc(v.tensor_copy(hfm[:, k, :], PTb[:, :]), "hfe", 1)

        @blk.gpsimd
        def _(g, tc=tc):
            for oc in range(DC):
                idx = tc * DC + oc
                if idx >= 2:
                    g.wait_ge(p.sem("dwmm"), idx - 1)
                p.inc(g.dma_start(out=wst3[:, idx % 2, :, :],
                                  in_=wdq_d.rearrange("(k p) m -> p k m", p=128)
                                  [:, :, oc * 128:(oc + 1) * 128]), "dwl", 16)

        @blk.tensor
        def _(t, tc=tc):
            p.wait(t, "hfe")
            for oc in range(DC):
                idx = tc * DC + oc
                t.wait_ge(p.sem("dwl"), 16 * (idx + 1))
                if idx >= 1:
                    t.wait_ge(p.sem("g1e"), idx)
                for k in range(22):
                    mm = t.matmul(P4[:, 0:128], wst3[:, idx % 2, k, :], hfm[:, k, :],
                                  start=(k == 0), stop=(k == 21))
                p.inc(mm, "dwmm", 1)

        @blk.scalar
        def _(s, tc=tc, csl=csl):
            for oc in range(DC):
                idx = tc * DC + oc
                s.wait_ge(p.sem("dwmm"), idx + 1)
                p.inc(s.activation(out=g1[:, oc, csl], in_=P4[:, 0:128], func=AF.Silu),
                      "g1e", 1)

    NCK = T // 128

    @blk.vector
    def _(v):
        p.wait(v, "g1e")
        p.inc(v.memset(vh[:, :, DH:DH + 1], 1.0), "vhi", 1)

    for tqb in range(NTT):
        tsl = slice(tqb * TT, (tqb + 1) * TT)

        for h in range(NH):
            hidx = tqb * NH + h

            @blk.gpsimd
            def _(g, h=h, tqb=tqb, hidx=hidx):
                if hidx == 0:
                    p.wait(g, "dwmm")
                else:
                    g.wait_ge(p.sem("avm"), hidx * NCK)
                p.inc(g.dma_start(out=kh[:, :], in_=kf_d[h]), "khl", 16)
                p.inc(g.dma_start(out=vh[:, :, 0:DH],
                                  in_=vf_d[h].rearrange("(c p) d -> p c d", p=128)),
                      "khl", 16)
                p.inc(g.dma_start(out=qh[:, 0:TT],
                                  in_=q1_d[h, :, tqb * TT:(tqb + 1) * TT]), "khl", 16)

            @blk.tensor
            def _(t, h=h, tqb=tqb, hidx=hidx):
                t.wait_ge(p.sem("khl"), 48 * (hidx + 1))
                if hidx >= 1:
                    t.wait_ge(p.sem("ocp"), hidx)
                if hidx == 0:
                    p.wait(t, "nm7")
                    p.wait(t, "guev")
                    p.wait(t, "g1e")
                    p.wait(t, "vhi")
                if h == 0 and tqb > 0:
                    p.wait(t, "s1e")
                    p.wait(t, "ln_xn")
                    p.wait(t, "ll_ev")
                base = hidx * NCK

                def smm(ck):
                    psB = P0 if ck % 2 == 0 else P1
                    if base + ck >= 2:
                        t.wait_ge(p.sem("pex"), base + ck - 1)
                    p.inc(t.matmul(psB[:, :], kh[:, ck * 128:(ck + 1) * 128],
                                   qh[:, 0:TT], start=True, stop=True), "smm", 1)

                smm(0)
                smm(1)
                for ck in range(NCK):
                    t.wait_ge(p.sem("pex"), base + ck + 1)
                    p.inc(t.matmul(P3[0:65, :], vh[:, ck, :], pbs[:, ck % 2, :],
                                   start=(ck == 0), stop=(ck == NCK - 1)), "avm", 1)
                    if ck + 2 < NCK:
                        smm(ck + 2)

            @blk.scalar
            def _(s, h=h, hidx=hidx):
                base = hidx * NCK
                for ck in range(NCK):
                    s.wait_ge(p.sem("smm"), base + ck + 1)
                    if base + ck >= 2:
                        s.wait_ge(p.sem("avm"), base + ck - 1)
                    psB = P0 if ck % 2 == 0 else P1
                    p.inc(s.activation(out=pbs[:, ck % 2, :], in_=psB[:, :], func=AF.Exp,
                                       bias=mb[:, h:h + 1], scale=0.125), "pex", 1)

            @blk.vector
            def _(v, h=h, hidx=hidx):
                v.wait_ge(p.sem("avm"), (hidx + 1) * NCK)
                p.wait(v, "s1mm")
                p.vsf(v, v.tensor_scalar_add(onr[0:1, :], P3[64:65, :], cst[0:1, 1:2]))
                p.vsf(v, v.reciprocal(onr[0:1, :], onr[0:1, :]))
                p.inc(v.tensor_copy(oev[:, :], P3[0:64, :]), "ocp", 1)

            @blk.tensor
            def _(t, hidx=hidx):
                t.wait_ge(p.sem("ocp"), hidx + 1)
                if hidx >= 1:
                    t.wait_ge(p.sem("oev"), hidx)
                p.inc(t.matmul(P4[0:64, :], ones[0:1, 0:1].to_broadcast([1, 64]),
                               onr[0:1, :], start=True, stop=True), "obc", 1)

            @blk.vector
            def _(v, h=h, hidx=hidx):
                v.wait_ge(p.sem("obc"), hidx + 1)
                p.inc(v.tensor_mul(ofm[(h % 2) * 64:(h % 2) * 64 + 64, h // 2, :],
                                   oev[:, :], P4[0:64, :]), "oev", 1)

        @blk.sync
        def _(sy, tqb=tqb, tsl=tsl):
            if tqb == 0:
                p.wait(sy, "xqt")
            else:
                p.wait(sy, "x2w", 16 * tqb)
            p.inc(sy.dma_start(out=xt[:, :, :],
                               in_=x1_d.rearrange("(c p) t -> p c t", p=128)[:, :, tsl]),
                  "x2l", 16)

        @blk.gpsimd
        def _(g, tqb=tqb):
            for oc in range(DC):
                idx = tqb * DC + oc
                if idx >= 2:
                    g.wait_ge(p.sem("s1mm"), idx - 1)
                p.inc(g.dma_start(out=wst2[:, idx % 2, :, :],
                                  in_=wo_d.rearrange("(c p) m -> p c m", p=128)
                                  [:, :, oc * 128:(oc + 1) * 128]), "wol", 16)

        for oc in range(DC):
            idx = tqb * DC + oc

            @blk.tensor
            def _(t, tqb=tqb, oc=oc, idx=idx):
                t.wait_ge(p.sem("oev"), (tqb + 1) * NH)
                t.wait_ge(p.sem("wol"), 16 * (idx + 1))
                if idx >= 2:
                    t.wait_ge(p.sem("s1e"), idx - 1)
                ps = P0 if idx % 2 == 0 else P1
                for c in range(DC):
                    mm = t.matmul(ps[:, :], wst2[:, idx % 2, c, :], ofm[:, c, :],
                                  start=(c == 0), stop=(c == DC - 1))
                p.inc(mm, "s1mm", 1)

            @blk.vector
            def _(v, tqb=tqb, oc=oc, idx=idx, tsl=tsl):
                v.wait_ge(p.sem("s1mm"), idx + 1)
                v.wait_ge(p.sem("x2l"), 16 * (tqb + 1))
                ps = P0 if idx % 2 == 0 else P1
                p.inc(v.tensor_scalar(s1t[:, :], ps[:, :], gb[:, 3, oc:oc + 1],
                                      None, ALU.add), "s1e", 1)
                v.tensor_mul(s1t[:, :], s1t[:, :], g1[:, oc, tsl])
                p.inc(v.tensor_add(xf[:, oc, :], xt[:, oc, :], s1t[:, :]), "x2c", 1)

        @blk.sync
        def _(sy, tqb=tqb, tsl=tsl):
            p.wait(sy, "x2c")
            p.inc(sy.dma_start(out=x2_o.rearrange("(c p) t -> p c t", p=128)[:, :, tsl],
                               in_=xf[:, :, :]), "x2w", 16)

        emit_ln_tile(p, blk, f"d{tqb}", xf, [xn2t[:, c, :] for c in range(DC)],
                     sq, srow, P2, P3, P4, P5, ones, tmp=tmp, eps_ap=cst[0:1, 0:1],
                     gain_ap=[gb[:, 0, c:c + 1] for c in range(DC)],
                     bias_ap=[gb[:, 1, c:c + 1] for c in range(DC)],
                     extra_wait=("x2c", "llmm"))

        @blk.gpsimd
        def _(g, tqb=tqb):
            for oc in range(DC):
                idx = tqb * DC + oc
                if idx >= 2:
                    g.wait_ge(p.sem("llmm"), idx - 1)
                p.inc(g.dma_start(out=wst2[:, idx % 2, :, :],
                                  in_=win_d.rearrange("(c p) m -> p c m", p=128)
                                  [:, :, oc * 128:(oc + 1) * 128]), "wil", 16)

        for oc in range(DC):
            idx = tqb * DC + oc

            @blk.tensor
            def _(t, tqb=tqb, oc=oc, idx=idx, tsl=tsl):
                p.wait(t, "ln_xn")
                t.wait_ge(p.sem("wil"), 16 * (idx + 1))
                if idx >= 2:
                    t.wait_ge(p.sem("ll_ev"), idx - 1)
                ps = P0 if idx % 2 == 0 else P1
                for c in range(DC):
                    mm = t.matmul(ps[:, :], wst2[:, idx % 2, c, :], xn2t[:, c, :],
                                  start=(c == 0), stop=(c == DC - 1))
                p.inc(mm, "llmm", 1)

            @blk.vector
            def _(v, oc=oc, idx=idx):
                v.wait_ge(p.sem("llmm"), idx + 1)
                if idx >= DC:
                    p.wait(v, "llw", 16 * (idx // DC))
                ps = P0 if idx % 2 == 0 else P1
                p.inc(v.tensor_scalar(xt[:, oc, :], ps[:, :], gb[:, 2, oc:oc + 1],
                                      None, ALU.add), "ll_ev", 1)

        @blk.sync
        def _(sy, tqb=tqb, tsl=tsl):
            p.wait(sy, "ll_ev")
            p.inc(sy.dma_start(out=ll_o.rearrange("(c p) t -> p c t", p=128)[:, :, tsl],
                               in_=xt[:, :, :]), "llw", 16)

    @blk.sync
    def _(sy):
        p.wait(sy, "x2w")
        p.wait(sy, "llw")
    return p



def build_launch_D():
    p = Prog()
    ll_d = p.din("ll", [256, T])
    cw_d = p.din("conv_w", [128, 2, 4])
    cb_d = p.din("conv_b", [128, 2])
    lc_o = p.dout("lc", [256, T])

    lin = p.sb("lin", [128, 3 + T])
    lout = p.sb("lout", [128, T])
    cw = p.sb("cw", [128, 2, 4])
    cb = p.sb("cb", [128, 2])

    blk = p.block()

    @blk.gpsimd
    def _(g):
        p.inc(g.dma_start(out=cw[:, :, :], in_=cw_d[:, :, :]), "wld", 16)
        p.inc(g.dma_start(out=cb[:, :], in_=cb_d[:, :]), "wld", 16)

    for ch in range(2):
        @blk.vector
        def _(v, ch=ch):
            if ch == 0:
                p.wait(v, "wld")
            p.inc(v.memset(lin[:, 0:3], 0.0), "mz", 1)

        @blk.sync
        def _(sy, ch=ch):
            p.wait(sy, "mz")
            if ch == 1:
                p.wait(sy, "cvw", 16)
            p.inc(sy.dma_start(out=lin[:, 3:], in_=ll_d[ch * 128:(ch + 1) * 128, :]),
                  "lld", 16)

        @blk.vector
        def _(v, ch=ch):
            v.wait_ge(p.sem("lld"), 16 * (ch + 1))
            v.tensor_scalar_mul(lout[:, :], lin[:, 0:T], cw[:, ch, 0:1])
            for k in range(1, 4):
                v.scalar_tensor_tensor(lout[:, :], lin[:, k:k + T], cw[:, ch, k:k + 1],
                                       lout[:, :], ALU.mult, ALU.add)
            p.inc(v.tensor_scalar_add(lout[:, :], lout[:, :], cb[:, ch:ch + 1]),
                  "cv", 1)

        @blk.scalar
        def _(s, ch=ch):
            s.wait_ge(p.sem("cv"), ch + 1)
            p.inc(s.activation(out=lout[:, :], in_=lout[:, :], func=AF.Silu), "cs", 1)

        @blk.sync
        def _(sy, ch=ch):
            sy.wait_ge(p.sem("cs"), ch + 1)
            p.inc(sy.dma_start(out=lc_o[ch * 128:(ch + 1) * 128, :], in_=lout[:, :]),
                  "cvw", 16)

    @blk.sync
    def _(sy):
        p.wait(sy, "cvw")
    return p



def build_launch_E():
    p = Prog()
    lcf_d = p.din("lcf", [D, T])
    wdt_d = p.din("wdt", [D, 256])
    bdt_d = p.din("bdt", [128, 2])
    wbc_d = p.din("wbc", [D, 32])
    A_d = p.din("A", [128, 2, 16])
    Dssm_d = p.din("Dssm", [128, 2])
    row0 = 0
    own_d = p.din("own", [256, T])

    y_o = p.dout("yssm", [256, T])

    SEG = 2048
    NSEG = T // SEG

    lcb = p.sb("lcb", [128, 2, DC, TT], BF16)
    wdt = p.sb("wdt_sb", [128, DC, 256], BF16)
    wbc = p.sb("wbc_sb", [128, DC, 32], BF16)
    bdt = p.sb("bdt_sb", [128, 2])
    A_sb = p.sb("A_sb", [128, 2, 16])
    Ds = p.sb("Ds_sb", [128, 2])
    cst = p.sb("cst", [128, 2])
    bc_sb = p.sb("bc_sb", [32, T])
    dtt = p.sb("dtt", [128, 2, SEG])
    lco = p.sb("lco", [128, 2, SEG])
    dl = p.sb("dl", [128, 2, SEG])
    Bb = p.sb("Bb", [128, SEG])
    Cb = p.sb("Cb", [128, SEG])
    dA = p.sb("dA", [128, SEG])
    uu = p.sb("uu", [128, SEG])
    hh = p.sb("hh", [128, SEG])
    yac = p.sb("yac", [128, 2, SEG])
    tmp = p.sb("tmp", [128, SEG])
    hini = p.sb("hini", [128, 2, 16])
    brow = p.sb("brow", [1, 2, SEG])
    sp1 = p.sb("sp1", [128, SEG])
    ones = p.sb("ones", [128, 1])

    bc_dram = p.dtmp("bcd", [32, T])

    P0 = p.ps("P0", [128, TT])
    P1 = p.ps("P1", [128, TT])
    P2 = p.ps("P2", [128, TT])
    P3 = p.ps("P3", [128, TT])

    blk = p.block()

    @blk.vector
    def _(v):
        v.memset(ones[:, :], 1.0)
        v.memset(cst[:, 0:1], 1.0)
        v.memset(hini[:, :, :], 0.0)
        p.inc(v.engine_nop(), "init", 1)

    @blk.scalar
    def _(s):
        p.wait(s, "init")

    @blk.tensor
    def _(t):
        p.wait(t, "init")

    @blk.gpsimd
    def _(g):
        p.inc(g.dma_start(out=wdt[:, :, :],
                          in_=wdt_d.rearrange("(c p) m -> p c m", p=128)), "wld", 16)
        p.inc(g.dma_start(out=wbc[:, :, :],
                          in_=wbc_d.rearrange("(c p) m -> p c m", p=128)), "wld", 16)
        p.inc(g.dma_start(out=bdt[:, :], in_=bdt_d[:, :]), "wld", 16)
        p.inc(g.dma_start(out=A_sb[:, :, :], in_=A_d[:, :, :]), "wld", 16)
        p.inc(g.dma_start(out=Ds[:, :], in_=Dssm_d[:, :]), "wld", 16)

    for it in range(T // TT):
        @blk.gpsimd
        def _(g, it=it):
            if it >= 2:
                g.wait_ge(p.sem("bcm"), it - 1)
            p.inc(g.dma_start(out=lcb[:, it % 2, :, :],
                              in_=lcf_d.rearrange("(c p) t -> p c t", p=128)
                              [:, :, it * TT:(it + 1) * TT]), "lcl", 16)

        @blk.tensor
        def _(t, it=it):
            t.wait_ge(p.sem("lcl"), 16 * (it + 1))
            p.wait(t, "wld")
            if it >= 2:
                t.wait_ge(p.sem("bce"), it - 1)
            ps = P2 if it % 2 == 0 else P3
            for c in range(DC):
                mm = t.matmul(ps[0:32, :], wbc[:, c, :], lcb[:, it % 2, c, :],
                              start=(c == 0), stop=(c == DC - 1))
            p.inc(mm, "bcm", 1)

        @blk.vector
        def _(v, it=it):
            v.wait_ge(p.sem("bcm"), it + 1)
            ps = P2 if it % 2 == 0 else P3
            p.inc(v.tensor_copy(bc_sb[:, it * TT:(it + 1) * TT], ps[0:32, :]), "bce", 1)

    @blk.sync
    def _(sy):
        p.wait(sy, "bce")
        p.inc(sy.dma_start(out=bc_dram[:, :], in_=bc_sb[0:32, :]), "bcw", 16)

    for sg in range(NSEG):
        ssl = slice(sg * SEG, (sg + 1) * SEG)

        @blk.sync
        def _(sy, sg=sg, ssl=ssl):
            if sg > 0:
                p.wait(sy, "yw", 16 * sg)
            p.inc(sy.dma_start(out=lco[:, :, :],
                               in_=own_d.rearrange("(c p) t -> p c t", p=128)[:, :, ssl]),
                  "lcol", 16)

        for st in range(SEG // TT):
            it = sg * (SEG // TT) + st

            @blk.gpsimd
            def _(g, sg=sg, st=st, it=it, ssl=ssl):
                g.wait_ge(p.sem("bcm"), 16)
                if it >= 2:
                    g.wait_ge(p.sem("dtm"), 2 * (it - 1))
                p.inc(g.dma_start(out=lcb[:, it % 2, :, :],
                                  in_=lcf_d.rearrange("(c p) t -> p c t", p=128)
                                  [:, :, sg * SEG + st * TT:sg * SEG + (st + 1) * TT]),
                      "lcl2", 16)

            @blk.tensor
            def _(t, it=it, st=st):
                t.wait_ge(p.sem("lcl2"), 16 * (it + 1))
                if it >= 1:
                    t.wait_ge(p.sem("dte"), 2 * it)
                for oc in range(2):
                    ps = P0 if oc == 0 else P1
                    for c in range(DC):
                        mm = t.matmul(ps[:, :], wdt[:, c, oc * 128:(oc + 1) * 128],
                                      lcb[:, it % 2, c, :],
                                      start=(c == 0), stop=(c == DC - 1))
                    p.inc(mm, "dtm", 1)

            @blk.scalar
            def _(s, it=it, st=st):
                for oc in range(2):
                    s.wait_ge(p.sem("dtm"), 2 * it + oc + 1)
                    if 2 * it + oc >= 1:
                        s.wait_ge(p.sem("dte"), 2 * it + oc)
                    s.activation(out=tmp[:, 0:TT], in_=(P0 if oc == 0 else P1)[:, :],
                                 func=AF.Abs, bias=bdt[:, oc:oc + 1])
                    s.activation(out=sp1[:, 0:TT], in_=tmp[:, 0:TT], func=AF.Exp,
                                 scale=-1.0)
                    p.inc(s.activation(out=sp1[:, 0:TT], in_=sp1[:, 0:TT], func=AF.Ln,
                                       bias=cst[:, 0:1]), "spl", 1)

            @blk.vector
            def _(v, it=it, st=st):
                for oc in range(2):
                    v.wait_ge(p.sem("spl"), 2 * it + oc + 1)
                    ps = P0 if oc == 0 else P1
                    v.tensor_scalar(tmp[:, 0:TT], ps[:, :], bdt[:, oc:oc + 1], 0.0,
                                    ALU.add, ALU.max)
                    v.tensor_add(tmp[:, 0:TT], tmp[:, 0:TT], sp1[:, 0:TT])
                    p.inc(v.tensor_copy(dtt[:, oc, st * TT:(st + 1) * TT], tmp[:, 0:TT]),
                          "dte", 1)

        @blk.vector
        def _(v, sg=sg):
            p.wait(v, "dte")
            v.wait_ge(p.sem("lcol"), 16 * (sg + 1))
            v.tensor_mul(dl[:, 0, :], dtt[:, 0, :], lco[:, 0, :])
            p.inc(v.tensor_mul(dl[:, 1, :], dtt[:, 1, :], lco[:, 1, :]), "dlv", 1)

        for n in range(16):
            nidx = sg * 16 + n

            @blk.sync
            def _(sy, sg=sg, n=n, ssl=ssl, nidx=nidx):
                if nidx >= 1:
                    p.wait(sy, "bbm", 4 * nidx)
                sy.wait_ge(p.sem("bcw"), 16)
                p.inc(sy.dma_start(out=brow[0:1, 0, :], in_=bc_dram[n:n + 1, ssl]),
                      "brl", 16)
                p.inc(sy.dma_start(out=brow[0:1, 1, :], in_=bc_dram[16 + n:17 + n, ssl]),
                      "brl", 16)

            for q4 in range(4):
                qidx = nidx * 4 + q4

                @blk.tensor
                def _(t, q4=q4, nidx=nidx, qidx=qidx):
                    t.wait_ge(p.sem("brl"), 32 * (nidx + 1))
                    if qidx >= 1:
                        t.wait_ge(p.sem("bbe"), qidx)
                    t.matmul(P2[:, :], ones[0:1, 0:1].to_broadcast([1, 128]),
                             brow[0:1, 0, q4 * TT:(q4 + 1) * TT], start=True, stop=True)
                    p.inc(t.matmul(P3[:, :], ones[0:1, 0:1].to_broadcast([1, 128]),
                                   brow[0:1, 1, q4 * TT:(q4 + 1) * TT],
                                   start=True, stop=True), "bbm", 1)

                @blk.vector
                def _(v, q4=q4, qidx=qidx):
                    v.wait_ge(p.sem("bbm"), qidx + 1)
                    if qidx >= 1:
                        p.wait(v, "yacc", 0) if False else None
                    v.tensor_copy(Bb[:, q4 * TT:(q4 + 1) * TT], P2[:, :])
                    p.inc(v.tensor_copy(Cb[:, q4 * TT:(q4 + 1) * TT], P3[:, :]),
                          "bbe", 1)

            for oc in range(2):
                oidx = (sg * 16 + n) * 2 + oc

                @blk.vector
                def _(v, sg=sg, n=n, oc=oc, oidx=oidx):
                    p.wait(v, "bbe")
                    p.wait(v, "dlv")
                    p.inc(v.tensor_scalar_mul(dA[:, :], dtt[:, oc, :],
                                              A_sb[:, oc, n:n + 1]), "dam", 1)

                @blk.scalar
                def _(s, oidx=oidx):
                    s.wait_ge(p.sem("dam"), oidx + 1)
                    p.inc(s.activation(out=dA[:, :], in_=dA[:, :], func=AF.Exp),
                          "dax", 1)

                @blk.vector
                def _(v, sg=sg, n=n, oc=oc, oidx=oidx):
                    v.wait_ge(p.sem("dax"), oidx + 1)
                    v.tensor_mul(uu[:, :], dl[:, oc, :], Bb[:, :])
                    v.tensor_tensor_scan(hh[:, :], dA[:, :], uu[:, :],
                                         hini[:, oc, n:n + 1], ALU.mult, ALU.add)
                    p.vsf(v, v.tensor_copy(hini[:, oc, n:n + 1], hh[:, SEG - 1:SEG]))
                    v.tensor_mul(tmp[:, :], hh[:, :], Cb[:, :])
                    if n == 0:
                        ya = v.tensor_copy(yac[:, oc, :], tmp[:, :])
                    else:
                        ya = v.tensor_add(yac[:, oc, :], yac[:, oc, :], tmp[:, :])
                    p.inc(ya, "yacc", 1)

        @blk.vector
        def _(v, sg=sg):
            p.wait(v, "yacc")
            for oc in range(2):
                p.inc(v.scalar_tensor_tensor(yac[:, oc, :], lco[:, oc, :],
                                             Ds[:, oc:oc + 1], yac[:, oc, :],
                                             ALU.mult, ALU.add), "yfin", 1)

        @blk.sync
        def _(sy, sg=sg, ssl=ssl):
            p.wait(sy, "yfin")
            p.inc(sy.dma_start(out=y_o.rearrange("(c p) t -> p c t", p=128)[:, :, ssl],
                               in_=yac[:, :, :]), "yw", 16)

    @blk.sync
    def _(sy):
        p.wait(sy, "yw")
    return p



def build_launch_F(debug_yg=False, skip_l3=False, skip_moe=False):
    p = Prog()
    x2_d = p.din("x2", [D, TPC])
    ys_d = p.din("yssm", [D, TPC])
    gw_d = p.din("gate_wT", [D, 8])
    ew_g = p.din("e_gate", [8, D, 1408])
    ew_u = p.din("e_up", [8, D, 1408])
    ew_d = p.din("e_down", [8, 1408, D])
    swg_d = p.din("sw_gate", [D, 2816])
    swu_d = p.din("sw_up", [D, 2816])
    swd_d = p.din("sw_down", [2816, D])
    wout_d = p.din("w_out", [D, D])
    bout_d = p.din("b_out", [128, DC])
    lng_d = p.din("ln_g", [128, DC])
    lnb_d = p.din("ln_b", [128, DC])
    wq3_d = p.din("wq3T", [D, D])
    wk3_d = p.din("wk3T", [D, D])
    wv3_d = p.din("wv3T", [D, D])
    wqi_d = p.din("wqiT", [D, D])
    wki_d = p.din("wkiT", [D, D])
    wvi_d = p.din("wviT", [D, D])
    bqi_d = p.din("bqi", [128, DC])
    bki_d = p.din("bki", [128, DC])
    bvi_d = p.din("bvi", [1, D])
    cos3_d = p.din("cos3", [D, TPC])
    sin3_d = p.din("sin3", [D, TPC])
    id_d = p.din("ident_in", [128, 128])

    x3_o = p.dout("x3", [D, TPC])
    xn3_o = p.dout("xn3", [D, TPC])
    q3_o = p.dout("q3h", [D, TPC], BF16)
    k3_o = p.dout("k3h", [D, TPC], BF16)
    v3_o = p.dout("v3h", [TPC, D], BF16)
    qn3_o = p.dout("qn3", [NH, TPC])
    kn3_o = p.dout("kn3", [NH, TPC])

    comb_dram = p.dtmp("combdram", [8, TPC])
    xn2_dram = p.dtmp("xn2d", [D, TPC])
    yg_dbg = p.dout("ygdbg", [D, TPC], BF16) if debug_yg else None
    s2_dram = p.dtmp("s2d", [D, TPC])
    dbgb_d = p.dtmp("dbgb", [256, TPC], BF16)
    g2_dram = p.dtmp("g2d", [D, TPC])

    xt = p.sb("xt", [128, DC, TT])
    xf = p.sb("xf", [128, DC, TT])
    yst = p.sb("yst", [128, DC, TT])
    xn = p.sb("xn", [128, DC, TPC], BF16)
    yg = p.sb("yg", [128, DC, TPC], BF16)
    hsb = p.sb("hsb", [128, 22, TT], BF16)
    stg = yst[:, 3:5, :]
    sq = p.sb("sq", [128, 1, TT])
    srow = p.sb("srow", [1, 2, TT])
    ones = p.sb("ones", [128, 2])
    cst = p.sb("cst", [128, 2])
    tmp = p.sb("tmpa", [128, TT])
    tmp2 = p.sb("tmpb", [128, 8])
    ident = p.sb("ident", [128, 128])
    wbuf = p.sb("wbuf", [128, 2, DC, 128], BF16)
    wbuf2 = p.sb("wbuf2", [128, 2, DC, 128], BF16)
    ewsb = p.sb("ewsb", [128, DC, 1408], BF16)
    dnsb = p.sb("dnsb", [128, 11, 1024], BF16)
    wbuf3 = dnsb.rearrange("p k m -> p (k m)").rearrange(
        "p (a b) -> p a b", b=128)[:, 0:22, :]
    zf = yst[:, 1:3, :]
    logit = p.sb("logit", [128, NTC, 8])
    lsc = p.sb("lsc", [128, 8])
    comb_tm = p.sb("comb_tm", [128, NTC, 8])
    comb_st = p.sb("comb_st", [8, 128])
    comb_row = p.sb("comb_row", [1, 1, TT])
    rwsb = p.sb("rwsb", [128, DC, 8], BF16)
    gb = p.sb("gb", [128, 5, DC])
    bvrow = p.sb("bvrow", [1, D])
    bvb = p.sb("bvb", [128, DC, 128])
    cs3 = stg
    eons = p.sb("eons", [128, 2])
    n2r = yst[0:2, 0, 0:TT]
    rpb = xf.rearrange("p c t -> p (c t)").bitcast(BF16)[:, 0:TT]

    P0 = p.ps("P0", [128, TT])
    P1 = p.ps("P1", [128, TT])
    P2 = p.ps("P2", [128, TT])
    P3 = p.ps("P3", [128, TT])
    P4 = p.ps("P4", [128, TT])
    P5 = p.ps("P5", [128, TT])
    PTf = p.ps("PTf", [8, 128])

    blk = p.block()

    @blk.vector
    def _(v):
        v.memset(ones[:, :], 1.0)
        v.memset(cst[:, 0:1], 1e-5)
        v.memset(cst[:, 1:2], 1e-20)
        v.memset(eons[:, :], 0.0)
        v.memset(eons[0:64, 0:1], 1.0)
        p.inc(v.memset(eons[64:128, 1:2], 1.0), "init", 1)

    @blk.scalar
    def _(s):
        p.wait(s, "init")

    @blk.tensor
    def _(t):
        p.wait(t, "init")

    @blk.sync
    def _(sy):
        p.inc(sy.dma_start(out=ident[:, :], in_=id_d[:, :]), "wlds", 16)

    @blk.gpsimd
    def _(g):
        p.inc(g.dma_start(out=gb[:, 0, :], in_=lng_d[:, :]), "wld", 16)
        p.inc(g.dma_start(out=gb[:, 1, :], in_=lnb_d[:, :]), "wld", 16)
        p.inc(g.dma_start(out=gb[:, 2, :], in_=bout_d[:, :]), "wld", 16)
        p.inc(g.dma_start(out=gb[:, 3, :], in_=bqi_d[:, :]), "wld", 16)
        p.inc(g.dma_start(out=gb[:, 4, :], in_=bki_d[:, :]), "wld", 16)
        p.inc(g.dma_start(out=bvrow[0:1, :], in_=bvi_d[:, :]), "wld", 16)
        p.inc(g.dma_start(out=rwsb[:, :, :],
                          in_=gw_d.rearrange("(c p) e -> p c e", p=128)), "wld", 16)

    for it in range(NTT):
        tsl = slice(it * TT, (it + 1) * TT)

        @blk.sync
        def _(sy, it=it, tsl=tsl):
            if it >= 1:
                p.wait(sy, "ygd", it)
                p.wait(sy, "xfw", 16 * it)
            p.inc(sy.dma_start(out=xt[:, :, :],
                               in_=x2_d.rearrange("(c p) t -> p c t", p=128)[:, :, tsl]),
                  "ln_xld", 16)
            p.inc(sy.dma_start(out=yst[:, :, :],
                               in_=ys_d.rearrange("(c p) t -> p c t", p=128)[:, :, tsl]),
                  "ysl", 16)

        emit_ln_tile(p, blk, f"f{it}", xt, [xn[:, c, tsl] for c in range(DC)],
                     sq, srow, P2, P3, P4, P5, ones, tmp=tmp, eps_ap=cst[0:1, 0:1],
                     xn_f32=[xf[:, c, :] for c in range(DC)])

        @blk.sync
        def _(sy, it=it, tsl=tsl):
            p.wait(sy, "ln_xn")
            p.inc(sy.dma_start(out=xn2_dram.rearrange("(c p) t -> p c t", p=128)[:, :, tsl],
                               in_=xf[:, :, :]), "xfw", 16)

        @blk.vector
        def _(v, it=it):
            p.wait(v, "ln_xn")
            v.wait_ge(p.sem("xfw"), 16 * (it + 1))
            for c in range(DC):
                cp = v.tensor_scalar(xf[:, c, :], xf[:, c, :], gb[:, 0, c:c + 1],
                                     gb[:, 1, c:c + 1], ALU.mult, ALU.add)
            p.inc(cp, "xiv", 1)

        @blk.scalar
        def _(s, it=it):
            p.wait(s, "xiv")
            s.wait_ge(p.sem("xfw"), 16 * (it + 1))
            for c in range(DC):
                ac = s.activation(out=xf[:, c, :], in_=xf[:, c, :], func=AF.Silu)
            p.inc(ac, "gsl", 1)

        @blk.vector
        def _(v, it=it, tsl=tsl):
            p.wait(v, "gsl")
            v.wait_ge(p.sem("ysl"), 16 * (it + 1))
            for c in range(DC):
                v.tensor_mul(tmp[:, :], yst[:, c, :], xf[:, c, :])
                cp = v.tensor_copy(yg[:, c, tsl], tmp[:, :])
            p.inc(cp, "ygd", 1)

        if debug_yg:
            @blk.sync
            def _(sy, it=it, tsl=tsl):
                p.wait(sy, "ygd")
                p.inc(sy.dma_start(out=yg_dbg.rearrange("(c p) t -> p c t", p=128)[:, :, tsl],
                                   in_=yg[:, :, tsl]), "ygw", 16)

    for it in range(NTT):
        @blk.tensor
        def _(t, it=it):
            p.wait(t, "ln_xn")
            p.wait(t, "wld")
            for tc in range(4):
                col0 = it * TT + tc * 128
                idx = it * 4 + tc
                if idx > 0:
                    t.wait_ge(p.sem("rtev"), idx)
                for c in range(DC):
                    mm = t.matmul(P4[:, 0:8], xn[:, c, col0:col0 + 128],
                                  rwsb[:, c, :], start=(c == 0), stop=(c == DC - 1))
                p.inc(mm, "rtmm", 1)

        @blk.vector
        def _(v, it=it):
            for tc in range(4):
                v.wait_ge(p.sem("rtmm"), it * 4 + tc + 1)
                p.inc(v.tensor_copy(logit[:, it * 4 + tc, :], P4[:, 0:8]), "rtev", 1)

    for tc in range(NTC):
        @blk.vector
        def _(v, tc=tc):
            if tc == 0:
                p.wait(v, "rtev")
            else:
                v.wait_ge(p.sem("rex"), tc)
            p.inc(v.reduce_max(lsc[:, 0:1], logit[:, tc, :], axis=AX.X, negate=True),
                  "rmx", 1)

        @blk.scalar
        def _(s, tc=tc):
            s.wait_ge(p.sem("rmx"), tc + 1)
            p.inc(s.activation(out=comb_tm[:, tc, :], in_=logit[:, tc, :], func=AF.Exp,
                               bias=lsc[:, 0:1], accum_out=lsc[:, 1:2]), "rex", 1)

        @blk.vector
        def _(v, tc=tc):
            v.wait_ge(p.sem("rex"), tc + 1)
            S = comb_tm[:, tc, :]
            M = logit[:, tc, :]
            p.vsf(v, v.reciprocal(lsc[:, 2:3], lsc[:, 1:2]))
            p.vsf(v, v.tensor_scalar_mul(S, S, lsc[:, 2:3]))
            p.vsf(v, v.reduce_max(lsc[:, 3:4], S, axis=AX.X, negate=True))
            p.vsf(v, v.tensor_scalar(M, S, lsc[:, 3:4], 0.0, ALU.add, ALU.is_ge))
            p.vsf(v, v.scalar_tensor_tensor(tmp[:, 0:8], M, -1e30, S, ALU.mult, ALU.add))
            p.vsf(v, v.reduce_max(lsc[:, 4:5], tmp[:, 0:8], axis=AX.X, negate=True))
            p.vsf(v, v.tensor_scalar(tmp2[:, 0:8], tmp[:, 0:8], lsc[:, 4:5], 0.0,
                                     ALU.add, ALU.is_ge))
            p.vsf(v, v.tensor_add(lsc[:, 5:6], lsc[:, 3:4], lsc[:, 4:5]))
            p.vsf(v, v.tensor_scalar(lsc[:, 5:6], lsc[:, 5:6], -1.0, 1e-20,
                                     ALU.mult, ALU.add))
            p.vsf(v, v.reciprocal(lsc[:, 5:6], lsc[:, 5:6]))
            p.vsf(v, v.tensor_scalar_mul(lsc[:, 5:6], lsc[:, 5:6], 2.5))
            p.vsf(v, v.tensor_scalar_mul(lsc[:, 6:7], lsc[:, 3:4], -1.0))
            p.vsf(v, v.tensor_mul(lsc[:, 6:7], lsc[:, 6:7], lsc[:, 5:6]))
            p.vsf(v, v.tensor_scalar_mul(M, M, lsc[:, 6:7]))
            p.vsf(v, v.tensor_scalar_mul(lsc[:, 6:7], lsc[:, 4:5], -1.0))
            p.vsf(v, v.tensor_mul(lsc[:, 6:7], lsc[:, 6:7], lsc[:, 5:6]))
            p.vsf(v, v.tensor_scalar_mul(tmp2[:, 0:8], tmp2[:, 0:8], lsc[:, 6:7]))
            p.inc(v.tensor_add(S, M, tmp2[:, 0:8]), "cmb", 1)

    @blk.tensor
    def _(t):
        p.wait(t, "cmb")
        p.wait(t, "wlds")
        for tc in range(NTC):
            if tc > 0:
                t.wait_ge(p.sem("ctv"), tc)
            p.inc(t.transpose(PTf[0:8, 0:128], comb_tm[:, tc, :], ident[:, :]), "ctm", 1)

    @blk.vector
    def _(v):
        for tc in range(NTC):
            v.wait_ge(p.sem("ctm"), tc + 1)
            if tc >= 1:
                v.wait_ge(p.sem("cbd"), 16 * tc)
            p.inc(v.tensor_copy(comb_st[0:8, :], PTf[0:8, 0:128]), "ctv", 1)

    @blk.sync
    def _(sy):
        for tc in range(NTC):
            sy.wait_ge(p.sem("ctv"), tc + 1)
            p.inc(sy.dma_start(out=comb_dram[:, tc * 128:(tc + 1) * 128],
                               in_=comb_st[0:8, :]), "cbd", 16)

    NF2 = 11
    NREG = NTT * DC

    for e in range((0 if skip_moe else 8)):
        @blk.gpsimd
        def _(g, e=e):
            if e > 0:
                g.wait_ge(p.sem("dmm"), e * NREG)
            p.inc(g.dma_start(out=ewsb[:, :, :],
                              in_=ew_g[e].rearrange("(c p) f -> p c f", p=128)),
                  "mwl", 16)
            p.inc(g.dma_start(out=dnsb[:, :, :],
                              in_=ew_d[e].rearrange("(k p) m -> p k m", p=128)),
                  "mdl", 16)
            for it in range(NTT):
                for fc in range(NF2):
                    idx = (e * NTT + it) * NF2 + fc
                    if idx >= 2:
                        g.wait_ge(p.sem("hmm"), idx - 1)
                    p.inc(g.dma_start(out=wbuf2[:, idx % 2, :, :],
                                      in_=ew_u[e].rearrange("(c p) f -> p c f", p=128)
                                      [:, :, fc * 128:(fc + 1) * 128]), "mul", 16)
                for oc in range(DC):
                    idx = (e * NTT + it) * DC + oc
                    g.wait_ge(p.sem("dev"), idx + 1)
                    if idx >= NREG:
                        g.wait_ge(p.sem("gad"), 16 * (idx - NREG + 1))
                    p.inc(g.dma_start(
                        out=g2_dram.rearrange("(c p) t -> p c t", p=128)
                        [:, oc, it * TT:(it + 1) * TT],
                        in_=stg[:, idx % 2, :],
                        accum_op=(ALU.bypass if e == 0 else ALU.add)), "gad", 16)


        @blk.sync
        def _(sy, e=e):
            for it in range(NTT):
                gi = e * NTT + it
                if gi >= 1:
                    sy.wait_ge(p.sem("cbc"), gi)
                sy.wait_ge(p.sem("cbd"), 16 * NTC)
                p.inc(sy.dma_start(out=comb_row[0:1, 0, :],
                                   in_=comb_dram[e:e + 1, it * TT:(it + 1) * TT]),
                      "crl", 16)

        @blk.tensor
        def _(t, e=e):
            t.wait_ge(p.sem("mwl"), 16 * (e + 1))
            t.wait_ge(p.sem("mdl"), 16 * (e + 1))
            if e == 0:
                p.wait(t, "ygd")
            for it in range(NTT):
                gi = e * NTT + it
                if gi * NF2 > 0:
                    t.wait_ge(p.sem("hev"), gi * NF2)
                t.wait_ge(p.sem("crl"), 16 * (gi + 1))
                p.inc(t.matmul(P5[:, 0:TT], ones[0:1, 0:1].to_broadcast([1, 128]),
                               comb_row[0:1, 0, :], start=True, stop=True), "cbc", 1)
                for fc in range(NF2):
                    idx = gi * NF2 + fc
                    if idx >= 2:
                        t.wait_ge(p.sem("hev"), idx - 1)
                    t.wait_ge(p.sem("mul"), 16 * (idx + 1))
                    psG = P0 if fc % 2 == 0 else P2
                    psU = P1 if fc % 2 == 0 else P3
                    for c in range(DC):
                        t.matmul(psG[:, :], ewsb[:, c, fc * 128:(fc + 1) * 128],
                                 xn[:, c, it * TT:(it + 1) * TT],
                                 start=(c == 0), stop=(c == DC - 1))
                    for c in range(DC):
                        mm = t.matmul(psU[:, :], wbuf2[:, idx % 2, c, :],
                                      xn[:, c, it * TT:(it + 1) * TT],
                                      start=(c == 0), stop=(c == DC - 1))
                    p.inc(mm, "hmm", 1)
                t.wait_ge(p.sem("hev"), (gi + 1) * NF2)
                for oc in range(DC):
                    idx = gi * DC + oc
                    if idx >= 1:
                        t.wait_ge(p.sem("dev"), idx)
                    for k in range(NF2):
                        mm = t.matmul(P4[:, :], dnsb[:, k, oc * 128:(oc + 1) * 128],
                                      hsb[:, k, :], start=(k == 0), stop=(k == NF2 - 1))
                    p.inc(mm, "dmm", 1)

        @blk.scalar
        def _(s, e=e):
            for it in range(NTT):
                for fc in range(NF2):
                    idx = (e * NTT + it) * NF2 + fc
                    s.wait_ge(p.sem("hmm"), idx + 1)
                    if idx >= 1:
                        s.wait_ge(p.sem("hev"), idx)
                    psG = P0 if fc % 2 == 0 else P2
                    p.inc(s.activation(out=sq[:, 0, :], in_=psG[:, :], func=AF.Silu),
                          "slu", 1)

        @blk.vector
        def _(v, e=e):
            for it in range(NTT):
                v.wait_ge(p.sem("cbc"), e * NTT + it + 1)
                for fc in range(NF2):
                    idx = (e * NTT + it) * NF2 + fc
                    v.wait_ge(p.sem("slu"), idx + 1)
                    psU = P1 if fc % 2 == 0 else P3
                    v.tensor_mul(tmp[:, :], sq[:, 0, :], psU[:, :])
                    v.tensor_mul(tmp[:, :], tmp[:, :], P5[:, 0:TT])
                    p.inc(v.tensor_copy(hsb[:, fc, :], tmp[:, :]), "hev", 1)
                for oc in range(DC):
                    idx = (e * NTT + it) * DC + oc
                    if idx >= 2:
                        v.wait_ge(p.sem("gad"), 16 * (idx - 1))
                    v.wait_ge(p.sem("dmm"), idx + 1)
                    p.inc(v.tensor_copy(stg[:, idx % 2, :], P4[:, :]), "dev", 1)


    for it in range(0 if skip_moe else NTT):
        @blk.gpsimd
        def _(g, it=it):
            if it == 0:
                g.wait_ge(p.sem("dmm"), 8 * NREG)
            for fc in range(22):
                idx = it * 22 + fc
                if idx >= 2:
                    g.wait_ge(p.sem("shmm"), idx - 1)
                p.inc(g.dma_start(
                    out=wbuf[:, idx % 2, :, :],
                    in_=swg_d.rearrange("(c p) f -> p c f", p=128)[:, :, fc * 128:(fc + 1) * 128]),
                    "swl", 16)
                p.inc(g.dma_start(
                    out=wbuf2[:, idx % 2, :, :],
                    in_=swu_d.rearrange("(c p) f -> p c f", p=128)[:, :, fc * 128:(fc + 1) * 128]),
                    "swl", 16)

        @blk.tensor
        def _(t, it=it):
            for fc in range(22):
                idx = it * 22 + fc
                t.wait_ge(p.sem("swl"), 32 * (idx + 1))
                if idx >= 2:
                    t.wait_ge(p.sem("shev"), idx - 1)
                psG = P0 if fc % 2 == 0 else P2
                psU = P1 if fc % 2 == 0 else P3
                for c in range(DC):
                    t.matmul(psG[:, :], wbuf[:, idx % 2, c, :],
                             xn[:, c, it * TT:(it + 1) * TT],
                             start=(c == 0), stop=(c == DC - 1))
                for c in range(DC):
                    mm = t.matmul(psU[:, :], wbuf2[:, idx % 2, c, :],
                                  xn[:, c, it * TT:(it + 1) * TT],
                                  start=(c == 0), stop=(c == DC - 1))
                p.inc(mm, "shmm", 1)

        @blk.scalar
        def _(s, it=it):
            for fc in range(22):
                idx = it * 22 + fc
                s.wait_ge(p.sem("shmm"), idx + 1)
                if idx >= 1:
                    s.wait_ge(p.sem("shev"), idx)
                psG = P0 if fc % 2 == 0 else P2
                p.inc(s.activation(out=sq[:, 0, :], in_=psG[:, :], func=AF.Silu),
                      "shsl", 1)

        @blk.vector
        def _(v, it=it):
            for fc in range(22):
                idx = it * 22 + fc
                v.wait_ge(p.sem("shsl"), idx + 1)
                if it >= 1 and fc == 0:
                    v.wait_ge(p.sem("sdmm"), it * DC)
                psU = P1 if fc % 2 == 0 else P3
                v.tensor_mul(tmp[:, :], sq[:, 0, :], psU[:, :])
                p.inc(v.tensor_copy(hsb[:, fc, :], tmp[:, :]), "shev", 1)

        @blk.gpsimd
        def _(g, it=it):
            for oc in range(DC):
                idx2 = it * DC + oc
                gidx = 8 * NREG + idx2
                if idx2 > 0:
                    g.wait_ge(p.sem("sdmm"), idx2)
                p.inc(g.dma_start(
                    out=wbuf3[:, :, :],
                    in_=swd_d.rearrange("(k p) m -> p k m", p=128)[:, :, oc * 128:(oc + 1) * 128]),
                    "sdl", 16)
                g.wait_ge(p.sem("sdev"), idx2 + 1)
                g.wait_ge(p.sem("gad"), 16 * (gidx - NREG + 1))
                p.inc(g.dma_start(
                    out=g2_dram.rearrange("(c p) t -> p c t", p=128)
                    [:, oc, it * TT:(it + 1) * TT],
                    in_=stg[:, gidx % 2, :], accum_op=ALU.add), "gad", 16)

        @blk.tensor
        def _(t, it=it):
            for oc in range(DC):
                idx2 = it * DC + oc
                t.wait_ge(p.sem("sdl"), 16 * (idx2 + 1))
                t.wait_ge(p.sem("shev"), it * 22 + 22)
                if idx2 >= 1:
                    t.wait_ge(p.sem("sdev"), idx2)
                for k in range(22):
                    mm = t.matmul(P4[:, :], wbuf3[:, k, :], hsb[:, k, :],
                                  start=(k == 0), stop=(k == 21))
                p.inc(mm, "sdmm", 1)

        @blk.vector
        def _(v, it=it):
            for oc in range(DC):
                idx2 = it * DC + oc
                v.wait_ge(p.sem("sdmm"), idx2 + 1)
                gidx = 8 * NREG + idx2
                if gidx >= 2:
                    v.wait_ge(p.sem("gad"), 16 * (gidx - 1))
                p.inc(v.tensor_copy(stg[:, gidx % 2, :], P4[:, :]), "sdev", 1)

    for it in range(NTT):
        tsl = slice(it * TT, (it + 1) * TT)

        @blk.sync
        def _(sy, it=it, tsl=tsl):
            p.wait(sy, "gad")
            p.wait(sy, "sdmm")
            if it >= 1:
                p.wait(sy, "x3w", 16 * it)
                p.wait(sy, "ln_xn")
            if it >= 1:
                p.wait(sy, "xnw", 16 * it)
            p.inc(sy.dma_start(out=xt[:, :, :],
                               in_=xn2_dram.rearrange("(c p) t -> p c t", p=128)[:, :, tsl]),
                  "f3a", 16)
            p.inc(sy.dma_start(out=yst[:, :, :],
                               in_=x2_d.rearrange("(c p) t -> p c t", p=128)[:, :, tsl]),
                  "f3b", 16)
            p.inc(sy.dma_start(out=xf[:, :, :],
                               in_=g2_dram.rearrange("(c p) t -> p c t", p=128)[:, :, tsl]),
                  "f3c", 16)

        @blk.scalar
        def _(s, it=it):
            s.wait_ge(p.sem("f3c"), 16 * (it + 1))
            if it >= 1:
                p.wait(s, "x3c")
            for c in range(DC):
                ac = s.activation(out=xf[:, c, :], in_=xf[:, c, :], func=AF.Silu)
            p.inc(ac, "g2s", 1)

        @blk.gpsimd
        def _(g, it=it):
            for oc in range(DC):
                idx = it * DC + oc
                if idx >= 2:
                    g.wait_ge(p.sem("s2mm"), idx - 1)
                p.inc(g.dma_start(out=wbuf[:, idx % 2, :, :],
                                  in_=wout_d.rearrange("(c p) m -> p c m", p=128)
                                  [:, :, oc * 128:(oc + 1) * 128]), "wotl", 16)

        for oc in range(DC):
            idx = it * DC + oc

            @blk.tensor
            def _(t, it=it, oc=oc, idx=idx, tsl=tsl):
                t.wait_ge(p.sem("wotl"), 16 * (idx + 1))
                if idx == 0:
                    p.wait(t, "sdmm")
                if idx >= 2:
                    t.wait_ge(p.sem("s2ev"), idx - 1)
                ps = P0 if idx % 2 == 0 else P1
                for c in range(DC):
                    mm = t.matmul(ps[:, :], wbuf[:, idx % 2, c, :], yg[:, c, tsl],
                                  start=(c == 0), stop=(c == DC - 1))
                p.inc(mm, "s2mm", 1)

            @blk.vector
            def _(v, it=it, oc=oc, idx=idx):
                v.wait_ge(p.sem("s2mm"), idx + 1)
                v.wait_ge(p.sem("f3a"), 16 * (it + 1))
                v.wait_ge(p.sem("f3b"), 16 * (it + 1))
                p.wait(v, "g2s")
                ps = P0 if idx % 2 == 0 else P1
                v.tensor_scalar(tmp[:, :], ps[:, :], gb[:, 2, oc:oc + 1], None, ALU.add)
                p.inc(v.engine_nop(), "s2ev", 1)
                v.tensor_add(tmp[:, :], tmp[:, :], xt[:, oc, :])
                v.tensor_mul(tmp[:, :], tmp[:, :], xf[:, oc, :])
                p.inc(v.tensor_add(yst[:, oc, :], yst[:, oc, :], tmp[:, :]), "x3c", 1)

        @blk.sync
        def _(sy, it=it, tsl=tsl):
            p.wait(sy, "x3c")
            p.inc(sy.dma_start(out=x3_o.rearrange("(c p) t -> p c t", p=128)[:, :, tsl],
                               in_=yst[:, :, :]), "x3w", 16)

        emit_ln_tile(p, blk, f"g{it}", yst, [xn[:, c, tsl] for c in range(DC)],
                     sq, srow, P2, P3, P4, P5, ones, tmp=tmp, eps_ap=cst[0:1, 0:1],
                     xn_f32=[xf[:, c, :] for c in range(DC)],
                     extra_wait=("x3c",))

        @blk.sync
        def _(sy, it=it, tsl=tsl):
            p.wait(sy, "ln_xn")
            p.inc(sy.dma_start(out=xn3_o.rearrange("(c p) t -> p c t", p=128)[:, :, tsl],
                               in_=xf[:, :, :]), "xnw", 16)

    if skip_l3:
        @blk.sync
        def _(sy):
            p.wait(sy, "x3w")
        return p

    @blk.tensor
    def _(t):
        p.wait(t, "wld")
        p.wait(t, "ln_xn")
        for half in range(2):
            if half == 1:
                p.wait(t, "bvc", 1)
            p.inc(t.matmul(P5[:, :], ones[0:1, 0:1].to_broadcast([1, 128]),
                           bvrow[0:1, half * TT:(half + 1) * TT], start=True, stop=True),
                  "bvm", 1)

    @blk.vector
    def _(v):
        for half in range(2):
            v.wait_ge(p.sem("bvm"), half + 1)
            p.inc(v.tensor_copy(bvb.rearrange("p c f -> p (c f)")[:, half * TT:(half + 1) * TT],
                                P5[:, :]), "bvc", 1)

    wvi_sb = ewsb[:, :, 0:1024]

    for which, w3_d, wi_d, bcol, out_d, n2_d in (
            ("q", wq3_d, wqi_d, 3, q3_o, qn3_o),
            ("k", wk3_d, wki_d, 4, k3_o, kn3_o),
            ("v", wv3_d, wvi_d, -1, v3_o, None)):

        @blk.gpsimd
        def _(g, which=which, w3_d=w3_d):
            for it in range(NTT):
                for cp in range(4):
                    for half in range(2):
                        idx = (it * 4 + cp) * 2 + half
                        if idx == 0 and which == "k":
                            g.wait_ge(p.sem("qpm"), 32)
                        if idx == 0 and which == "v":
                            g.wait_ge(p.sem("kpm"), 32)
                        if idx >= 2:
                            g.wait_ge(p.sem(which + "pm"), idx - 1)
                        oc = cp if half == 0 else cp + 4
                        p.inc(g.dma_start(out=wbuf2[:, idx % 2, :, :],
                                          in_=w3_d.rearrange("(c p) m -> p c m", p=128)
                                          [:, :, oc * 128:(oc + 1) * 128]),
                              which + "pl", 16)

        for it in range(NTT):
            tsl = slice(it * TT, (it + 1) * TT)
            for cp in range(4):
                pidx = it * 4 + cp

                @blk.sync
                def _(sy, it=it, cp=cp, pidx=pidx, tsl=tsl, which=which):
                    if pidx >= 1:
                        p.wait(sy, which + "rp", pidx)
                    elif which == "q":
                        p.wait(sy, "ln_xn")
                    elif which == "k":
                        p.wait(sy, "qrp")
                    else:
                        p.wait(sy, "krp")
                    p.inc(sy.dma_start(out=cs3[:, 0, :],
                                       in_=cos3_d[cp * 128:(cp + 1) * 128, tsl]),
                          which + "cs", 16)
                    p.inc(sy.dma_start(out=cs3[:, 1, :],
                                       in_=sin3_d[cp * 128:(cp + 1) * 128, tsl]),
                          which + "cs", 16)

                for half in range(2):
                    idx = pidx * 2 + half

                    @blk.tensor
                    def _(t, it=it, cp=cp, half=half, idx=idx, tsl=tsl, which=which):
                        t.wait_ge(p.sem(which + "pl"), 16 * (idx + 1))
                        if which == "q" and idx == 0:
                            p.wait(t, "ln_xn")
                            p.wait(t, "s2mm")
                        if which == "k" and idx <= 1:
                            t.wait_ge(p.sem("qpe"), 32)
                        if which == "v" and idx <= 1:
                            t.wait_ge(p.sem("kpe"), 32)
                        if idx >= 2:
                            t.wait_ge(p.sem(which + "pe"), idx - 1)
                        ps = P0 if idx % 2 == 0 else P1
                        for c in range(DC):
                            mm = t.matmul(ps[:, :], wbuf2[:, idx % 2, c, :],
                                          xn[:, c, tsl], start=(c == 0), stop=(c == DC - 1))
                        p.inc(mm, which + "pm", 1)

                    @blk.vector
                    def _(v, half=half, idx=idx, which=which):
                        v.wait_ge(p.sem(which + "pm"), idx + 1)
                        if idx == 0 and which == "q":
                            v.wait_ge(p.sem("x3w"), 16 * NTT)
                        ps = P0 if idx % 2 == 0 else P1
                        p.inc(v.tensor_copy(zf[:, half, :], ps[:, :]), which + "pe", 1)

                @blk.vector
                def _(v, it=it, cp=cp, pidx=pidx, tsl=tsl, which=which):
                    v.wait_ge(p.sem(which + "pe"), 2 * (pidx + 1))
                    v.wait_ge(p.sem(which + "cs"), 32 * (pidx + 1))
                    v.tensor_mul(tmp[:, :], zf[:, 0, :], cs3[:, 0, :])
                    v.tensor_mul(sq[:, 0, :], zf[:, 1, :], cs3[:, 1, :])
                    v.tensor_add(tmp[:, :], tmp[:, :], sq[:, 0, :])
                    v.tensor_copy(yg[:, cp, tsl], tmp[:, :])
                    v.tensor_mul(tmp[:, :], zf[:, 1, :], cs3[:, 0, :])
                    v.tensor_mul(sq[:, 0, :], zf[:, 0, :], cs3[:, 1, :])
                    v.tensor_sub(tmp[:, :], tmp[:, :], sq[:, 0, :])
                    p.inc(v.tensor_copy(yg[:, cp + 4, tsl], tmp[:, :]), which + "rp", 1)

        if debug_yg and which == "q":
            @blk.gpsimd
            def _(g):
                p.wait(g, "qrp")
                p.inc(g.dma_start(out=yg_dbg.rearrange("(c p) t -> p c t", p=128),
                                  in_=yg[:, :, :]), "ydq", 16)

        if which != "v":
            @blk.gpsimd
            def _(g, which=which, wi_d=wi_d):
                for oc in range(DC):
                    for it in range(NTT):
                        idx = oc * NTT + it
                        if idx == 0:
                            if which == "q":
                                g.wait_ge(p.sem("s2mm"), 32)
                            else:
                                g.wait_ge(p.sem("qim"), 32)
                        if idx >= 2:
                            g.wait_ge(p.sem(which + "im"), idx - 1)
                        p.inc(g.dma_start(out=wbuf[:, idx % 2, :, :],
                                          in_=wi_d.rearrange("(c p) m -> p c m", p=128)
                                          [:, :, oc * 128:(oc + 1) * 128]),
                              which + "il", 16)

            for oc in range(DC):
                for it in range(NTT):
                    idx = oc * NTT + it
                    tsl = slice(it * TT, (it + 1) * TT)

                    @blk.tensor
                    def _(t, oc=oc, it=it, idx=idx, tsl=tsl, which=which):
                        t.wait_ge(p.sem(which + "il"), 16 * (idx + 1))
                        if idx == 0:
                            p.wait(t, which + "rp")
                        if idx >= 2:
                            t.wait_ge(p.sem(which + "ie"), idx - 1)
                        ps = P2 if idx % 2 == 0 else P3
                        for c in range(DC):
                            mm = t.matmul(ps[:, :], wbuf[:, idx % 2, c, :],
                                          yg[:, c, tsl], start=(c == 0), stop=(c == DC - 1))
                        p.inc(mm, which + "im", 1)

                    @blk.vector
                    def _(v, oc=oc, idx=idx, which=which, bcol=bcol):
                        v.wait_ge(p.sem(which + "im"), idx + 1)
                        if idx >= 1:
                            p.wait(v, which + "sq", idx)
                            p.wait(v, which + "ow", 16 * idx)
                        ps = P2 if idx % 2 == 0 else P3
                        v.tensor_scalar(tmp[:, :], ps[:, :], gb[:, bcol, oc:oc + 1],
                                        None, ALU.add)
                        p.inc(v.engine_nop(), which + "ie", 1)
                        p.inc(v.tensor_copy(rpb[:, :], tmp[:, :]), which + "rb", 1)

                    @blk.scalar
                    def _(s, idx=idx, which=which):
                        s.wait_ge(p.sem(which + "rb"), idx + 1)
                        p.inc(s.activation(out=sq[:, 0, :], in_=tmp[:, :],
                                           func=AF.Square), which + "sq", 1)

                    @blk.tensor
                    def _(t, idx=idx, which=which):
                        t.wait_ge(p.sem(which + "sq"), idx + 1)
                        if idx >= 1:
                            p.wait(t, which + "nc", idx)
                        p.inc(t.matmul(P4[0:2, :], eons[:, :], sq[:, 0, :],
                                       start=True, stop=True), which + "n2", 1)

                    @blk.vector
                    def _(v, idx=idx, which=which):
                        v.wait_ge(p.sem(which + "n2"), idx + 1)
                        if idx >= 1:
                            p.wait(v, which + "nw", 16 * idx)
                        p.inc(v.tensor_copy(n2r[:, :], P4[0:2, :]), which + "nc", 1)

                    @blk.scalar
                    def _(s, oc=oc, idx=idx, tsl=tsl, which=which, out_d=out_d, n2_d=n2_d):
                        s.wait_ge(p.sem(which + "nc"), idx + 1)
                        p.inc(s.dma_start(
                            out=n2_d.rearrange("(o j) t -> j o t", j=2)[:, oc, tsl],
                            in_=n2r[:, :]), which + "nw", 16)
                        s.wait_ge(p.sem(which + "rb"), idx + 1)
                        p.inc(s.dma_start(
                            out=out_d.rearrange("(c p) t -> p c t", p=128)[:, oc, tsl],
                            in_=rpb[:, :]), which + "ow", 16)
        else:
            @blk.gpsimd
            def _(g):
                g.wait_ge(p.sem("vrp"), 16)
                p.inc(g.dma_start(out=wvi_sb,
                                  in_=wvi_d.rearrange("(c p) m -> p c m", p=128)),
                      "vil", 16)

            for tc in range(NTC):
                for half in range(2):
                    idx = tc * 2 + half

                    @blk.tensor
                    def _(t, tc=tc, half=half, idx=idx):
                        t.wait_ge(p.sem("vil"), 16)
                        if idx == 0:
                            p.wait(t, "vrp")
                            p.wait(t, "bvc")
                        if idx >= 2:
                            t.wait_ge(p.sem("vte"), idx - 1)
                        ps = P2 if idx % 2 == 0 else P3
                        for c in range(DC):
                            mm = t.matmul(ps[:, :], yg[:, c, tc * 128:(tc + 1) * 128],
                                          wvi_sb[:, c, half * TT:(half + 1) * TT],
                                          start=(c == 0), stop=(c == DC - 1))
                        p.inc(mm, "vtm", 1)

                    @blk.vector
                    def _(v, half=half, idx=idx):
                        v.wait_ge(p.sem("vtm"), idx + 1)
                        if idx == 0:
                            v.wait_ge(p.sem("kow"), 16 * 32)
                        if idx >= 1:
                            p.wait(v, "vtw", 16 * idx)
                        ps = P2 if idx % 2 == 0 else P3
                        p.inc(v.tensor_tensor(rpb[:, :], ps[:, :],
                                              bvb.rearrange("p c f -> p (c f)")
                                              [:, half * TT:(half + 1) * TT], ALU.add),
                              "vte", 1)

                    @blk.scalar
                    def _(s, tc=tc, half=half, idx=idx):
                        s.wait_ge(p.sem("vte"), idx + 1)
                        p.inc(s.dma_start(out=v3_o[tc * 128:(tc + 1) * 128,
                                                   half * TT:(half + 1) * TT],
                                          in_=rpb[:, :]), "vtw", 16)

    @blk.sync
    def _(sy):
        p.wait(sy, "x3w")
        p.wait(sy, "xnw")
        p.wait(sy, "qow")
        p.wait(sy, "kow")
        p.wait(sy, "qnw")
        p.wait(sy, "knw")
        p.wait(sy, "vtw")
    return p



def build_launch_G():
    p = Prog()
    NCK = T // 128
    q3_d = p.din("q3", [NH, DH, TPC], BF16)
    kf_d = p.din("k3f", [NH, DH, T], BF16)
    vf_d = p.din("v3f", [NH, T, DH], BF16)
    qn_d = p.din("qn3", [NH, TPC])
    kn_d = p.din("kn3f", [NH, T])
    wo_d = p.din("woT", [D, D])
    bo_d = p.din("bo", [128, DC])
    msk_d = p.din("mask", [NTT, NCK, 128, TT], BF16)

    s3_o = p.dout("s3", [D, TPC])
    mb_dram = p.dtmp("mbd", [1, NH])

    khb = p.sb("khb", [64, T], BF16)
    vh = p.sb("vh", [128, NCK, DH + 1], BF16)
    qh = p.sb("qh", [64, TT], BF16)
    pbs = p.sb("pbs", [128, 2, TT], BF16)
    ofm = p.sb("ofm", [128, DC, TT], BF16)
    mskb = p.sb("mskb", [128, NCK, TT], BF16)
    onr = p.sb("onr", [1, TT])
    oev = p.sb("oev", [64, TT])
    s1t = p.sb("s1t", [128, TT])
    xf = p.sb("xf", [128, DC, TT])
    wst2 = p.sb("wst2", [128, 2, DC, 128], BF16)
    ones = p.sb("ones", [128, 2])
    cst = p.sb("cst", [128, 2])
    mrow = p.sb("mrow", [1, NH])
    mb = p.sb("mb", [128, NH])
    nsc = p.sb("nsc", [16, 4])
    gb = p.sb("gb", [128, 1, DC])
    qn_sb = p.sb("qn_sb", [16, TPC])
    kn_sb = p.sb("kn_sb", [16, TPC])

    P0 = p.ps("P0", [128, TT])
    P1 = p.ps("P1", [128, TT])
    P3 = p.ps("P3", [128, TT])
    P4 = p.ps("P4", [128, TT])
    PTf = p.ps("PTf", [128, 128])

    blk = p.block()

    @blk.vector
    def _(v):
        v.memset(ones[:, :], 1.0)
        v.memset(cst[:, 1:2], 1e-38)
        p.inc(v.memset(vh[:, :, DH:DH + 1], 1.0), "init", 1)

    @blk.scalar
    def _(s):
        p.wait(s, "init")

    @blk.tensor
    def _(t):
        p.wait(t, "init")

    @blk.gpsimd
    def _(g):
        p.wait(g, "init")
        p.inc(g.dma_start(out=gb[:, 0, :], in_=bo_d[:, :]), "wld", 16)

    @blk.sync
    def _(sy):
        p.inc(sy.dma_start(out=qn_sb[:, :], in_=qn_d[:, :]), "nld", 16)

    @blk.vector
    def _(v):
        p.wait(v, "nld")
        p.vsf(v, v.reduce_max(nsc[:, 0:1], qn_sb[:, :], axis=AX.X))

    for kc4 in range(4):
        @blk.sync
        def _(sy, kc4=kc4):
            if kc4 > 0:
                p.wait(sy, "nkm", kc4)
            p.inc(sy.dma_start(out=kn_sb[:, :],
                               in_=kn_d[:, kc4 * TPC:(kc4 + 1) * TPC]), "nld2", 16)

        @blk.vector
        def _(v, kc4=kc4):
            v.wait_ge(p.sem("nld2"), 16 * (kc4 + 1))
            p.vsf(v, v.reduce_max(nsc[:, 3:4], kn_sb[:, :], axis=AX.X))
            if kc4 == 0:
                p.inc(v.tensor_copy(nsc[:, 1:2], nsc[:, 3:4]), "nkm", 1)
            else:
                p.inc(v.tensor_max(nsc[:, 1:2], nsc[:, 1:2], nsc[:, 3:4]), "nkm", 1)

    @blk.vector
    def _(v):
        p.wait(v, "nkm")
        p.vsf(v, v.engine_nop())
        p.inc(v.tensor_mul(nsc[:, 2:3], nsc[:, 0:1], nsc[:, 1:2]), "nm1", 1)

    @blk.scalar
    def _(s):
        p.wait(s, "nm1")
        p.inc(s.activation(out=nsc[:, 3:4], in_=nsc[:, 2:3], func=AF.Sqrt), "nm2", 1)

    @blk.vector
    def _(v):
        p.wait(v, "nm2")
        p.inc(v.tensor_scalar_mul(nsc[:, 2:3], nsc[:, 3:4], -0.125), "nm3", 1)

    @blk.sync
    def _(sy):
        p.wait(sy, "nm3")
        p.inc(sy.dma_start(out=mb_dram.rearrange("o h -> (o h)")[None, :],
                           in_=nsc[:, 2:3]), "nm4", 16)
        sy.wait_ge(p.sem("nm4"), 16)
        p.inc(sy.dma_start(out=mrow[0:1, :], in_=mb_dram[0:1, :]), "nm5", 16)

    @blk.tensor
    def _(t):
        t.wait_ge(p.sem("nm5"), 16)
        p.inc(t.matmul(PTf[:, 0:NH], ones[0:1, 0:1].to_broadcast([1, 128]),
                       mrow[0:1, :], start=True, stop=True), "nm6", 1)

    @blk.vector
    def _(v):
        p.wait(v, "nm6")
        p.inc(v.tensor_copy(mb[:, :], PTf[:, 0:NH]), "nm7", 1)

    for tqb in range(NTT):
        tsl = slice(tqb * TT, (tqb + 1) * TT)

        @blk.sync
        def _(sy, tqb=tqb):
            if tqb >= 1:
                p.wait(sy, "mke", tqb * NH * NCK)
            p.inc(sy.dma_start(out=mskb[:, :, :],
                               in_=msk_d[tqb].rearrange("c p t -> p c t")), "mskl", 16)

        for h in range(NH):
            hidx = tqb * NH + h

            @blk.gpsimd
            def _(g, h=h, tqb=tqb, hidx=hidx):
                if hidx >= 1:
                    g.wait_ge(p.sem("avm"), hidx * NCK)
                else:
                    p.wait(g, "init")
                p.inc(g.dma_start(out=khb[:, :], in_=kf_d[h]), "khl", 16)
                p.inc(g.dma_start(out=vh[:, :, 0:DH],
                                  in_=vf_d[h].rearrange("(c p) d -> p c d", p=128)),
                      "khl", 16)
                p.inc(g.dma_start(out=qh[:, 0:TT],
                                  in_=q3_d[h, :, tqb * TT:(tqb + 1) * TT]), "khl", 16)

            @blk.tensor
            def _(t, h=h, tqb=tqb, hidx=hidx):
                t.wait_ge(p.sem("khl"), 48 * (hidx + 1))
                if hidx >= 1:
                    t.wait_ge(p.sem("ocp"), hidx)
                if hidx == 0:
                    p.wait(t, "nm7")
                if h == 0 and tqb > 0:
                    p.wait(t, "s1e")
                base = hidx * NCK

                def smm(ck):
                    psB = P0 if ck % 2 == 0 else P1
                    if base + ck >= 2:
                        t.wait_ge(p.sem("pex"), base + ck - 1)
                    p.inc(t.matmul(psB[:, :], khb[:, ck * 128:(ck + 1) * 128],
                                   qh[:, 0:TT], start=True, stop=True), "smm", 1)

                smm(0)
                smm(1)
                for ck in range(NCK):
                    t.wait_ge(p.sem("pex"), base + ck + 1)
                    p.inc(t.matmul(P3[0:65, :], vh[:, ck, :], pbs[:, ck % 2, :],
                                   start=(ck == 0), stop=(ck == NCK - 1)), "avm", 1)
                    if ck + 2 < NCK:
                        smm(ck + 2)

            @blk.vector
            def _(v, h=h, tqb=tqb, hidx=hidx):
                base = hidx * NCK
                for ck in range(NCK):
                    v.wait_ge(p.sem("smm"), base + ck + 1)
                    if h == 0 and ck == 0:
                        v.wait_ge(p.sem("mskl"), 16 * (tqb + 1))
                    psB = P0 if ck % 2 == 0 else P1
                    p.inc(v.tensor_tensor(psB[:, :], psB[:, :], mskb[:, ck, :],
                                          ALU.add), "mke", 1)

            @blk.scalar
            def _(s, h=h, hidx=hidx):
                base = hidx * NCK
                for ck in range(NCK):
                    s.wait_ge(p.sem("mke"), base + ck + 1)
                    if base + ck >= 2:
                        s.wait_ge(p.sem("avm"), base + ck - 1)
                    psB = P0 if ck % 2 == 0 else P1
                    p.inc(s.activation(out=pbs[:, ck % 2, :], in_=psB[:, :], func=AF.Exp,
                                       bias=mb[:, h:h + 1], scale=0.125), "pex", 1)

            @blk.vector
            def _(v, h=h, hidx=hidx):
                v.wait_ge(p.sem("avm"), (hidx + 1) * NCK)
                p.wait(v, "s1mm")
                p.vsf(v, v.tensor_scalar_add(onr[0:1, :], P3[64:65, :], cst[0:1, 1:2]))
                p.vsf(v, v.reciprocal(onr[0:1, :], onr[0:1, :]))
                p.inc(v.tensor_copy(oev[:, :], P3[0:64, :]), "ocp", 1)

            @blk.tensor
            def _(t, hidx=hidx):
                t.wait_ge(p.sem("ocp"), hidx + 1)
                if hidx >= 1:
                    t.wait_ge(p.sem("oev"), hidx)
                p.inc(t.matmul(P4[0:64, :], ones[0:1, 0:1].to_broadcast([1, 64]),
                               onr[0:1, :], start=True, stop=True), "obc", 1)

            @blk.vector
            def _(v, h=h, hidx=hidx):
                v.wait_ge(p.sem("obc"), hidx + 1)
                p.inc(v.tensor_mul(ofm[(h % 2) * 64:(h % 2) * 64 + 64, h // 2, :],
                                   oev[:, :], P4[0:64, :]), "oev", 1)

        @blk.gpsimd
        def _(g, tqb=tqb):
            for oc in range(DC):
                idx = tqb * DC + oc
                if idx >= 2:
                    g.wait_ge(p.sem("s1mm"), idx - 1)
                p.inc(g.dma_start(out=wst2[:, idx % 2, :, :],
                                  in_=wo_d.rearrange("(c p) m -> p c m", p=128)
                                  [:, :, oc * 128:(oc + 1) * 128]), "wol", 16)

        for oc in range(DC):
            idx = tqb * DC + oc

            @blk.tensor
            def _(t, tqb=tqb, oc=oc, idx=idx):
                t.wait_ge(p.sem("oev"), (tqb + 1) * NH)
                t.wait_ge(p.sem("wol"), 16 * (idx + 1))
                t.wait_ge(p.sem("avm"), (tqb + 1) * NH * NCK)
                if idx >= 2:
                    t.wait_ge(p.sem("s1e"), idx - 1)
                ps = P0 if idx % 2 == 0 else P1
                for c in range(DC):
                    mm = t.matmul(ps[:, :], wst2[:, idx % 2, c, :], ofm[:, c, :],
                                  start=(c == 0), stop=(c == DC - 1))
                p.inc(mm, "s1mm", 1)

            @blk.vector
            def _(v, tqb=tqb, oc=oc, idx=idx):
                v.wait_ge(p.sem("s1mm"), idx + 1)
                p.wait(v, "wld")
                if tqb >= 1:
                    p.wait(v, "s3w", 16 * tqb)
                ps = P0 if idx % 2 == 0 else P1
                p.inc(v.tensor_scalar(s1t[:, :], ps[:, :], gb[:, 0, oc:oc + 1],
                                      None, ALU.add), "s1e", 1)
                p.inc(v.tensor_copy(xf[:, oc, :], s1t[:, :]), "s3c", 1)

        @blk.sync
        def _(sy, tqb=tqb, tsl=tsl):
            p.wait(sy, "s3c")
            p.inc(sy.dma_start(out=s3_o.rearrange("(c p) t -> p c t", p=128)[:, :, tsl],
                               in_=xf[:, :, :]), "s3w", 16)

    @blk.sync
    def _(sy):
        p.wait(sy, "s3w")
    return p



SLW = 2560
NTI = 5
NQC = 20
NHC = 22


def build_launch_H(final=False):
    p = Prog()
    xin_d = p.din("xin", [D, SLW])
    wqkv_d = p.din("wqkv", [D, 3072], BF16)
    wo_d = p.din("wo", [D, D], BF16)
    w1_d = p.din("w1p", [D, 5632], BF16)
    w2_d = p.din("w2p", [2816, D], BF16)
    b1a_d = p.din("b1a", [128, NHC])
    b1g_d = p.din("b1g", [128, NHC])
    b2_d = p.din("b2", [128, DC])
    lng_d = p.din("lng", [128, DC])
    lnb_d = p.din("lnb", [128, DC])
    cs_d = p.din("cs4", [4, 128, SLW], BF16)
    msk_d = p.din("mskh", [6, 128, TT], BF16)
    id_d = p.din("identb", [128, 128], BF16)
    eop_d = p.din("eonsP", [128, 2])
    ebc_d = p.din("ebc", [2, 128])
    if final:
        x3_d = p.din("x3f", [D, TPC])
        s3_d = p.din("s3f", [D, TPC])
        x4_o = p.dout("x4", [D, TPC])
    xo_o = p.dout("xout", [D, SLW])
    o_dram = p.dtmp("od", [D, SLW], BF16)

    x = p.sb("x", [128, DC, SLW])
    qsb = p.sb("qsb", [128, SLW], BF16)
    ksb = p.sb("ksb", [128, SLW], BF16)
    vsb = p.sb("vsb", [128, SLW], BF16)
    vt = p.sb("vt", [128, NQC, 128], BF16)
    mskb = p.sb("mskb", [128, 6, TT], BF16)
    h_sb = p.sb("h_sb", [128, NHC, TT], BF16)
    xt = h_sb.rearrange("p k w -> p (k w)").bitcast(F32)[:, 0:DC * TT].rearrange(
        "p (c w) -> p c w", w=TT)
    xsb = p.sb("xsb", [128, 2, DC, TT], BF16)
    xnb = p.sb("xnb", [128, DC, TT], BF16)
    wqkv_sb = p.sb("wqkv_sb", [128, 2, 24, 128], BF16)
    wf_sb = p.sb("wf_sb", [128, 2, 16, 128], BF16)
    wd_sb = p.sb("wd_sb", [128, 2, NHC, 128], BF16)
    ob_sb = p.sb("ob_sb", [128, 1, DC, TT], BF16)
    cs_sb = p.sb("cs_sb", [128, 1, 4, TT], BF16)
    sqf = p.sb("sqf", [128, TT])
    fbb = p.sb("fbb", [128, TT], BF16)
    t1b = p.sb("t1b", [128, TT], BF16)
    t2b = p.sb("t2b", [128, TT], BF16)
    pbs = p.sb("pbs", [128, TT], BF16)
    obq = p.sb("obq", [128, 2, 128], BF16)
    mxc = p.sb("mxc", [128, 4])
    gg = p.sb("gg", [128, TT])
    tmp = p.sb("tmpa", [128, TT])
    t3b = tmp.bitcast(BF16)[:, 0:TT]
    stb = gg.bitcast(BF16)[:, 0:TT].rearrange("p (a w) -> p a w", w=128)
    sq = ob_sb.rearrange("p b c w -> p (b c w)").bitcast(F32).rearrange(
        "p (a w) -> p a w", w=TT)[:, 0:2, :]
    nsr = sq[0:2, :, :]
    srow = h_sb.rearrange("p k w -> p (k w)").bitcast(F32)[:, DC * TT:(DC + 2) * TT]        .rearrange("p (a w) -> p a w", w=TT)[0:1, :, :]
    ones = p.sb("ones", [128, 2])
    cst = p.sb("cst", [128, 2])
    identb = p.sb("idh", [128, 128], BF16)
    eonsP = p.sb("eoph", [128, 2])
    ebc = p.sb("ebch", [2, 128])
    gb = p.sb("gb", [128, 5, DC])
    b1s = p.sb("b1s", [128, 2, NHC])

    P0 = p.ps("P0", [128, TT])
    P1 = p.ps("P1", [128, TT])
    P2 = p.ps("P2", [128, TT])
    P3 = p.ps("P3", [128, TT])
    PN = p.ps("PN", [128, TT])
    PT = p.ps("PT", [128, TT])
    PO = p.ps("PO", [128, TT])
    PB = p.ps("PB", [128, TT], BF16)

    blk = p.block()

    @blk.vector
    def _(v):
        v.memset(ones[:, :], 1.0)
        v.memset(cst[:, 0:1], 1e-5)
        p.inc(v.memset(cst[:, 1:2], 1e-12), "init", 1)

    @blk.scalar
    def _(s):
        p.wait(s, "init")

    @blk.tensor
    def _(t):
        p.wait(t, "init")

    @blk.gpsimd
    def _(g):
        p.wait(g, "init")
        p.inc(g.dma_start(out=identb[:, :], in_=id_d[:, :]), "wld", 16)
        p.inc(g.dma_start(out=eonsP[:, :], in_=eop_d[:, :]), "wld", 16)
        p.inc(g.dma_start(out=ebc[:, :], in_=ebc_d[:, :]), "wld", 16)
        p.inc(g.dma_start(out=gb[:, 0, :], in_=lng_d[:, :]), "wld", 16)
        p.inc(g.dma_start(out=gb[:, 1, :], in_=lnb_d[:, :]), "wld", 16)
        p.inc(g.dma_start(out=gb[:, 2, :], in_=b2_d[:, :]), "wld", 16)
        p.inc(g.dma_start(out=b1s[:, 0, :], in_=b1a_d[:, :]), "wld", 16)
        p.inc(g.dma_start(out=b1s[:, 1, :], in_=b1g_d[:, :]), "wld", 16)
        p.inc(g.dma_start(out=mskb[:, :, :],
                          in_=msk_d.rearrange("q p t -> p q t")), "wld", 16)

    @blk.sync
    def _(sy):
        p.inc(sy.dma_start(out=x[:, :, :],
                           in_=xin_d.rearrange("(c p) t -> p c t", p=128)), "xld", 16)

    for hp in range(8):
        @blk.gpsimd
        def _(g, hp=hp):
            if hp >= 2:
                g.wait_ge(p.sem("qkm"), (hp - 1) * 15)
            for s3i in range(3):
                p.inc(g.dma_start(
                    out=wqkv_sb[:, hp % 2, s3i * 8:(s3i + 1) * 8, :],
                    in_=wqkv_d.rearrange("(c p) m -> p c m", p=128)
                    [:, :, s3i * 1024 + hp * 128:s3i * 1024 + (hp + 1) * 128]),
                    "qkl", 16)

        for ti in range(NTI):
            csl = slice(ti * TT, (ti + 1) * TT)

            @blk.vector
            def _(v, hp=hp, ti=ti, csl=csl):
                idx = hp * NTI + ti
                if idx == 0:
                    p.wait(v, "xld")
                if idx >= 2:
                    p.wait(v, "qkm", 3 * (idx - 1))
                for c in range(4):
                    v.tensor_copy(xsb[:, idx % 2, c, :], x[:, c, csl])
                if ti == 0:
                    v.memset(xsb[:, idx % 2, 4:8, 0:1], 0.0)
                    for c in range(4, 8):
                        cp = v.tensor_copy(xsb[:, idx % 2, c, 1:TT],
                                           x[:, c, 0:TT - 1])
                else:
                    for c in range(4, 8):
                        cp = v.tensor_copy(xsb[:, idx % 2, c, :],
                                           x[:, c, ti * TT - 1:(ti + 1) * TT - 1])
                p.inc(cp, "xst", 1)

            @blk.tensor
            def _(t, hp=hp, ti=ti, csl=csl):
                idx = hp * NTI + ti
                t.wait_ge(p.sem("qkl"), 48 * (hp + 1))
                t.wait_ge(p.sem("xst"), idx + 1)
                if idx >= 1:
                    p.wait(t, "qke", 3 * idx)
                for s3i, ps in ((0, P0), (1, P1), (2, P2)):
                    for c in range(DC):
                        mm = t.matmul(ps[:, :], wqkv_sb[:, hp % 2, s3i * 8 + c, :],
                                      xsb[:, idx % 2, c, :],
                                      start=(c == 0), stop=(c == DC - 1))
                    p.inc(mm, "qkm", 1)

            @blk.vector
            def _(v, hp=hp, ti=ti, csl=csl):
                idx = hp * NTI + ti
                v.wait_ge(p.sem("qkm"), 3 * (idx + 1))
                if ti == 0 and hp >= 1:
                    p.wait(v, "avm")
                v.tensor_copy(qsb[:, csl], P0[:, :])
                v.tensor_copy(ksb[:, csl], P1[:, :])
                p.inc(v.tensor_copy(vsb[:, csl], P2[:, :]), "qke", 3)

        for ti in range(NTI):
            csl = slice(ti * TT, (ti + 1) * TT)
            idx = hp * NTI + ti

            @blk.sync
            def _(sy, hp=hp, ti=ti, idx=idx, csl=csl):
                if idx >= 1:
                    p.wait(sy, "rpk", idx)
                p.inc(sy.dma_start(out=cs_sb[:, 0, :, :],
                                   in_=cs_d[:, :, csl].rearrange("f p t -> p f t")),
                      "csl", 16)

            for which, strip, tcol in (("q", qsb, 0), ("k", ksb, 2)):
                @blk.scalar
                def _(s, which=which, strip=strip, idx=idx, csl=csl):
                    s.wait_ge(p.sem("qke"), 3 * (idx + 1))
                    if idx >= 1 or which == "k":
                        p.wait(s, "fme", 2 * idx + (1 if which == "k" else 0))
                    p.inc(s.activation(out=sqf[:, :], in_=strip[:, csl],
                                       func=AF.Square), which + "sq", 1)

                @blk.tensor
                def _(t, which=which, idx=idx, hp=hp, ti=ti):
                    t.wait_ge(p.sem(which + "sq"), idx + 1)
                    p.wait(t, "wld")
                    if hp >= 1 and ti == 0 and which == "q":
                        p.wait(t, "pex")
                    p.inc(t.matmul(PN[0:2, :], eonsP[:, :], sqf[:, :],
                                   start=True, stop=True), which + "n2", 1)

                @blk.scalar
                def _(s, which=which, idx=idx):
                    s.wait_ge(p.sem(which + "n2"), idx + 1)
                    p.inc(s.activation(out=nsr[:, 0, :], in_=PN[0:2, :],
                                       func=AF.Sqrt), which + "sr", 1)

                @blk.vector
                def _(v, which=which, idx=idx):
                    v.wait_ge(p.sem(which + "sr"), idx + 1)
                    p.vsf(v, v.tensor_scalar(nsr[:, 1, :], nsr[:, 0, :],
                                             cst[0:2, 1:2], None, ALU.max))
                    p.inc(v.reciprocal(nsr[:, 1, :], nsr[:, 1, :]), which + "rc", 1)

                @blk.tensor
                def _(t, which=which, idx=idx, hp=hp, ti=ti):
                    t.wait_ge(p.sem(which + "rc"), idx + 1)
                    if hp >= 1 and ti == 0 and which == "q":
                        p.wait(t, "ste")
                        p.wait(t, "ote")
                        p.wait(t, "pex")
                    p.inc(t.matmul(PT[:, :], ebc[:, :], nsr[:, 1, :],
                                   start=True, stop=True), which + "fb", 1)

                @blk.vector
                def _(v, which=which, strip=strip, tcol=tcol, idx=idx, csl=csl):
                    v.wait_ge(p.sem(which + "fb"), idx + 1)
                    p.vsf(v, v.tensor_copy(fbb[:, :], PT[:, :]))
                    p.inc(v.tensor_mul(strip[:, csl], strip[:, csl], fbb[:, :]),
                          "fme", 1)
                    v.wait_ge(p.sem("csl"), 16 * (idx + 1))
                    v.tensor_mul(t1b[:, :], strip[:, csl], cs_sb[:, 0, tcol, :])
                    cp = v.tensor_mul(t2b[:, :], strip[:, csl],
                                      cs_sb[:, 0, tcol + 1, :])
                    p.vsf(v, cp)
                    for bb in range(4):
                        sl_o = slice(bb * 32, bb * 32 + 32)
                        sl_i = slice((bb ^ 1) * 32, (bb ^ 1) * 32 + 32)
                        v.tensor_copy(t3b[sl_o, :], t2b[sl_i, :])
                    p.vsf(v, v.engine_nop())
                    p.inc(v.tensor_add(strip[:, csl], t1b[:, :], t3b[:, :]),
                          "rp" + which, 1)

            @blk.tensor
            def _(t, hp=hp, ti=ti, idx=idx):
                t.wait_ge(p.sem("qke"), 3 * (idx + 1))
                if ti == 0 and hp >= 1:
                    p.wait(t, "avm")
                    p.wait(t, "onv")
                    p.wait(t, "ote")
                for ck in range(4):
                    for b in range(2):
                        if idx * 8 + ck * 2 + b >= 1:
                            p.wait(t, "vte", idx * 8 + ck * 2 + b)
                        p.inc(t.transpose(PB[0:128, 0:64],
                                          vsb[64 * b:64 * b + 64,
                                              ti * TT + ck * 128:ti * TT + (ck + 1) * 128],
                                          identb[64 * b:64 * b + 64, 64 * b:64 * b + 64]),
                              "vtm", 1)

            @blk.vector
            def _(v, hp=hp, ti=ti, idx=idx):
                for ck in range(4):
                    for b in range(2):
                        v.wait_ge(p.sem("vtm"), idx * 8 + ck * 2 + b + 1)
                        p.inc(v.tensor_copy(vt[:, ti * 4 + ck, 64 * b:64 * b + 64],
                                            PB[0:128, 0:64]), "vte", 1)

        for b in range(2):
            for qc in range(NQC):
                aidx = (hp * 2 + b) * NQC + qc
                kw0 = max(0, (qc // 2) * 256 - 256)
                qrow = slice(64 * b, 64 * b + 64)

                @blk.tensor
                def _(t, hp=hp, b=b, qc=qc, aidx=aidx, kw0=kw0, qrow=qrow):
                    p.wait(t, "rpq", (hp + 1) * NTI)
                    p.wait(t, "rpk", (hp + 1) * NTI)
                    p.wait(t, "vte", (hp + 1) * NTI * 8)
                    if aidx >= 2:
                        t.wait_ge(p.sem("pex"), aidx - 1)
                    PS = PN if aidx % 2 == 0 else PT
                    p.inc(t.matmul(PS[:, :],
                                   qsb[qrow, qc * 128:(qc + 1) * 128],
                                   ksb[qrow, kw0:kw0 + TT],
                                   start=True, stop=True), "smm", 1)

                @blk.vector
                def _(v, b=b, qc=qc, aidx=aidx):
                    v.wait_ge(p.sem("smm"), aidx + 1)
                    ms = {0: 0, 1: 1, 4: 2, 5: 3}.get(qc, 4 + (qc % 2))
                    PS = PN if aidx % 2 == 0 else PT
                    p.vsf(v, v.tensor_tensor(PS[:, :], PS[:, :], mskb[:, ms, :],
                                             ALU.add))
                    p.vsf(v, v.reduce_max(mxc[:, 0:1], PS[:, :], axis=AX.X,
                                          negate=True))
                    p.inc(v.tensor_scalar_mul(mxc[:, 1:2], mxc[:, 0:1], 8.0),
                          "mka", 1)

                @blk.scalar
                def _(s, aidx=aidx):
                    s.wait_ge(p.sem("mka"), aidx + 1)
                    if aidx >= 1:
                        s.wait_ge(p.sem("stm"), aidx * 4)
                    PS = PN if aidx % 2 == 0 else PT
                    p.inc(s.activation(out=pbs[:, :], in_=PS[:, :], func=AF.Exp,
                                       scale=8.0, bias=mxc[:, 1:2],
                                       accum_out=mxc[:, 2:3]), "pex", 1)

                @blk.vector
                def _(v, aidx=aidx):
                    v.wait_ge(p.sem("pex"), aidx + 1)
                    p.vsf(v, v.tensor_scalar_add(mxc[:, 2:3], mxc[:, 2:3],
                                                 cst[:, 1:2]))
                    p.inc(v.reciprocal(mxc[:, 3:4], mxc[:, 2:3]), "den", 1)

                @blk.tensor
                def _(t, aidx=aidx):
                    for ck in range(4):
                        sidx = aidx * 4 + ck
                        t.wait_ge(p.sem("pex"), aidx + 1)
                        if ck == 0 and aidx >= 1:
                            t.wait_ge(p.sem("ote"), aidx)
                        if sidx >= 1:
                            p.wait(t, "ste", sidx)
                        p.inc(t.transpose(PB[:, 0:128],
                                          pbs[:, ck * 128:(ck + 1) * 128],
                                          identb[:, :]), "stm", 1)

                @blk.vector
                def _(v, aidx=aidx):
                    for ck in range(4):
                        sidx = aidx * 4 + ck
                        v.wait_ge(p.sem("stm"), sidx + 1)
                        if ck == 0 and aidx >= 1:
                            v.wait_ge(p.sem("avm"), aidx)
                        p.inc(v.tensor_copy(stb[:, ck, :], PB[:, 0:128]), "ste", 1)

                @blk.tensor
                def _(t, b=b, qc=qc, aidx=aidx, kw0=kw0):
                    t.wait_ge(p.sem("ste"), (aidx + 1) * 4)
                    if aidx >= 1:
                        p.wait(t, "onv", aidx)
                    for ck in range(4):
                        mm = t.matmul(PO[:, 0:64], stb[:, ck, :],
                                      vt[:, kw0 // 128 + ck, 64 * b:64 * b + 64],
                                      start=(ck == 0), stop=(ck == 3))
                    p.inc(mm, "avm", 1)

                @blk.vector
                def _(v, aidx=aidx):
                    v.wait_ge(p.sem("avm"), aidx + 1)
                    v.wait_ge(p.sem("den"), aidx + 1)
                    if aidx >= 2:
                        p.wait(v, "oww", 16 * (aidx - 1))
                    p.inc(v.tensor_scalar_mul(obq[:, aidx % 2, 0:64], PO[:, 0:64],
                                              mxc[:, 3:4]), "onv", 1)

                @blk.tensor
                def _(t, aidx=aidx):
                    t.wait_ge(p.sem("onv"), aidx + 1)
                    if aidx >= 1:
                        p.wait(t, "ote", aidx)
                    p.inc(t.transpose(PB[0:64, 0:128], obq[:, aidx % 2, 0:64],
                                      identb[:, :]), "otm", 1)

                @blk.vector
                def _(v, aidx=aidx):
                    v.wait_ge(p.sem("otm"), aidx + 1)
                    p.inc(v.tensor_copy(obq[0:64, aidx % 2, 0:128],
                                        PB[0:64, 0:128]), "ote", 1)

                @blk.sync
                def _(sy, hp=hp, b=b, qc=qc, aidx=aidx):
                    p.wait(sy, "ote", aidx + 1)
                    h = hp * 2 + b
                    p.inc(sy.dma_start(
                        out=o_dram[h * 64:(h + 1) * 64, qc * 128:(qc + 1) * 128],
                        in_=obq[0:64, aidx % 2, 0:128]), "oww", 16)

    for ti in range(NTI):
        csl = slice(ti * TT, (ti + 1) * TT)

        @blk.sync
        def _(sy, ti=ti, csl=csl):
            if ti == 0:
                p.wait(sy, "oww")
            else:
                p.wait(sy, "wom", ti * DC)
            p.inc(sy.dma_start(out=ob_sb[:, 0, :, :],
                               in_=o_dram.rearrange("(c p) t -> p c t", p=128)
                               [:, :, csl]), "obl", 16)

        @blk.gpsimd
        def _(g, ti=ti):
            for oc in range(DC):
                idx = ti * DC + oc
                if idx >= 2:
                    g.wait_ge(p.sem("wom"), idx - 1)
                p.inc(g.dma_start(out=wd_sb[:, idx % 2, 0:DC, :],
                                  in_=wo_d.rearrange("(c p) m -> p c m", p=128)
                                  [:, :, oc * 128:(oc + 1) * 128]), "wol", 16)

        for oc in range(DC):
            idx = ti * DC + oc

            @blk.tensor
            def _(t, ti=ti, oc=oc, idx=idx):
                t.wait_ge(p.sem("obl"), 16 * (ti + 1))
                t.wait_ge(p.sem("wol"), 16 * (idx + 1))
                if idx == 0:
                    p.wait(t, "avm")
                if idx >= 2:
                    t.wait_ge(p.sem("xr1"), idx - 1)
                ps = P0 if idx % 2 == 0 else P1
                for c in range(DC):
                    mm = t.matmul(ps[:, :], wd_sb[:, idx % 2, c, :],
                                  ob_sb[:, 0, c, :],
                                  start=(c == 0), stop=(c == DC - 1))
                p.inc(mm, "wom", 1)

            @blk.vector
            def _(v, ti=ti, oc=oc, idx=idx, csl=csl):
                v.wait_ge(p.sem("wom"), idx + 1)
                ps = P0 if idx % 2 == 0 else P1
                p.inc(v.tensor_add(x[:, oc, csl], x[:, oc, csl], ps[:, :]),
                      "xr1", 1)

    for ti in range(NTI):
        csl = slice(ti * TT, (ti + 1) * TT)

        @blk.vector
        def _(v, ti=ti, csl=csl):
            p.wait(v, "xr1", (ti + 1) * DC)
            if ti >= 1:
                p.wait(v, "xr2", ti * DC)
            for c in range(4):
                v.tensor_copy(xt[:, c, :], x[:, c, csl])
            if ti == 0:
                v.memset(xt[:, 4:8, 0:1], 0.0)
                for c in range(4, 8):
                    cp = v.tensor_copy(xt[:, c, 1:TT], x[:, c, 0:TT - 1])
            else:
                for c in range(4, 8):
                    cp = v.tensor_copy(xt[:, c, :],
                                       x[:, c, ti * TT - 1:(ti + 1) * TT - 1])
            p.inc(cp, "ln_xld", 1)

        emit_ln_tile(p, blk, f"h{ti}", xt, [xnb[:, c, :] for c in range(DC)],
                     sq, srow, P0, P1, P2, P3, ones, tmp=tmp, eps_ap=cst[0:1, 0:1],
                     gain_ap=[gb[:, 0, c:c + 1] for c in range(DC)],
                     bias_ap=[gb[:, 1, c:c + 1] for c in range(DC)])

        @blk.gpsimd
        def _(g, ti=ti):
            for hc in range(NHC):
                idx = ti * NHC + hc
                if idx >= 2:
                    g.wait_ge(p.sem("agm"), 2 * (idx - 1))
                p.inc(g.dma_start(out=wf_sb[:, idx % 2, 0:8, :],
                                  in_=w1_d.rearrange("(c p) m -> p c m", p=128)
                                  [:, :, hc * 128:(hc + 1) * 128]), "w1l", 16)
                p.inc(g.dma_start(out=wf_sb[:, idx % 2, 8:16, :],
                                  in_=w1_d.rearrange("(c p) m -> p c m", p=128)
                                  [:, :, (NHC + hc) * 128:(NHC + hc + 1) * 128]),
                      "w1l", 16)

        for hc in range(NHC):
            idx = ti * NHC + hc

            @blk.tensor
            def _(t, ti=ti, hc=hc, idx=idx):
                p.wait(t, "ln_xn")
                t.wait_ge(p.sem("w1l"), 32 * (idx + 1))
                if idx >= 2:
                    p.wait(t, "hev", idx - 1)
                psA = P0 if idx % 2 == 0 else P2
                psG = P1 if idx % 2 == 0 else P3
                for c in range(DC):
                    t.matmul(psA[:, :], wf_sb[:, idx % 2, c, :], xnb[:, c, :],
                             start=(c == 0), stop=(c == DC - 1))
                for c in range(DC):
                    mm = t.matmul(psG[:, :], wf_sb[:, idx % 2, 8 + c, :],
                                  xnb[:, c, :], start=(c == 0), stop=(c == DC - 1))
                p.inc(mm, "agm", 2)

            @blk.scalar
            def _(s, hc=hc, idx=idx):
                s.wait_ge(p.sem("agm"), 2 * (idx + 1))
                if idx >= 1:
                    s.wait_ge(p.sem("hev"), idx)
                psG = P1 if idx % 2 == 0 else P3
                p.inc(s.activation(out=gg[:, :], in_=psG[:, :], func=AF.Gelu,
                                   bias=b1s[:, 1, hc:hc + 1]), "glu", 1)

            @blk.vector
            def _(v, hc=hc, idx=idx):
                v.wait_ge(p.sem("glu"), idx + 1)
                psA = P0 if idx % 2 == 0 else P2
                v.tensor_scalar(tmp[:, :], psA[:, :], b1s[:, 0, hc:hc + 1],
                                None, ALU.add)
                v.tensor_mul(tmp[:, :], tmp[:, :], gg[:, :])
                p.inc(v.tensor_copy(h_sb[:, hc, :], tmp[:, :]), "hev", 1)

        @blk.gpsimd
        def _(g, ti=ti):
            for oc in range(DC):
                idx = ti * DC + oc
                if idx <= 1:
                    g.wait_ge(p.sem("wom"), NTI * DC)
                if idx >= 2:
                    g.wait_ge(p.sem("w2m"), idx - 1)
                p.inc(g.dma_start(out=wd_sb[:, idx % 2, :, :],
                                  in_=w2_d.rearrange("(k p) m -> p k m", p=128)
                                  [:, :, oc * 128:(oc + 1) * 128]), "w2l", 16)

        for oc in range(DC):
            idx = ti * DC + oc

            @blk.tensor
            def _(t, ti=ti, oc=oc, idx=idx):
                p.wait(t, "hev", (ti + 1) * NHC)
                t.wait_ge(p.sem("w2l"), 16 * (idx + 1))
                if idx >= 1:
                    p.wait(t, "w2e", idx)
                ps = P2 if idx % 2 == 0 else P3
                for k in range(NHC):
                    mm = t.matmul(ps[:, :], wd_sb[:, idx % 2, k, :], h_sb[:, k, :],
                                  start=(k == 0), stop=(k == NHC - 1))
                p.inc(mm, "w2m", 1)

            @blk.vector
            def _(v, ti=ti, oc=oc, idx=idx, csl=csl):
                v.wait_ge(p.sem("w2m"), idx + 1)
                if ti >= 1:
                    p.wait(v, "xow", 16 * ti)
                ps = P2 if idx % 2 == 0 else P3
                v.tensor_scalar(tmp[:, :], ps[:, :], gb[:, 2, oc:oc + 1],
                                None, ALU.add)
                p.inc(v.engine_nop(), "w2e", 1)
                p.inc(v.tensor_add(x[:, oc, csl], x[:, oc, csl], tmp[:, :]),
                      "xr2", 1)

        @blk.sync
        def _(sy, ti=ti, csl=csl):
            p.wait(sy, "xr2")
            p.inc(sy.dma_start(out=xo_o.rearrange("(c p) t -> p c t", p=128)[:, :, csl],
                               in_=x[:, :, csl]), "xow", 16)

    if final:
        for ti in range(1, NTI):
            csl = slice(ti * TT, (ti + 1) * TT)
            osl = slice((ti - 1) * TT, ti * TT)

            @blk.sync
            def _(sy, ti=ti, osl=osl):
                if ti == 1:
                    p.wait(sy, "w2m")
                if ti >= 2:
                    p.wait(sy, "x4c", (ti - 1) * DC)
                p.inc(sy.dma_start(out=xt[:, :, :], in_=x3_d.rearrange(
                    "(c p) t -> p c t", p=128)[:, :, osl]), "x3l", 16)
                p.inc(sy.dma_start(out=ob_sb.rearrange("p b c w -> p (b c w)")
                                   .bitcast(F32).rearrange("p (c w) -> p c w", w=TT)
                                   [:, 0:4, :], in_=s3_d.rearrange(
                                       "(c p) t -> p c t", p=128)[:, 0:4, osl]),
                      "s3l", 16)
                p.inc(sy.dma_start(out=xnb.rearrange("p c w -> p (c w)")
                                   .bitcast(F32).rearrange("p (c w) -> p c w", w=TT)
                                   [:, 0:4, :], in_=s3_d.rearrange(
                                       "(c p) t -> p c t", p=128)[:, 4:8, osl]),
                      "s3l", 16)

            for oc in range(DC):
                idx = (ti - 1) * DC + oc

                @blk.scalar
                def _(s, ti=ti, oc=oc, idx=idx, csl=csl):
                    s.wait_ge(p.sem("xow"), 16 * (ti + 1))
                    if idx >= 1:
                        s.wait_ge(p.sem("x4c"), idx)
                    p.inc(s.activation(out=gg[:, :], in_=x[:, oc, csl],
                                       func=AF.Silu), "fsl", 1)

                @blk.vector
                def _(v, ti=ti, oc=oc, idx=idx):
                    v.wait_ge(p.sem("fsl"), idx + 1)
                    v.wait_ge(p.sem("x3l"), 16 * ti)
                    v.wait_ge(p.sem("s3l"), 32 * ti)
                    if idx >= 1:
                        p.wait(v, "x4w", 16 * idx)
                    if oc < 4:
                        sb3 = ob_sb.rearrange("p b c w -> p (b c w)").bitcast(F32)
                        sb3 = sb3.rearrange("p (c w) -> p c w", w=TT)[:, oc, :]
                    else:
                        sb3 = xnb.rearrange("p c w -> p (c w)").bitcast(F32)
                        sb3 = sb3.rearrange("p (c w) -> p c w", w=TT)[:, oc - 4, :]
                    v.tensor_mul(tmp[:, :], gg[:, :], sb3)
                    p.inc(v.tensor_add(tmp[:, :], tmp[:, :], xt[:, oc, :]),
                          "x4c", 1)

                @blk.sync
                def _(sy, ti=ti, oc=oc, idx=idx, osl=osl):
                    p.wait(sy, "x4c", idx + 1)
                    p.inc(sy.dma_start(out=x4_o.rearrange("(c p) t -> p c t", p=128)
                                       [:, oc, osl], in_=tmp[:, :]), "x4w", 16)

        @blk.sync
        def _(sy):
            p.wait(sy, "x4w")

    @blk.sync
    def _(sy):
        p.wait(sy, "xow")
    return p



def host_h_masks(r):
    out = np.zeros((6, 128, TT), np.float32)
    qcs = [0, 1, 4, 5, 6, 7]
    for si, qc in enumerate(qcs):
        kw0 = max(0, (qc // 2) * 256 - 256)
        qp = qc * 128 + np.arange(128)[:, None]
        kj = kw0 + np.arange(TT)[None, :]
        d = qp - kj
        m = np.where((d < 0) | (d > 256), -1e30, 0.0).astype(np.float32)
        if r == 0:
            m = np.where(kj < 512, -1e30, m)
        out[si] = m
    return out.astype(ml_dtypes.bfloat16)


def host_h_tables(q0, qs, ks):
    t = np.arange(q0 - 512, q0 + TPC, dtype=np.float64)
    t = np.maximum(t, 0.0)
    half = np.arange(0, 64, 2, dtype=np.float64)
    invf = 1.0 / 10000.0 ** (half / 64.0)
    bs = (half + 0.4 * 64) / (1.4 * 64)
    power = (t - 4096.0) / 128.0
    ang = t[None, :] * invf[:, None]
    sc = bs[:, None] ** power[None, :]
    cosv = np.cos(ang)
    sinv = np.sin(ang)
    d64 = np.arange(64)
    i32 = d64 % 32
    sig = np.where(d64 < 32, 1.0, -1.0)
    qs = np.asarray(qs, np.float64)
    ks = np.asarray(ks, np.float64)
    csS = qs[d64, None] * cosv[i32] * sc[i32]
    ssS = sig[:, None] * qs[d64, None] * sinv[i32] * sc[i32]
    csI = ks[d64, None] * cosv[i32] / sc[i32]
    ssI = sig[:, None] * ks[d64, None] * sinv[i32] / sc[i32]
    out = np.stack([np.tile(csS, (2, 1)), np.tile(ssS, (2, 1)),
                    np.tile(csI, (2, 1)), np.tile(ssI, (2, 1))])
    return np.ascontiguousarray(out.astype(ml_dtypes.bfloat16))


def host_h_weights(Wqkv, Wo, W1, b1, W2, b2, lng, lnb):
    BF = ml_dtypes.bfloat16
    w1p = np.zeros((D, 5632), np.float32)
    w1p[:, 0:2730] = W1[:, 0:2730]
    w1p[:, 2816:2816 + 2730] = W1[:, 2730:5460]
    b1a = np.zeros(2816, np.float32)
    b1g = np.zeros(2816, np.float32)
    b1a[0:2730] = b1[0:2730]
    b1g[0:2730] = b1[2730:5460]
    w2p = np.zeros((2816, D), np.float32)
    w2p[0:2730] = W2

    def b128(b):
        return np.ascontiguousarray(np.asarray(b, np.float32).reshape(-1, 128).T)
    return dict(
        wqkv=np.ascontiguousarray(Wqkv.astype(BF)),
        wo=np.ascontiguousarray(Wo.astype(BF)),
        w1p=np.ascontiguousarray(w1p.astype(BF)),
        w2p=np.ascontiguousarray(w2p.astype(BF)),
        b1a=b128(b1a), b1g=b128(b1g), b2=b128(b2),
        lng=b128(lng), lnb=b128(lnb))


def host_h_consts():
    eonsP = np.zeros((128, 2), np.float32)
    eonsP[0:64, 0] = 1.0
    eonsP[64:128, 1] = 1.0
    ebc = np.zeros((2, 128), np.float32)
    ebc[0, 0:64] = 1.0
    ebc[1, 64:128] = 1.0
    return dict(identb=IDENT128.astype(ml_dtypes.bfloat16), eonsP=eonsP, ebc=ebc)
